# version 02
# baseline (speedup 1.0000x reference)
import numpy as np
import ml_dtypes
from contextlib import ExitStack
import concourse.bass as bass
import concourse.mybir as mybir
from concourse.bass_utils import run_bass_kernel_spmd
from concourse.bass_core_rust import IndirectOffsetOnAxis

F32 = mybir.dt.float32
BF16 = mybir.dt.bfloat16
U32 = mybir.dt.uint32
AF = mybir.ActivationFunctionType
ALU = mybir.AluOpType
AX = mybir.AxisListType

ENGINES = ("te", "ve", "act", "po", "sp")

class Prog:
    def __init__(self, name="prog"):
        self.nc = bass.Bass("TRN2", target_bir_lowering=False)
        self.es = ExitStack()
        self.ops = {e: [] for e in ENGINES}
        self.sems = {}
        self.counts = {}
        self.name = name

    def sem(self, name):
        if name not in self.sems:
            self.sems[name] = self.es.enter_context(self.nc.semaphore(name))
            self.counts[name] = 0
        return self.sems[name]

    def at(self, name):
        self.sem(name)
        return self.counts[name]

    def sb(self, name, shape, dtype):
        es = getattr(self, "_scope_es", None) or self.es
        return es.enter_context(self.nc.sbuf_tensor(name, shape, dtype))

    def fence(self, tag="fence"):
        snap = [(s, v) for s, v in self.counts.items() if v > 0]
        for e in ENGINES:
            self.ops[e].append((lambda E: None, list(snap), None, 1, tag))

    def scope(self):
        from contextlib import ExitStack, contextmanager
        @contextmanager
        def _s():
            old = getattr(self, "_scope_es", None)
            self._scope_es = ExitStack()
            try:
                yield
            finally:
                self._scope_es.close()
                self._scope_es = old
                self.fence("scope_fence")
        return _s()

    def ps(self, name, shape, dtype=F32):
        es = getattr(self, "_scope_es", None) or self.es
        return es.enter_context(self.nc.psum_tensor(name, shape, dtype))

    def din(self, name, shape, dtype):
        return self.nc.dram_tensor(name, shape, dtype, kind="ExternalInput").ap()

    def dout(self, name, shape, dtype):
        return self.nc.dram_tensor(name, shape, dtype, kind="ExternalOutput").ap()

    def dtmp(self, name, shape, dtype):
        return self.nc.dram_tensor(name, shape, dtype).ap()

    AUTOSYNC = ("ve", "act", "po")

    def emit(self, eng, fn, waits=(), inc=None, inc_k=1, tag="", chain=True):
        w = [(s, v) for (s, v) in waits]
        if eng in self.AUTOSYNC and chain:
            ec = eng + "_c"
            if inc is None:
                inc = ec
                inc_k = 1
            if inc == ec:
                w.append((ec, self.at(ec)))
        self.ops[eng].append((fn, w, inc, inc_k, tag))
        if inc is not None:
            self.sem(inc)
            self.counts[inc] += inc_k
            return self.counts[inc]
        return None

    def dma(self, eng, out, in_, waits=(), stream=None, tag="", **kw):
        s = stream or (eng + "_dma")
        return self.emit(eng, lambda E, o=out, i=in_: E.dma_start(out=o, in_=i, **kw),
                         waits=waits, inc=s, inc_k=16, tag=tag)

    def finalize(self, final_waits=()):
        for (s, v) in final_waits:
            self.emit("sp", lambda E: None, waits=[(s, v)], tag="final_wait")
        nc = self.nc
        with nc.Block() as blk:
            def mk_body(ename):
                def body(E):
                    for fn, waits, inc, inc_k, tag in self.ops[ename]:
                        for (s, v) in waits:
                            E.wait_ge(self.sems[s], v)
                        instr = fn(E)
                        if inc is not None:
                            assert instr is not None, f"op with inc returned None: {tag}"
                            instr.then_inc(self.sems[inc], inc_k)
                return body
            if self.ops["sp"]:
                blk.sync(mk_body("sp"))
            if self.ops["te"]:
                blk.tensor(mk_body("te"))
            if self.ops["ve"]:
                blk.vector(mk_body("ve"))
            if self.ops["act"]:
                blk.scalar(mk_body("act"))
            if self.ops["po"]:
                blk.gpsimd(mk_body("po"))
        return nc

    def close(self):
        self.es.close()


def t_matmul(p, out_ap, lhsT, rhs, start, stop, waits=(), inc=None, tag=""):
    return p.emit("te", lambda E, o=out_ap, l=lhsT, r=rhs, st=start, sp=stop:
                  E.matmul(o, l, r, start=st, stop=sp),
                  waits=waits, inc=inc, tag=tag)

def t_transpose(p, out_ap, in_ap, ident, waits=(), inc=None, tag=""):
    return p.emit("te", lambda E, o=out_ap, i=in_ap, d=ident: E.transpose(o, i, d),
                  waits=waits, inc=inc, tag=tag)

def v_op(p, fn, waits=(), inc=None, tag=""):
    return p.emit("ve", fn, waits=waits, inc=inc, tag=tag)

def a_op(p, fn, waits=(), inc=None, tag=""):
    return p.emit("act", fn, waits=waits, inc=inc, tag=tag)

def g_op(p, fn, waits=(), inc=None, inc_k=1, tag=""):
    return p.emit("po", fn, waits=waits, inc=inc, inc_k=inc_k, tag=tag)

def bcast_ap(t, P, n, m, col=0):
    ap = t[:, col:col+n]
    a = ap.ap
    return bass.AP(ap.tensor, ap.offset, [list(a[0]), list(a[1]), [0, m]])


"""Shared MoE routing emitter.

Computes, replicated from full-sequence logits [8192,16] fp32 (DRAM, token-major):
  - per tile j (64 tiles of 128 tokens): softmax probs, top-2 (w1,w2,e1,e2)
  - global exclusive cumsum positions pos1/pos2 per token
Results left in SBUF arrays (fp32):
  warr1/warr2 [128,64], earr1/earr2 [128,64], pos1/pos2 [128,64]
Caller decides what to do with them (scatter gather-lists / compute slotids).
"""

def emit_routing(p, lg_dram, own_tiles=None, lgt_filler=None):
    if own_tiles is None:
        own_tiles = list(range(64))
    lgt = p.sb("r_lgt", [128, 64 * 16], F32)
    exs = p.sb("r_exs", [128, 16], F32)
    m1 = p.sb("r_m1", [128, 1], F32)
    nm = p.sb("r_nm", [128, 1], F32)
    ssum = p.sb("r_ssum", [128, 1], F32)
    rs = p.sb("r_rs", [128, 1], F32)
    probs = p.sb("r_probs", [128, 16], F32)
    m8 = p.sb("r_m8", [128, 8], F32)
    i8 = p.sb("r_i8", [128, 8], U32)
    i1f = p.sb("r_i1f", [128, 1], F32)
    i2f = p.sb("r_i2f", [128, 1], F32)
    oh1 = p.sb("r_oh1", [128, 64 * 16], BF16)
    oh2 = p.sb("r_oh2", [128, 64 * 16], BF16)
    ohs = p.sb("r_ohs", [128, 16], F32)
    iota16 = p.sb("r_iota16", [128, 16], F32)
    nem = p.sb("r_nem", [16, 8192], BF16)
    ones16 = p.sb("r_ones16", [16, 1], F32)
    sincl = p.sb("r_sincl", [16, 8192], F32)
    sext = p.sb("r_sext", [128, 16], F32)
    tmp16 = p.sb("r_tmp16", [128, 16], F32)
    warr1 = p.sb("r_w1", [128, 64], F32)
    warr2 = p.sb("r_w2", [128, 64], F32)
    earr1 = p.sb("r_e1", [128, 64], F32)
    earr2 = p.sb("r_e2", [128, 64], F32)
    pos1 = p.sb("r_pos1", [128, 64], F32)
    pos2 = p.sb("r_pos2", [128, 64], F32)
    trp16 = p.ps("r_trp16", [16, 128], F32)
    trp128 = p.ps("r_trp128", [128, 16], F32)

    lg_ve = 0
    if lgt_filler is None:
        p.dma("sp", lgt[:], bass.AP(lg_dram.tensor, 0, [[16, 128], [2048, 64], [1, 16]]),
              stream="r_ld", tag="lgt")
    else:
        lg_ve = lgt_filler(p, lgt)
    lg_done = p.at("r_ld")

    g_op(p, lambda E: E.iota(iota16[:], [[1, 16]], base=0, channel_multiplier=0,
                             allow_small_or_imprecise_dtypes=True), tag="iota16")
    io_done = p.at("po_c")
    v_op(p, lambda E: E.memset(ones16[:], 1.0), tag="ones16")

    id_done = p.at("ve_c")
    id_f = p.id_f

    for j in range(64):
        lj = lgt[:, j * 16:(j + 1) * 16]
        v_op(p, lambda E, lj=lj: E.reduce_max(m1[:], lj, axis=AX.X),
             waits=[("r_ld", lg_done), ("ve_c", max(lg_ve, p.at("ve_c")))], tag=f"rm{j}")
        v_op(p, lambda E: E.tensor_scalar(nm[:], m1[:], -1.0, None, ALU.mult), tag=f"rn{j}")
        nm_ve = p.at("ve_c")
        ae = a_op(p, lambda E, lj=lj: E.activation(exs[:], lj, AF.Exp, bias=nm[:], scale=1.0),
                  waits=[("ve_c", nm_ve)], tag=f"re{j}")
        v_op(p, lambda E: E.reduce_sum(ssum[:], exs[:], axis=AX.X),
             waits=[("act_c", ae)], tag=f"rs{j}")
        v_op(p, lambda E: E.reciprocal(rs[:], ssum[:]), tag=f"rr{j}")
        v_op(p, lambda E: E.tensor_scalar(probs[:], exs[:], rs[:], None, ALU.mult), tag=f"rp{j}")
        v_op(p, lambda E: E.max(m8[:], probs[:]), tag=f"rx{j}")
        v_op(p, lambda E: E.max_index(i8[:], m8[:], probs[:]), tag=f"ri{j}")
        v_op(p, lambda E, j=j: E.tensor_copy(warr1[:, j:j + 1], m8[:, 0:1]), tag=f"rw1{j}")
        v_op(p, lambda E, j=j: E.tensor_copy(warr2[:, j:j + 1], m8[:, 1:2]), tag=f"rw2{j}")
        v_op(p, lambda E: E.tensor_copy(i1f[:], i8[:, 0:1]), tag=f"rc1{j}")
        v_op(p, lambda E: E.tensor_copy(i2f[:], i8[:, 1:2]), tag=f"rc2{j}")
        v_op(p, lambda E, j=j: E.tensor_copy(earr1[:, j:j + 1], i1f[:]), tag=f"re1{j}")
        v_op(p, lambda E, j=j: E.tensor_copy(earr2[:, j:j + 1], i2f[:]), tag=f"re2{j}")
        v_op(p, lambda E, j=j: E.tensor_scalar(oh1[:, j * 16:(j + 1) * 16], iota16[:],
                                               i1f[:], None, ALU.is_equal),
             waits=[("po_c", io_done)], tag=f"ro1{j}")
        v_op(p, lambda E, j=j: E.tensor_scalar(oh2[:, j * 16:(j + 1) * 16], iota16[:],
                                               i2f[:], None, ALU.is_equal), tag=f"ro2{j}")
        v_op(p, lambda E, j=j: E.tensor_add(ohs[:], oh1[:, j * 16:(j + 1) * 16],
                                            oh2[:, j * 16:(j + 1) * 16]), tag=f"roa{j}")
        ve_ohs = p.at("ve_c")
        tt = t_transpose(p, trp16[:], ohs[:], id_f[:],
                         waits=[("ve_c", ve_ohs)], inc="te_c", tag=f"rt{j}")
        v_op(p, lambda E, j=j: E.tensor_copy(nem[:, j * 128:(j + 1) * 128], trp16[:]),
             waits=[("te_c", tt)], tag=f"rte{j}")
    nem_done = p.at("ve_c")
    ones_b = bass.AP(ones16[:].tensor, 0, [list(ones16[:].ap[0]), [0, 8192]])
    v_op(p, lambda E: E.tensor_tensor_scan(sincl[:], ones_b, nem[:], 0.0,
                                           ALU.mult, ALU.add),
         waits=[("ve_c", nem_done)], tag="rscan")
    v_op(p, lambda E: E.tensor_sub(sincl[:], sincl[:], nem[:]), tag="rsub")
    sx_done = p.at("ve_c")
    for j in own_tiles:
        tt = t_transpose(p, trp128[:], sincl[:, j * 128:(j + 1) * 128], id_f[0:16, 0:16],
                         waits=[("ve_c", sx_done), ("ve_c", p.at("ve_c"))],
                         inc="te_c", tag=f"rpt{j}")
        ve0 = v_op(p, lambda E: E.tensor_copy(sext[:], trp128[:]),
                   waits=[("te_c", tt)], tag=f"rpe{j}")
        v_op(p, lambda E, j=j: E.tensor_mul(tmp16[:], oh1[:, j * 16:(j + 1) * 16], sext[:]),
             tag=f"rq1{j}")
        v_op(p, lambda E, j=j: E.reduce_sum(pos1[:, j:j + 1], tmp16[:], axis=AX.X),
             tag=f"rr1{j}")
        v_op(p, lambda E, j=j: E.tensor_mul(tmp16[:], oh2[:, j * 16:(j + 1) * 16], sext[:]),
             tag=f"rq2{j}")
        v_op(p, lambda E, j=j: E.reduce_sum(pos2[:, j:j + 1], tmp16[:], axis=AX.X),
             tag=f"rr2{j}")
    done = p.at("ve_c")
    return dict(warr1=warr1, warr2=warr2, earr1=earr1, earr2=earr2,
                pos1=pos1, pos2=pos2, done=done)

"""Single-launch fused megakernel: both layers, 3 AllGathers, activations
resident in SBUF between phases (x1 bounced via local DRAM).

Inputs per core: x [1024,2048] f32, wq [2048,2048] f32, wk/wv [2048,512],
gwt [2048,16], wo [2048,2048], e0vec [128,1] f32, wg/wu [2,2048,1408],
wd [2,1408,2048], myidx [1024,1] u32 (GLOBAL token ids of own tokens),
wsg/wsu [2048,2816], wsd [2816,2048], wq1 [16,2048,128], bqT [128,16],
wk1/wv1 [4,2048,128], bkT/bvT [128,4], masks [64,128,128] bf16,
wfcT [2048,5120], wprT [5120,2048].   Output: out [1024,2048] f32.

AG row convention: gathered full-sequence tensors are rank-major: row
r*1024+i = token tok_idx(r)[i]. The MoE token list stores AG-row ids
(base agbase(j) per global tile). rodram slot/weight arrays are indexed by
GLOBAL token id (tile-major j*128+p), gathered with myidx.
L1 causality: iterate k-chunks in GLOBAL order, index AG-ordered buffers at
agcol(kc)."""

def chunks_of(c):
    return sorted([c, 15 - c, 16 + c, 31 - c, 32 + c, 47 - c, 48 + c, 63 - c])

_OWNER = {}
for _c in range(8):
    for _m, _j in enumerate(chunks_of(_c)):
        _OWNER[_j] = (_c, _m)

def agcol(j):
    r, m = _OWNER[j]
    return 8 * r + m

def agbase(j):
    r, m = _OWNER[j]
    return r * 1024 + m * 128

SCALE = 1.0 / (128.0 ** 0.5)


def build_fused():
    p = Prog("FUSED")
    x_in = p.din("x", [1024, 2048], F32)
    wq = p.din("wq", [2048, 2048], F32)
    wk = p.din("wk", [2048, 512], F32)
    wv = p.din("wv", [2048, 512], F32)
    gwt = p.din("gwt", [2048, 16], F32)
    wo = p.din("wo", [2048, 2048], F32)
    e0vec = p.din("e0vec", [128, 1], F32)
    wg = p.din("wg", [2, 2048, 1408], F32)
    wu = p.din("wu", [2, 2048, 1408], F32)
    wd = p.din("wd", [2, 1408, 2048], F32)
    myidx = p.din("myidx", [1024, 1], U32)
    wsg = p.din("wsg", [2048, 2816], F32)
    wsu = p.din("wsu", [2048, 2816], F32)
    wsd = p.din("wsd", [2816, 2048], F32)
    wq1 = p.din("wq1", [16, 2048, 128], F32)
    bqT = p.din("bqT", [128, 16], F32)
    wk1 = p.din("wk1", [4, 2048, 128], F32)
    bkT = p.din("bkT", [128, 4], F32)
    wv1 = p.din("wv1", [4, 2048, 128], F32)
    bvT = p.din("bvT", [128, 4], F32)
    masks = p.din("masks", [64, 128, 128], BF16)
    wfcT = p.din("wfcT", [2048, 5120], F32)
    wprT = p.din("wprT", [5120, 2048], F32)
    out = p.dout("out", [1024, 2048], F32)

    xn_c = p.dtmp("xn_c", [1024, 2048], BF16)
    kh_c = p.dtmp("kh_c", [512, 1024], BF16)
    v_c = p.dtmp("v_c", [1024, 512], BF16)
    lg_c = p.dtmp("lg_c", [16, 1024], F32)
    xn_ag = p.dtmp("xn_ag", [8192, 2048], BF16)
    kh_ag = p.dtmp("kh_ag", [4096, 1024], BF16)
    v_ag = p.dtmp("v_ag", [8192, 512], BF16)
    lg_ag = p.dtmp("lg_ag", [128, 1024], F32)
    attn_d = p.dtmp("attn_d", [2048, 1024], BF16)
    listd = p.dtmp("listd", [144, 128], U32)
    out_c = p.dtmp("out_c", [2048, 2048], BF16)
    outf_ag = p.dtmp("outf_ag", [16384, 2048], BF16)
    rodram = p.dtmp("rodram", [4, 64, 128], F32)
    x1_d = p.dtmp("x1_d", [1024, 2048], F32)
    k1_c = p.dtmp("k1_c", [512, 1024], BF16)
    v1_c = p.dtmp("v1_c", [1024, 512], BF16)
    k1_ag = p.dtmp("k1_ag", [4096, 1024], BF16)
    v1_ag = p.dtmp("v1_ag", [8192, 512], BF16)

    idw = p.sb("idw", [128, 128], F32)
    id_f = p.sb("id_f", [128, 128], F32)
    id_b = p.sb("id_b", [128, 128], BF16)
    eps5 = p.sb("eps5", [128, 1], F32)
    eps6 = p.sb("eps6", [128, 1], F32)
    ones_b = p.sb("ones_b", [128, 1], BF16)
    ones_row = p.sb("ones_row", [1, 128], F32)
    onescol = p.sb("onescol", [128, 1], F32)
    iota_p = p.sb("iota_p", [128, 1], F32)
    g_op(p, lambda E: E.iota(idw[:], [[-1, 128]], base=0, channel_multiplier=1,
                             allow_small_or_imprecise_dtypes=True), tag="iota")
    g_op(p, lambda E: E.iota(iota_p[:], [[0, 1]], base=0, channel_multiplier=1,
                             allow_small_or_imprecise_dtypes=True), tag="iop")
    io_done = p.at("po_c")
    v_op(p, lambda E: E.tensor_scalar(id_f[:], idw[:], 0.0, None, ALU.is_equal),
         waits=[("po_c", io_done)], tag="idf")
    v_op(p, lambda E: E.tensor_copy(id_b[:], id_f[:]), tag="idb")
    v_op(p, lambda E: E.memset(eps5[:], 1e-5), tag="eps5")
    v_op(p, lambda E: E.memset(eps6[:], 1e-6), tag="eps6")
    v_op(p, lambda E: E.memset(ones_b[:], 1.0), tag="ones_b")
    v_op(p, lambda E: E.memset(ones_row[:], 1.0), tag="ones_row")
    v_op(p, lambda E: E.memset(onescol[:], 1.0), tag="onescol")
    p.id_f = id_f
    const_done = p.at("ve_c")

    def ln_tiles(pfx, x_dram, dst_fm_sb, xn_tm_dram=None, logits=None,
                 x_ready=()):
        xt_b = [p.sb(pfx + f"xt{i}", [128, 2048], F32) for i in range(2)]
        xnf32 = p.sb(pfx + "xnf32", [128, 2048], F32)
        xn_tm_bf = p.sb(pfx + "xntm", [128, 2048], BF16)
        sumx = p.sb(pfx + "sumx", [128, 8], F32)
        ssq = p.sb(pfx + "ssq", [128, 8], F32)
        meanv = p.sb(pfx + "mean", [128, 8], F32)
        varv = p.sb(pfx + "var", [128, 8], F32)
        m2 = p.sb(pfx + "m2", [128, 8], F32)
        rstd = p.sb(pfx + "rstd", [128, 8], F32)
        sq_scr = p.sb(pfx + "sq", [128, 2048], F32)
        lgscr = p.sb(pfx + "lg", [128, 2048], F32)
        trps = p.ps(pfx + "trps", [128, 128], F32)
        if logits is not None:
            gwt_sb, lgT_sb, lgps, gwt_ld = logits
        prev_tile_te = p.at("te_c")
        last_xn_ve = [0, 0]
        for t in range(8):
            par = t % 2
            ldx = p.dma("sp", xt_b[par][:], x_dram[t * 128:(t + 1) * 128, :],
                        waits=[("ve_c", last_xn_ve[par])] + list(x_ready),
                        stream="ld" + pfx + str(par), tag=f"{pfx}x{t}")
            xt = xt_b[par][:]
            v_op(p, lambda E, xt=xt, t=t: E.reduce_sum(sumx[:, t:t + 1], xt, axis=AX.X),
                 waits=[("ld" + pfx + str(par), ldx)], tag=f"{pfx}sum{t}")
            a_op(p, lambda E, xt=xt, t=t: E.activation(sq_scr[:], xt, AF.Square,
                                                       accum_out=ssq[:, t:t + 1]),
                 waits=[("ld" + pfx + str(par), ldx), ("ve_c", p.at("ve_c"))],
                 tag=f"{pfx}ssq{t}")
            act_ssq = p.at("act_c")
            v_op(p, lambda E, t=t: E.tensor_scalar(meanv[:, t:t + 1], sumx[:, t:t + 1],
                                                   1.0 / 2048, None, ALU.mult),
                 tag=f"{pfx}mn{t}")
            v_op(p, lambda E, t=t: E.tensor_scalar(varv[:, t:t + 1], ssq[:, t:t + 1],
                                                   1.0 / 2048, None, ALU.mult),
                 waits=[("act_c", act_ssq)], tag=f"{pfx}vr{t}")
            v_op(p, lambda E, t=t: E.scalar_tensor_tensor(m2[:, t:t + 1],
                                                          meanv[:, t:t + 1], 1.0,
                                                          meanv[:, t:t + 1],
                                                          ALU.mult, ALU.mult),
                 tag=f"{pfx}m2{t}")
            v_op(p, lambda E, t=t: E.tensor_sub(varv[:, t:t + 1], varv[:, t:t + 1],
                                                m2[:, t:t + 1]), tag=f"{pfx}v2{t}")
            ve_var = p.at("ve_c")
            a_op(p, lambda E, t=t: E.activation(rstd[:, t:t + 1], varv[:, t:t + 1],
                                                AF.Ln, bias=eps5[:], scale=1.0),
                 waits=[("ve_c", max(ve_var, const_done))], tag=f"{pfx}ln{t}")
            a_op(p, lambda E, t=t: E.activation(rstd[:, t:t + 1], rstd[:, t:t + 1],
                                                AF.Exp, bias=0.0, scale=-0.5),
                 tag=f"{pfx}ex{t}")
            act_rstd = p.at("act_c")
            xn_ve = v_op(p, lambda E, xt=xt, t=t: E.tensor_scalar(
                xnf32[:], xt, meanv[:, t:t + 1], rstd[:, t:t + 1],
                ALU.subtract, ALU.mult),
                waits=[("act_c", act_rstd), ("te_c", prev_tile_te)], tag=f"{pfx}xn{t}")
            last_xn_ve[par] = xn_ve
            if xn_tm_dram is not None:
                cvt = v_op(p, lambda E: E.tensor_copy(xn_tm_bf[:], xnf32[:]),
                           waits=[("st" + pfx, p.counts.get("st" + pfx, 0))],
                           tag=f"{pfx}xtm{t}")
                p.dma("sp", xn_tm_dram[t * 128:(t + 1) * 128, :], xn_tm_bf[:],
                      waits=[("ve_c", cvt)], stream="st" + pfx, tag=f"{pfx}oxtm{t}")
            for c in range(16):
                tt = t_transpose(p, trps[:], xnf32[:, c * 128:(c + 1) * 128], id_f[:],
                                 waits=[("ve_c", p.at("ve_c")), ("ve_c", const_done),
                                        ("act_c", p.at("act_c"))],
                                 inc="te_c", tag=f"{pfx}tr{t}_{c}")
                a_ev = a_op(p, lambda E, c=c: E.activation(
                    lgscr[:, c * 128:(c + 1) * 128], trps[:], AF.Identity),
                    waits=[("te_c", tt)], tag=f"{pfx}lgev{t}_{c}")
                v_op(p, lambda E, c=c, t=t: E.tensor_copy(
                    dst_fm_sb[:, c * 1024 + t * 128: c * 1024 + (t + 1) * 128],
                    lgscr[:, c * 128:(c + 1) * 128]),
                    waits=[("act_c", a_ev)], tag=f"{pfx}fm{t}_{c}")
                if logits is not None:
                    t_matmul(p, lgps[:, 0:128], gwt_sb[:, c * 16:(c + 1) * 16],
                             lgscr[:, c * 128:(c + 1) * 128],
                             start=(c == 0), stop=(c == 15),
                             waits=[("act_c", p.at("act_c")), ("ve_c", p.at("ve_c")),
                                    ("ld" + pfx + "g", gwt_ld)],
                             inc="te_c", tag=f"{pfx}lgmm{t}_{c}")
            prev_tile_te = p.at("te_c")
            if logits is not None:
                v_op(p, lambda E, t=t: E.tensor_copy(lgT_sb[:, t * 128:(t + 1) * 128],
                                                     lgps[:, 0:128]),
                     waits=[("te_c", prev_tile_te)], tag=f"{pfx}lge2{t}")

    def stream_panel(pfx, wsrc_fn, n_chunks, dst, chunk_cols, stage):
        cv = 0
        for c in range(n_chunks):
            ldw = p.dma("sp", stage[:, 0:chunk_cols], wsrc_fn(c),
                        stream="ldw", tag=f"{pfx}l{c}")
            cv = v_op(p, lambda E, c=c: E.tensor_copy(
                dst[:, c * chunk_cols:(c + 1) * chunk_cols], stage[:, 0:chunk_cols]),
                waits=[("ldw", ldw), ("te_c", p.at("te_c"))], tag=f"{pfx}c{c}")
            p.ops["sp"].append((lambda E: None, [("ve_c", cv)], None, 1, "ws"))
        return cv

    with p.scope():
        xn_fm_sb = p.sb("XN0", [128, 16 * 1024], BF16)
        s_tm_sb = p.sb("S0", [128, 8 * 2048], BF16)
        with p.scope():
            qh_sb = p.sb("QH", [128, 16 * 1024], BF16)
            with p.scope():
                gwt_sb = p.sb("A_gwt", [128, 256], F32)
                lgT_sb = p.sb("A_lgT", [16, 1024], F32)
                lgps = p.ps("A_lgps", [16, 128], F32)
                p.dma("sp", gwt_sb[:],
                      bass.AP(gwt.tensor, 0, [[16, 128], [2048, 16], [1, 16]]),
                      stream="ldAg", tag="Agwt")
                gwt_ld = p.at("ldAg")
                ln_tiles("A", x_in, xn_fm_sb, xn_tm_dram=xn_c,
                         logits=(gwt_sb, lgT_sb, lgps, gwt_ld))
                ve_lg = p.at("ve_c")
                p.dma("sp", lg_c, lgT_sb[:], waits=[("ve_c", ve_lg)], stream="stA",
                      tag="AolgT")

                wstage = p.sb("A_wst", [128, 512], F32)
                wpan = p.sb("A_wpan", [128, 16 * 512], BF16)
                sqq = p.sb("A_sqq", [128, 512], F32)
                ssqq = p.sb("A_ssqq", [128, 4], F32)
                rstd4 = p.sb("A_rstd4", [128, 4], F32)
                qtm_sb = p.sb("A_qtm", [128, 512], BF16)
                qev = p.sb("A_qev", [128, 128], BF16)
                qps = p.ps("A_qps", [128, 512], F32)
                trbf = p.ps("A_trbf", [128, 128], BF16)

                def projA(wdram, ngrp, dest_kind, wtag):
                    for grp in range(ngrp):
                        cv = stream_panel(
                            "A" + wtag + str(grp),
                            lambda c, grp=grp: wdram[c * 128:(c + 1) * 128,
                                                     grp * 512:(grp + 1) * 512],
                            16, wpan, 512, wstage)
                        prev_cv = 0
                        for t in range(8):
                            for c in range(16):
                                t_matmul(p, qps[:, 0:512],
                                         xn_fm_sb[:, c * 1024 + t * 128:
                                                  c * 1024 + (t + 1) * 128],
                                         wpan[:, c * 512:(c + 1) * 512],
                                         start=(c == 0), stop=(c == 15),
                                         waits=[("ve_c", max(cv, prev_cv)),
                                                ("act_c", p.at("act_c"))] if c == 0 else (),
                                         inc="te_c", tag=f"A{wtag}m{grp}_{t}_{c}")
                            te_q = p.at("te_c")
                            if dest_kind[0] == "vtm":
                                cvv = v_op(p, lambda E: E.tensor_copy(qtm_sb[:],
                                                                      qps[:, 0:512]),
                                           waits=[("te_c", te_q),
                                                  ("stA", p.counts.get("stA", 0))],
                                           tag=f"A{wtag}ev{grp}_{t}")
                                prev_cv = cvv
                                p.dma("sp", dest_kind[1][t * 128:(t + 1) * 128, :],
                                      qtm_sb[:], waits=[("ve_c", cvv)], stream="stA",
                                      tag=f"A{wtag}o{t}")
                                continue
                            a1 = a_op(p, lambda E: E.activation(sqq[:], qps[:, 0:512],
                                                                AF.Square),
                                      waits=[("te_c", te_q)], tag=f"A{wtag}s{grp}_{t}")
                            r1 = v_op(p, lambda E: E.tensor_reduce(
                                ssqq[:], sqq[:].rearrange("p (h d) -> p h d", h=4),
                                AX.X, ALU.add), waits=[("act_c", a1)],
                                tag=f"A{wtag}r{grp}_{t}")
                            a2 = a_op(p, lambda E: E.activation(rstd4[:], ssqq[:],
                                                                AF.Ln, bias=eps6[:],
                                                                scale=1.0 / 128),
                                      waits=[("ve_c", max(r1, const_done))],
                                      tag=f"A{wtag}n{grp}_{t}")
                            a3 = a_op(p, lambda E: E.activation(rstd4[:], rstd4[:],
                                                                AF.Exp, bias=0.0,
                                                                scale=-0.5),
                                      tag=f"A{wtag}x{grp}_{t}")
                            cvq = v_op(p, lambda E: E.tensor_tensor(
                                qtm_sb[:].rearrange("p (h d) -> p h d", h=4),
                                qps[:, 0:512].rearrange("p (h d) -> p h d", h=4),
                                bcast_ap(rstd4, 128, 4, 128), ALU.mult),
                                waits=[("act_c", a3)], tag=f"A{wtag}q{grp}_{t}")
                            prev_cv = cvq
                            for k in range(4):
                                tt = t_transpose(p, trbf[:],
                                                 qtm_sb[:, k * 128:(k + 1) * 128],
                                                 id_b[:],
                                                 waits=[("ve_c", p.at("ve_c"))],
                                                 inc="te_c", tag=f"A{wtag}t{grp}_{t}_{k}")
                                h = grp * 4 + k
                                if dest_kind[0] == "qh":
                                    v_op(p, lambda E, h=h, t=t: E.tensor_copy(
                                        qh_sb[:, h * 1024 + t * 128:
                                              h * 1024 + (t + 1) * 128], trbf[:]),
                                        waits=[("te_c", tt)], tag=f"Aqh{grp}_{t}_{k}")
                                else:
                                    ev = v_op(p, lambda E: E.tensor_copy(qev[:], trbf[:]),
                                              waits=[("te_c", tt),
                                                     ("stA", p.counts.get("stA", 0))],
                                              tag=f"A{wtag}e{grp}_{t}_{k}")
                                    p.dma("sp",
                                          dest_kind[1][h * 128:(h + 1) * 128,
                                                       t * 128:(t + 1) * 128], qev[:],
                                          waits=[("ve_c", ev)], stream="stA",
                                          tag=f"A{wtag}o{grp}_{t}_{k}")

                projA(wq, 4, ("qh",), "q")
                projA(wk, 1, ("dram", kh_c), "k")
                projA(wv, 1, ("vtm", v_c), "v")
            for nm, (ci, co) in {"cc1a": (xn_c, xn_ag), "cc1b": (kh_c, kh_ag),
                                 "cc1c": (v_c, v_ag), "cc1d": (lg_c, lg_ag)}.items():
                g_op(p, lambda E, ci=ci, co=co: E.collective_compute(
                    "AllGather", ALU.bypass, replica_groups=[list(range(8))],
                    ins=[ci.opt()], outs=[co.opt()]),
                    waits=[("stA", p.at("stA"))] if nm == "cc1a" else (),
                    inc=nm, inc_k=1, tag="AG1" + nm)
                p.ops["po"].append((lambda E: None, [(nm, 1)], None, 1, "AG1w"))

            with p.scope():
                kbufB = p.sb("B_k", [128, 8192], BF16)
                vbufB = p.sb("B_v", [128, 64 * 128], BF16)
                esB = p.sb("B_es", [128, 1024], BF16)
                rcB = p.sb("B_rc", [1, 1024], F32)
                bcB = p.sb("B_bc", [128, 1024], F32)
                aoB = p.sb("B_ao", [128, 1024], BF16)
                lgpB = p.ps("B_lgp", [128, 1024], F32)
                avpB = p.ps("B_avp", [128, 1024], F32)
                dnpB = p.ps("B_dnp", [1, 1024], F32)
                bcpB = p.ps("B_bcp", [128, 1024], F32)
                for h in range(16):
                    g = h // 4
                    prev_sweep = p.at("te_c")
                    for r_ in range(8):
                        p.dma("sp", kbufB[:, r_ * 1024:(r_ + 1) * 1024],
                              kh_ag[512 * r_ + 128 * g: 512 * r_ + 128 * (g + 1), :],
                              waits=[("te_c", prev_sweep), ("cc1b", 1), ("cc1c", 1)] if r_ == 0 else (),
                              stream="Bldk", tag=f"Bk{h}_{r_}")
                    p.dma("sp", vbufB[:],
                          bass.AP(v_ag.tensor, g * 128,
                                  [[512, 128], [128 * 512, 64], [1, 128]]),
                          stream="Bldk", tag=f"Bv{h}")
                    ldk = p.at("Bldk")
                    prev_act = p.at("act_c")
                    prev_ve = p.at("ve_c")
                    for kc in range(64):
                        w0 = [("Bldk", ldk)]
                        if kc == 0:
                            w0 += [("act_c", prev_act), ("ve_c", prev_ve)]
                        else:
                            w0 = [("act_c", p.at("act_c"))]
                        t_matmul(p, lgpB[:, 0:512], kbufB[:, kc * 128:(kc + 1) * 128],
                                 qh_sb[:, h * 1024: h * 1024 + 512], True, True,
                                 waits=w0, inc="te_c", tag=f"Bl{h}_{kc}a")
                        t_matmul(p, lgpB[:, 512:1024], kbufB[:, kc * 128:(kc + 1) * 128],
                                 qh_sb[:, h * 1024 + 512: h * 1024 + 1024], True, True,
                                 inc="te_c", tag=f"Bl{h}_{kc}b")
                        te_lg = p.at("te_c")
                        a_exp = a_op(p, lambda E: E.activation(esB[:], lgpB[:], AF.Exp,
                                                               bias=0.0, scale=SCALE),
                                     waits=[("te_c", te_lg)], tag=f"Be{h}_{kc}")
                        st0 = (kc == 0); sp0 = (kc == 63)
                        t_matmul(p, avpB[:, 0:512], vbufB[:, kc * 128:(kc + 1) * 128],
                                 esB[:, 0:512], st0, sp0,
                                 waits=[("act_c", a_exp)], inc="te_c", tag=f"Ba{h}_{kc}a")
                        t_matmul(p, avpB[:, 512:1024], vbufB[:, kc * 128:(kc + 1) * 128],
                                 esB[:, 512:1024], st0, sp0, inc="te_c",
                                 tag=f"Ba{h}_{kc}b")
                        t_matmul(p, dnpB[0:1, 0:512], ones_b[:], esB[:, 0:512], st0, sp0,
                                 waits=[("ve_c", const_done)] if kc == 0 else (),
                                 inc="te_c", tag=f"Bd{h}_{kc}a")
                        t_matmul(p, dnpB[0:1, 512:1024], ones_b[:], esB[:, 512:1024],
                                 st0, sp0, inc="te_c", tag=f"Bd{h}_{kc}b")
                    te_sweep = p.at("te_c")
                    rc = v_op(p, lambda E: E.reciprocal(rcB[:], dnpB[:]),
                              waits=[("te_c", te_sweep)], tag=f"Br{h}")
                    t_matmul(p, bcpB[:, 0:512], ones_row[:], rcB[0:1, 0:512], True, True,
                             waits=[("ve_c", rc)], inc="te_c", tag=f"Bb{h}a")
                    bc_mm = t_matmul(p, bcpB[:, 512:1024], ones_row[:],
                                     rcB[0:1, 512:1024], True, True, inc="te_c",
                                     tag=f"Bb{h}b")
                    bce = v_op(p, lambda E: E.tensor_copy(bcB[:], bcpB[:]),
                               waits=[("te_c", bc_mm)], tag=f"Bc{h}")
                    ao = v_op(p, lambda E: E.scalar_tensor_tensor(
                        aoB[:], avpB[:], 1.0, bcB[:], ALU.mult, ALU.mult),
                        waits=[("ve_c", bce), ("stB", p.counts.get("stB", 0))],
                        tag=f"Bo{h}")
                    p.dma("sp", attn_d[h * 128:(h + 1) * 128, :], aoB[:],
                          waits=[("ve_c", ao)], stream="stB", tag=f"Bso{h}")
                attn_done_st = p.at("stB")

                wstB = p.sb("B_wst", [128, 512], F32)
                wpanB = p.sb("B_wpan", [128, 16 * 512], BF16)
                atc = [p.sb(f"B_atc{i}", [128, 2048], BF16) for i in range(2)]
                for grp in range(4):
                    cv = stream_panel("Bw" + str(grp),
                                      lambda c, grp=grp: wo[c * 128:(c + 1) * 128,
                                                            grp * 512:(grp + 1) * 512],
                                      16, wpanB, 512, wstB)
                    prev_ev = 0
                    for t in range(8):
                        par = t % 2
                        lda = p.dma("sp", atc[par][:],
                                    bass.AP(attn_d.tensor, t * 128,
                                            [[1024, 128], [128 * 1024, 16], [1, 128]]),
                                    waits=[("stB", attn_done_st),
                                           ("te_c", p.at("te_c"))],
                                    stream="Blda", tag=f"Bla{grp}_{t}")
                        for c in range(16):
                            t_matmul(p, bcpB[:, 0:512],
                                     atc[par][:, c * 128:(c + 1) * 128],
                                     wpanB[:, c * 512:(c + 1) * 512],
                                     start=(c == 0), stop=(c == 15),
                                     waits=[("ve_c", max(cv, prev_ev)),
                                            ("Blda", lda)] if c == 0 else (),
                                     inc="te_c", tag=f"Bm{grp}_{t}_{c}")
                        te_w = p.at("te_c")
                        ev = v_op(p, lambda E, t=t, grp=grp: E.tensor_copy(
                            s_tm_sb[:, t * 2048 + grp * 512: t * 2048 + (grp + 1) * 512],
                            bcpB[:, 0:512]), waits=[("te_c", te_w)], tag=f"Bv{grp}_{t}")
                        prev_ev = ev
        s_done = p.at("ve_c")

        e0_sb = p.sb("e0_sb", [128, 1], F32)
        p.dma("sp", e0_sb[:], e0vec, stream="ld_e0", tag="e0")
        zer = p.sb("zer", [128, 128], U32)
        ze = v_op(p, lambda E: E.memset(zer[:], 0), tag="zer")
        p.dma("sp", listd[0:128, :], zer[:], waits=[("ve_c", ze)], stream="lz", tag="lz0")
        p.dma("sp", listd[128:144, :], zer[0:16, :], waits=[("ve_c", ze)], stream="lz",
              tag="lz1")
        lz_done = p.at("lz")

        with p.scope():
            lgrow = p.sb("C_lgrow", [16, 128], F32)
            trp_lg = p.ps("C_trplg", [128, 16], F32)

            def lg_filler(p2, lgt):
                last = 0
                for j in range(64):
                    r_, m_ = _OWNER[j]
                    l1 = p.dma("sp", lgrow[:],
                               lg_ag[16 * r_:16 * (r_ + 1), 128 * m_:128 * (m_ + 1)],
                               waits=[("cc1d", 1), ("ve_c", last)],
                               stream="Clgl", tag=f"Clg{j}")
                    tt = t_transpose(p, trp_lg[:], lgrow[:], id_f[0:16, 0:16],
                                     waits=[("Clgl", l1), ("ve_c", p.at("ve_c"))],
                                     inc="te_c", tag=f"Clt{j}")
                    last = v_op(p, lambda E, j=j: E.tensor_copy(
                        lgt[:, j * 16:(j + 1) * 16], trp_lg[:]),
                        waits=[("te_c", tt)], tag=f"Cle{j}")
                return last

            r = emit_routing(p, None, own_tiles=list(range(64)), lgt_filler=lg_filler)

            slot = p.sb("C_slot", [128, 1], F32)
            kept = p.sb("C_kept", [128, 1], F32)
            erel = p.sb("C_erel", [128, 1], F32)
            cge = p.sb("C_cge", [128, 1], F32)
            dstf = p.sb("C_dstf", [128, 1], F32)
            dstu = p.sb("C_dstu", [128, 1], U32)
            tknu = p.sb("C_tknu", [128, 1], U32)
            tknf = p.sb("C_tknf", [128, 1], F32)
            for j in range(64):
                for k in (1, 2):
                    pos = (r["pos1"] if k == 1 else r["pos2"])[:, j:j + 1]
                    ear = (r["earr1"] if k == 1 else r["earr2"])[:, j:j + 1]
                    v_op(p, lambda E, pos=pos: E.tensor_scalar_min(slot[:], pos, 1023.0),
                         waits=[("ve_c", r["done"])] if (j == 0 and k == 1) else (),
                         tag=f"Cs{j}_{k}")
                    v_op(p, lambda E, pos=pos: E.tensor_scalar(kept[:], pos, 1023.5,
                                                               None, ALU.is_le),
                         tag=f"Ck{j}_{k}")
                    v_op(p, lambda E, ear=ear: E.tensor_sub(erel[:], ear, e0_sb[:]),
                         waits=[("ld_e0", p.at("ld_e0"))] if (j == 0 and k == 1) else (),
                         tag=f"Ce{j}_{k}")
                    v_op(p, lambda E: E.tensor_scalar_add(erel[:], erel[:], 16.0),
                         tag=f"Ca{j}_{k}")
                    v_op(p, lambda E: E.tensor_scalar(cge[:], erel[:], 15.5, None,
                                                      ALU.is_gt), tag=f"Cg{j}_{k}")
                    v_op(p, lambda E: E.scalar_tensor_tensor(erel[:], cge[:], -16.0,
                                                             erel[:], ALU.mult,
                                                             ALU.add), tag=f"Cm{j}_{k}")
                    v_op(p, lambda E: E.scalar_tensor_tensor(dstf[:], erel[:], 1024.0,
                                                             slot[:], ALU.mult,
                                                             ALU.add), tag=f"C0{j}_{k}")
                    tb_ = 16384.0 + (k - 1) * 128
                    v_op(p, lambda E, tb_=tb_: E.tensor_scalar(cge[:], iota_p[:], 1.0,
                                                               tb_, ALU.mult, ALU.add),
                         tag=f"Ct{j}_{k}")
                    v_op(p, lambda E: E.tensor_sub(dstf[:], dstf[:], cge[:]),
                         tag=f"C1{j}_{k}")
                    v_op(p, lambda E: E.scalar_tensor_tensor(dstf[:], dstf[:], 1.0,
                                                             kept[:], ALU.mult,
                                                             ALU.mult), tag=f"C2{j}_{k}")
                    v_op(p, lambda E: E.tensor_add(dstf[:], dstf[:], cge[:]),
                         tag=f"C3{j}_{k}")
                    v_op(p, lambda E: E.tensor_copy(dstu[:], dstf[:]), tag=f"C4{j}_{k}")
                    base = float(agbase(j))
                    v_op(p, lambda E, base=base: E.tensor_scalar(tknf[:], iota_p[:],
                                                                 1.0, base, ALU.mult,
                                                                 ALU.add),
                         tag=f"C5{j}_{k}")
                    cvt = v_op(p, lambda E: E.tensor_copy(tknu[:], tknf[:]),
                               tag=f"C6{j}_{k}")
                    g_op(p, lambda E: E.indirect_dma_start(
                        out=bass.AP(listd.tensor, 0, [[1, 144 * 128], [1, 1]]),
                        out_offset=IndirectOffsetOnAxis(dstu[:], 0),
                        in_=tknu[:], in_offset=None),
                        waits=[("ve_c", cvt), ("lz", lz_done)], inc="sc_dma",
                        inc_k=16, tag=f"C7{j}_{k}")
                    p.ops["ve"].append((lambda E: None, [("sc_dma", p.at("sc_dma"))],
                                        None, 1, "scsync"))
            scat_done = p.at("sc_dma")

            slq = p.sb("C_slq", [128, 64], F32)
            tmpq = p.sb("C_tmpq", [128, 64], F32)
            sl64 = p.sb("C_sl64", [64, 128], F32)
            trq = p.ps("C_trq", [64, 128], F32)
            for k in (1, 2):
                posk = r["pos1"] if k == 1 else r["pos2"]
                eark = r["earr1"] if k == 1 else r["earr2"]
                v_op(p, lambda E, posk=posk: E.tensor_scalar_min(tmpq[:], posk[:],
                                                                 1023.0),
                     waits=[("ve_c", r["done"])], tag=f"Cq{k}")
                v_op(p, lambda E, eark=eark: E.scalar_tensor_tensor(
                    slq[:], eark[:], 1024.0, tmpq[:], ALU.mult, ALU.add),
                    tag=f"Cw{k}")
                tt = t_transpose(p, trq[:], slq[:, 0:64], id_f[:],
                                 waits=[("ve_c", p.at("ve_c"))], inc="te_c",
                                 tag=f"Cx{k}")
                e1 = v_op(p, lambda E: E.tensor_copy(sl64[:], trq[0:64, :]),
                          waits=[("te_c", tt), ("rod", p.counts.get("rod", 0))],
                          tag=f"Cy{k}")
                p.dma("sp", bass.AP(rodram.tensor, (k - 1) * 64 * 128,
                                    [[128, 64], [1, 128]]),
                      sl64[:], waits=[("ve_c", e1)], stream="rod", tag=f"Cz{k}")
                wk_ = r["warr1"] if k == 1 else r["warr2"]
                tw = t_transpose(p, trq[:], wk_[:, 0:64], id_f[:],
                                 waits=[("ve_c", p.at("ve_c"))], inc="te_c",
                                 tag=f"Cu{k}")
                e2 = v_op(p, lambda E: E.tensor_copy(sl64[:], trq[0:64, :]),
                          waits=[("te_c", tw), ("rod", p.at("rod"))], tag=f"Cv{k}")
                p.dma("sp", bass.AP(rodram.tensor, (1 + k) * 64 * 128,
                                    [[128, 64], [1, 128]]),
                      sl64[:], waits=[("ve_c", e2)], stream="rod", tag=f"Cr{k}")
            rod_done = p.at("rod")
        bar_ve = p.at("ve_c"); bar_te = p.at("te_c"); bar_act = p.at("act_c")
        bar_po = p.at("po_c")

        with p.scope():
            lrow = p.sb("C_lrow", [128, 1], U32)
            xg_tm = p.sb("C_xgtm", [128, 2048], BF16)
            xg_fm = p.sb("C_xgfm", [128, 16 * 1024], BF16)
            trbfC = p.ps("C_trbf", [128, 128], BF16)
            gpsC = p.ps("C_gps", [128, 512], F32)
            upsC = p.ps("C_ups", [128, 512], F32)
            dpsC = p.ps("C_dps", [128, 512], F32)
            g_sbC = p.sb("C_g", [128, 1024], BF16)
            u_sbC = p.sb("C_u", [128, 1024], BF16)
            gu_fm = p.sb("C_gu", [128, 11 * 1024], BF16)
            wstC = p.sb("C_wst", [128, 512], F32)
            wpang = p.sb("C_wpg", [128, 16 * 128], BF16)
            wpanu = p.sb("C_wpu", [128, 16 * 128], BF16)
            wpand = p.sb("C_wpd", [128, 11 * 512], BF16)
            ev_sbC = p.sb("C_ev", [128, 512], BF16)

            for le in range(2):
                for st in range(8):
                    row = le * 8 + st
                    ldl = p.dma("sp", lrow[:],
                                listd[row:row + 1, :].rearrange("a b -> b a"),
                                waits=[("sc_dma", scat_done),
                                       ("gt_dma", p.counts.get("gt_dma", 0)),
                                       ("ve_c", bar_ve), ("act_c", bar_act),
                                       ("te_c", bar_te), ("po_c", bar_po)],
                                stream="ldl", tag=f"CL{le}_{st}")
                    g_op(p, lambda E: E.indirect_dma_start(
                        out=xg_tm[:], out_offset=None, in_=xn_ag,
                        in_offset=IndirectOffsetOnAxis(lrow[:], 0)),
                        waits=[("ldl", ldl), ("te_c", p.at("te_c")), ("cc1a", 1)],
                        inc="gt_dma", inc_k=16, tag=f"CG{le}_{st}")
                    gt = p.at("gt_dma")
                    for c in range(16):
                        tt = t_transpose(p, trbfC[:], xg_tm[:, c * 128:(c + 1) * 128],
                                         id_b[:],
                                         waits=[("gt_dma", gt), ("ve_c", p.at("ve_c"))],
                                         inc="te_c", tag=f"CT{le}_{st}_{c}")
                        v_op(p, lambda E, c=c, st=st: E.tensor_copy(
                            xg_fm[:, c * 1024 + st * 128: c * 1024 + (st + 1) * 128],
                            trbfC[:]), waits=[("te_c", tt)], tag=f"CE{le}_{st}_{c}")
                xg_done = p.at("ve_c")
                for i in range(11):
                    stream_panel(f"CA{le}_{i}",
                                 lambda c, le=le, i=i: wg[le, c * 128:(c + 1) * 128,
                                                         i * 128:(i + 1) * 128],
                                 16, wpang, 128, wstC)
                    cvu = stream_panel(f"CB{le}_{i}",
                                       lambda c, le=le, i=i: wu[le,
                                                               c * 128:(c + 1) * 128,
                                                               i * 128:(i + 1) * 128],
                                       16, wpanu, 128, wstC)
                    for sg_ in range(2):
                        for c in range(16):
                            t_matmul(p, gpsC[:, 0:512], wpang[:, c * 128:(c + 1) * 128],
                                     xg_fm[:, c * 1024 + sg_ * 512:
                                           c * 1024 + (sg_ + 1) * 512],
                                     start=(c == 0), stop=(c == 15),
                                     waits=[("ve_c", max(cvu, xg_done, p.at("ve_c")))] if c == 0 else (),
                                     inc="te_c", tag=f"CM{le}_{i}_{sg_}_{c}")
                        te_g = p.at("te_c")
                        ag_ = a_op(p, lambda E, sg_=sg_: E.activation(
                            g_sbC[:, sg_ * 512:(sg_ + 1) * 512], gpsC[:, 0:512],
                            AF.Silu), waits=[("te_c", te_g)], tag=f"CS{le}_{i}_{sg_}")
                        for c in range(16):
                            t_matmul(p, upsC[:, 0:512], wpanu[:, c * 128:(c + 1) * 128],
                                     xg_fm[:, c * 1024 + sg_ * 512:
                                           c * 1024 + (sg_ + 1) * 512],
                                     start=(c == 0), stop=(c == 15),
                                     inc="te_c", tag=f"CU{le}_{i}_{sg_}_{c}")
                        te_u = p.at("te_c")
                        v_op(p, lambda E, sg_=sg_: E.tensor_copy(
                            u_sbC[:, sg_ * 512:(sg_ + 1) * 512], upsC[:, 0:512]),
                            waits=[("te_c", te_u)], tag=f"CV{le}_{i}_{sg_}")
                        v_op(p, lambda E, i=i, sg_=sg_: E.tensor_mul(
                            gu_fm[:, i * 1024 + sg_ * 512: i * 1024 + (sg_ + 1) * 512],
                            g_sbC[:, sg_ * 512:(sg_ + 1) * 512],
                            u_sbC[:, sg_ * 512:(sg_ + 1) * 512]),
                            waits=[("act_c", ag_)], tag=f"CW{le}_{i}_{sg_}")
                gu_done = p.at("ve_c")
                for og in range(4):
                    cvd = stream_panel(f"CD{le}_{og}",
                                       lambda i, le=le, og=og: wd[le,
                                                                 i * 128:(i + 1) * 128,
                                                                 og * 512:(og + 1) * 512],
                                       11, wpand, 512, wstC)
                    for st in range(8):
                        for i in range(11):
                            t_matmul(p, dpsC[:, 0:512],
                                     gu_fm[:, i * 1024 + st * 128:
                                           i * 1024 + (st + 1) * 128],
                                     wpand[:, i * 512:(i + 1) * 512],
                                     start=(i == 0), stop=(i == 10),
                                     waits=[("ve_c", max(cvd, gu_done, p.at("ve_c")))] if i == 0 else (),
                                     inc="te_c", tag=f"CP{le}_{og}_{st}_{i}")
                        te_d = p.at("te_c")
                        ev = v_op(p, lambda E: E.tensor_copy(ev_sbC[:], dpsC[:, 0:512]),
                                  waits=[("te_c", te_d),
                                         ("stC", p.counts.get("stC", 0))],
                                  tag=f"CQ{le}_{og}_{st}")
                        p.dma("sp", out_c[le * 1024 + st * 128:
                                          le * 1024 + (st + 1) * 128,
                                          og * 512:(og + 1) * 512], ev_sbC[:],
                              waits=[("ve_c", ev)], stream="stC",
                              tag=f"CO{le}_{og}_{st}")

        g_op(p, lambda E: E.collective_compute(
            "AllGather", ALU.bypass, replica_groups=[list(range(8))],
            ins=[out_c.opt()], outs=[outf_ag.opt()]),
            waits=[("stC", p.at("stC"))], inc="cc2", inc_k=1, tag="AG2")
        p.ops["po"].append((lambda E: None, [("cc2", 1)], None, 1, "AG2w"))

        with p.scope():
            myi = p.sb("D_myi", [128, 1], U32)
            sidf = p.sb("D_sidf", [128, 2], F32)
            sidu = p.sb("D_sidu", [128, 1], U32)
            wvD = p.sb("D_wv", [128, 2], F32)
            gbuf = p.sb("D_gbuf", [128, 2048], BF16)
            y_all = p.sb("D_yall", [128, 8 * 2048], BF16)
            yacc = p.sb("D_yacc", [128, 2048], F32)
            for t in range(8):
                lm = p.dma("sp", myi[:], myidx[t * 128:(t + 1) * 128, :],
                           waits=[("gt_dma", p.at("gt_dma")), ("ve_c", p.at("ve_c")),
                                  ("rod", rod_done)],
                           stream="ldm", tag=f"Dm{t}")
                for k in (0, 1):
                    g_op(p, lambda E, k=k: E.indirect_dma_start(
                        out=sidf[:, k:k + 1], out_offset=None,
                        in_=bass.AP(rodram.tensor, 0, [[1, 32768], [1, 1]]),
                        in_offset=IndirectOffsetOnAxis(myi[:], 0),
                        element_offset=k * 64 * 128),
                        waits=[("ldm", p.at("ldm")), ("ve_c", p.at("ve_c"))],
                        inc="gt_dma", inc_k=16, tag=f"DS{t}_{k}")
                    g_op(p, lambda E, k=k: E.indirect_dma_start(
                        out=wvD[:, k:k + 1], out_offset=None,
                        in_=bass.AP(rodram.tensor, 0, [[1, 32768], [1, 1]]),
                        in_offset=IndirectOffsetOnAxis(myi[:], 0),
                        element_offset=(2 + k) * 64 * 128),
                        inc="gt_dma", inc_k=16, tag=f"DW{t}_{k}")
                gt0 = p.at("gt_dma")
                for k in (0, 1):
                    cu = v_op(p, lambda E, k=k: E.tensor_copy(sidu[:], sidf[:, k:k + 1]),
                              waits=[("gt_dma", gt0)], tag=f"DC{t}_{k}")
                    g_op(p, lambda E: E.indirect_dma_start(
                        out=gbuf[:], out_offset=None, in_=outf_ag,
                        in_offset=IndirectOffsetOnAxis(sidu[:], 0)),
                        waits=[("ve_c", cu), ("cc2", 1), ("ve_c", p.at("ve_c"))],
                        inc="gt_dma", inc_k=16, tag=f"DG{t}_{k}")
                    gtk = p.at("gt_dma")
                    if k == 0:
                        v_op(p, lambda E: E.tensor_scalar(yacc[:], gbuf[:], wvD[:, 0:1],
                                                          None, ALU.mult),
                             waits=[("gt_dma", gtk)], tag=f"D0{t}")
                    else:
                        v_op(p, lambda E: E.scalar_tensor_tensor(
                            yacc[:], gbuf[:], wvD[:, 1:2], yacc[:], ALU.mult, ALU.add),
                            waits=[("gt_dma", gtk)], tag=f"D1{t}")
                        v_op(p, lambda E, t=t: E.tensor_copy(
                            y_all[:, t * 2048:(t + 1) * 2048], yacc[:]), tag=f"D2{t}")
            y_done = p.at("ve_c")

            guD = p.sb("D_gu", [128, 22 * 512], BF16)
            wpgD = p.sb("D_wpg", [128, 16 * 128], BF16)
            wpuD = p.sb("D_wpu", [128, 16 * 128], BF16)
            wpdD = p.sb("D_wpd", [128, 22 * 512], BF16)
            wstD = p.sb("D_wst", [128, 512], F32)
            g_sbD = p.sb("D_g", [128, 512], BF16)
            u_sbD = p.sb("D_u", [128, 512], BF16)
            ysD = p.sb("D_ys", [128, 512], F32)
            silD = p.sb("D_sil", [128, 512], BF16)
            xslD = p.sb("D_xsl", [128, 512], F32)
            o_sbD = p.sb("D_o", [128, 512], F32)
            gpsD = p.ps("D_gps", [128, 512], F32)
            upsD = p.ps("D_ups", [128, 512], F32)
            dpsD = p.ps("D_dps", [128, 512], F32)
            for tg in range(2):
                for i in range(22):
                    stream_panel(f"DA{tg}_{i}",
                                 lambda c, i=i: wsg[c * 128:(c + 1) * 128,
                                                   i * 128:(i + 1) * 128],
                                 16, wpgD, 128, wstD)
                    stream_panel(f"DB{tg}_{i}",
                                 lambda c, i=i: wsu[c * 128:(c + 1) * 128,
                                                   i * 128:(i + 1) * 128],
                                 16, wpuD, 128, wstD)
                    for c in range(16):
                        t_matmul(p, gpsD[:, 0:512], wpgD[:, c * 128:(c + 1) * 128],
                                 xn_fm_sb[:, c * 1024 + tg * 512:
                                          c * 1024 + (tg + 1) * 512],
                                 start=(c == 0), stop=(c == 15),
                                 waits=[("ve_c", p.at("ve_c"))] if c == 0 else (),
                                 inc="te_c", tag=f"DM{tg}_{i}_{c}")
                    te_g = p.at("te_c")
                    agD = a_op(p, lambda E: E.activation(g_sbD[:], gpsD[:, 0:512],
                                                         AF.Silu),
                               waits=[("te_c", te_g)], tag=f"DI{tg}_{i}")
                    for c in range(16):
                        t_matmul(p, upsD[:, 0:512], wpuD[:, c * 128:(c + 1) * 128],
                                 xn_fm_sb[:, c * 1024 + tg * 512:
                                          c * 1024 + (tg + 1) * 512],
                                 start=(c == 0), stop=(c == 15),
                                 inc="te_c", tag=f"DU{tg}_{i}_{c}")
                    te_u = p.at("te_c")
                    v_op(p, lambda E: E.tensor_copy(u_sbD[:], upsD[:, 0:512]),
                         waits=[("te_c", te_u)], tag=f"DV{tg}_{i}")
                    v_op(p, lambda E, i=i: E.tensor_mul(guD[:, i * 512:(i + 1) * 512],
                                                        g_sbD[:], u_sbD[:]),
                         waits=[("act_c", agD)], tag=f"DX{tg}_{i}")
                gu_doneD = p.at("ve_c")
                for og in range(4):
                    cvd = stream_panel(f"DD{tg}_{og}",
                                       lambda i, og=og: wsd[i * 128:(i + 1) * 128,
                                                           og * 512:(og + 1) * 512],
                                       22, wpdD, 512, wstD)
                    for tt_ in range(4):
                        t = tg * 4 + tt_
                        for i in range(22):
                            t_matmul(p, dpsD[:, 0:512],
                                     guD[:, i * 512 + tt_ * 128:
                                         i * 512 + (tt_ + 1) * 128],
                                     wpdD[:, i * 512:(i + 1) * 512],
                                     start=(i == 0), stop=(i == 21),
                                     waits=[("ve_c", max(cvd, gu_doneD, p.at("ve_c")))] if i == 0 else (),
                                     inc="te_c", tag=f"DP{tg}_{og}_{tt_}_{i}")
                        te_d = p.at("te_c")
                        lx = p.dma("sp", xslD[:], x_in[t * 128:(t + 1) * 128,
                                                       og * 512:(og + 1) * 512],
                                   waits=[("ve_c", p.at("ve_c"))], stream="ldsD",
                                   tag=f"DL{t}_{og}")
                        sv = v_op(p, lambda E, t=t, og=og: E.scalar_tensor_tensor(
                            ysD[:], dpsD[:, 0:512], 1.0,
                            y_all[:, t * 2048 + og * 512: t * 2048 + (og + 1) * 512],
                            ALU.mult, ALU.add),
                            waits=[("te_c", te_d), ("ve_c", y_done)], tag=f"DY{t}_{og}")
                        asil = a_op(p, lambda E: E.activation(silD[:], ysD[:], AF.Silu),
                                    waits=[("ve_c", sv)], tag=f"DF{t}_{og}")
                        v_op(p, lambda E, t=t, og=og: E.tensor_mul(
                            ysD[:], silD[:],
                            s_tm_sb[:, t * 2048 + og * 512: t * 2048 + (og + 1) * 512]),
                            waits=[("act_c", asil), ("ve_c", s_done)], tag=f"DZ{t}_{og}")
                        ov = v_op(p, lambda E: E.tensor_add(o_sbD[:], ysD[:], xslD[:]),
                                  waits=[("stD", p.counts.get("stD", 0)),
                                         ("ldsD", lx)], tag=f"DO{t}_{og}")
                        p.dma("sp", x1_d[t * 128:(t + 1) * 128,
                                         og * 512:(og + 1) * 512], o_sbD[:],
                              waits=[("ve_c", ov)], stream="stD", tag=f"DR{t}_{og}")
    x1_done = p.at("stD")

    with p.scope():
        xn1_fm_sb = p.sb("XN1", [128, 16 * 1024], BF16)
        s1_sb = p.sb("S1", [128, 16 * 1024], BF16)
        with p.scope():
            q1_sb = p.sb("Q1", [128, 16 * 1024], BF16)
            with p.scope():
                ln_tiles("E", x1_d, xn1_fm_sb, xn_tm_dram=None, logits=None,
                         x_ready=[("stD", x1_done)])
                xn1_done = p.at("ve_c")
                bq_sb = p.sb("E_bq", [128, 16], F32)
                bk_sb = p.sb("E_bk", [128, 4], F32)
                bv_sb = p.sb("E_bv", [128, 4], F32)
                p.dma("sp", bq_sb[:], bqT, stream="ldEb", tag="Ebq")
                p.dma("sp", bk_sb[:], bkT, stream="ldEb", tag="Ebk")
                p.dma("sp", bv_sb[:], bvT, stream="ldEb", tag="Ebv")
                ld_b = p.at("ldEb")
                wstE = p.sb("E_wst", [128, 512], F32)
                wpanE = p.sb("E_wpan", [128, 16 * 128], BF16)
                qeE = p.sb("E_qe", [128, 1024], BF16)
                vtmE = p.sb("E_vtm", [128, 128], BF16)
                qpsE = p.ps("E_qps", [128, 512], F32)
                trbfE = p.ps("E_trbf", [128, 128], BF16)

                def projE(wdram, bias_sb, nheads, dest, wtag, to_tm=False):
                    for h in range(nheads):
                        cv = stream_panel(f"E{wtag}{h}",
                                          lambda c, h=h: wdram[h,
                                                              c * 128:(c + 1) * 128, :],
                                          16, wpanE, 128, wstE)
                        for grp in range(2):
                            for c in range(16):
                                t_matmul(p, qpsE[:, 0:512],
                                         wpanE[:, c * 128:(c + 1) * 128],
                                         xn1_fm_sb[:, c * 1024 + grp * 512:
                                                   c * 1024 + (grp + 1) * 512],
                                         start=(c == 0), stop=(c == 15),
                                         waits=[("ve_c", max(cv, xn1_done)),
                                                ("act_c", p.at("act_c"))] if c == 0 else (),
                                         inc="te_c", tag=f"E{wtag}m{h}_{grp}_{c}")
                            te_q = p.at("te_c")
                            a_op(p, lambda E, h=h, grp=grp: E.activation(
                                qeE[:, grp * 512:(grp + 1) * 512], qpsE[:, 0:512],
                                AF.Identity, bias=bias_sb[:, h:h + 1], scale=1.0),
                                waits=[("te_c", te_q), ("ldEb", ld_b),
                                       ("stE", p.counts.get("stE", 0)),
                                       ("ve_c", p.at("ve_c"))],
                                tag=f"E{wtag}e{h}_{grp}")
                        ae_all = p.at("act_c")
                        if not to_tm:
                            if dest is None:
                                v_op(p, lambda E, h=h: E.tensor_copy(
                                    q1_sb[:, h * 1024:(h + 1) * 1024], qeE[:]),
                                    waits=[("act_c", ae_all)], tag=f"Eq{h}")
                            else:
                                p.dma("sp", dest[h * 128:(h + 1) * 128, :], qeE[:],
                                      waits=[("act_c", ae_all)], stream="stE",
                                      tag=f"E{wtag}o{h}")
                        else:
                            for t in range(8):
                                tt = t_transpose(p, trbfE[:],
                                                 qeE[:, t * 128:(t + 1) * 128], id_b[:],
                                                 waits=[("act_c", ae_all),
                                                        ("ve_c", p.at("ve_c"))],
                                                 inc="te_c", tag=f"E{wtag}t{h}_{t}")
                                ev = v_op(p, lambda E: E.tensor_copy(vtmE[:], trbfE[:]),
                                          waits=[("te_c", tt),
                                                 ("stE", p.counts.get("stE", 0))],
                                          tag=f"E{wtag}v{h}_{t}")
                                p.dma("sp", dest[t * 128:(t + 1) * 128,
                                                 h * 128:(h + 1) * 128], vtmE[:],
                                      waits=[("ve_c", ev)], stream="stE",
                                      tag=f"E{wtag}o{h}_{t}")

                projE(wq1, bq_sb, 16, None, "q")
                projE(wk1, bk_sb, 4, k1_c, "k")
                projE(wv1, bv_sb, 4, v1_c, "v", to_tm=True)

            for nm, (ci, co) in {"cc3a": (k1_c, k1_ag), "cc3b": (v1_c, v1_ag)}.items():
                g_op(p, lambda E, ci=ci, co=co: E.collective_compute(
                    "AllGather", ALU.bypass, replica_groups=[list(range(8))],
                    ins=[ci.opt()], outs=[co.opt()]),
                    waits=[("stE", p.at("stE"))] if nm == "cc3a" else (),
                    inc=nm, inc_k=1, tag="AG3" + nm)
                p.ops["po"].append((lambda E: None, [(nm, 1)], None, 1, "AG3w"))

            with p.scope():
                kbufF = p.sb("F_k", [128, 8192], BF16)
                vbufF = p.sb("F_v", [128, 64 * 128], BF16)
                mk_sb = p.sb("F_mk", [128, 64 * 128], BF16)
                esF = p.sb("F_es", [128, 1024], BF16)
                rcF = p.sb("F_rc", [1, 1024], F32)
                bcF = p.sb("F_bc", [128, 1024], F32)
                lgpF = p.ps("F_lgp", [128, 1024], F32)
                avpF = p.ps("F_avp", [128, 1024], F32)
                dnpF = p.ps("F_dnp", [1, 1024], F32)
                bcpF = p.ps("F_bcp", [128, 1024], F32)
                p.dma("sp", mk_sb[:],
                      bass.AP(masks.tensor, 0, [[128, 128], [16384, 64], [1, 128]]),
                      stream="ldF", tag="Fmk")
                ld_mk = p.at("ldF")
                for h in range(16):
                    g = h // 4
                    prev_sweep = p.at("te_c")
                    for r_ in range(8):
                        p.dma("sp", kbufF[:, r_ * 1024:(r_ + 1) * 1024],
                              k1_ag[512 * r_ + 128 * g: 512 * r_ + 128 * (g + 1), :],
                              waits=[("te_c", prev_sweep), ("cc3a", 1), ("cc3b", 1)] if r_ == 0 else (),
                              stream="ldkF", tag=f"Fk{h}_{r_}")
                    p.dma("sp", vbufF[:],
                          bass.AP(v1_ag.tensor, g * 128,
                                  [[512, 128], [128 * 512, 64], [1, 128]]),
                          stream="ldkF", tag=f"Fv{h}")
                    ldkF = p.at("ldkF")
                    prev_act = p.at("act_c")
                    prev_ve = p.at("ve_c")
                    for kc in range(64):
                        ac = agcol(kc)
                        m0 = kc // 8
                        s = m0 * 128
                        pieces = []
                        if s < 512:
                            pieces.append((s, 512))
                        pieces.append((max(s, 512), 1024))
                        w0 = [("ldkF", ldkF)]
                        if kc == 0:
                            w0 += [("act_c", prev_act), ("ve_c", prev_ve)]
                        else:
                            w0 = [("act_c", p.at("act_c")), ("ve_c", p.at("ve_c"))]
                        first = True
                        for (a, b) in pieces:
                            t_matmul(p, lgpF[:, a:b], kbufF[:, ac * 128:(ac + 1) * 128],
                                     q1_sb[:, h * 1024 + a: h * 1024 + b], True, True,
                                     waits=w0 if first else (), inc="te_c",
                                     tag=f"Fl{h}_{kc}_{a}")
                            first = False
                        te_lg = p.at("te_c")
                        a_exp = a_op(p, lambda E, s=s: E.activation(
                            esF[:, s:1024], lgpF[:, s:1024], AF.Exp,
                            bias=0.0, scale=SCALE),
                            waits=[("te_c", te_lg)], tag=f"Fe{h}_{kc}")
                        mm = v_op(p, lambda E, s=s, kc=kc: E.tensor_mul(
                            esF[:, s:s + 128], esF[:, s:s + 128],
                            mk_sb[:, kc * 128:(kc + 1) * 128]),
                            waits=[("act_c", a_exp), ("ldF", ld_mk)], tag=f"Fm{h}_{kc}")
                        for (a, b) in pieces:
                            st_ = (kc == 0)
                            sp_ = (kc == 31) if b == 512 else (kc == 63)
                            t_matmul(p, avpF[:, a:b], vbufF[:, ac * 128:(ac + 1) * 128],
                                     esF[:, a:b], st_, sp_,
                                     waits=[("ve_c", mm)], inc="te_c",
                                     tag=f"Fa{h}_{kc}_{a}")
                            t_matmul(p, dnpF[0:1, a:b], ones_b[:], esF[:, a:b],
                                     st_, sp_,
                                     waits=[("ve_c", const_done)] if kc == 0 else (),
                                     inc="te_c", tag=f"Fd{h}_{kc}_{a}")
                    te_sweep = p.at("te_c")
                    rcv = v_op(p, lambda E: E.reciprocal(rcF[:], dnpF[:]),
                               waits=[("te_c", te_sweep)], tag=f"Fr{h}")
                    t_matmul(p, bcpF[:, 0:512], ones_row[:], rcF[0:1, 0:512], True,
                             True, waits=[("ve_c", rcv)], inc="te_c", tag=f"Fb{h}a")
                    bc_mm = t_matmul(p, bcpF[:, 512:1024], ones_row[:],
                                     rcF[0:1, 512:1024], True, True, inc="te_c",
                                     tag=f"Fb{h}b")
                    bce = v_op(p, lambda E: E.tensor_copy(bcF[:], bcpF[:]),
                               waits=[("te_c", bc_mm)], tag=f"Fc{h}")
                    v_op(p, lambda E, h=h: E.scalar_tensor_tensor(
                        s1_sb[:, h * 1024:(h + 1) * 1024], avpF[:], 1.0, bcF[:],
                        ALU.mult, ALU.mult), waits=[("ve_c", bce)], tag=f"Fo{h}")
        s1_done = p.at("ve_c")

        with p.scope():
            wstG = p.sb("G_wst", [128, 512], F32)
            junk = p.sb("G_junk", [128, 512], F32)
            csum = p.sb("G_csum", [128, 1], F32)
            accG = p.sb("G_acc", [128, 2], F32)
            scal = p.sb("G_scal", [1, 8], F32)
            perm = p.sb("G_perm", [1, 6], F32)
            thrb = p.sb("G_thrb", [128, 6], F32)
            tot_ps = p.ps("G_tot", [1, 8], F32)
            bc_ps = p.ps("G_bcp", [128, 6], F32)
            v_op(p, lambda E: E.memset(accG[:], 0.0), tag="Ga0")
            acc0 = p.at("ve_c")
            av = 0
            for wi, (wdr, nr, nc_, col) in enumerate([(wfcT, 2048, 5120, 0),
                                                      (wprT, 5120, 2048, 1)]):
                for rr in range(nr // 128):
                    for cc in range(nc_ // 512):
                        ldw = p.dma("sp", wstG[:],
                                    wdr[rr * 128:(rr + 1) * 128,
                                        cc * 512:(cc + 1) * 512],
                                    stream="ldw", tag=f"GA{wi}_{rr}_{cc}")
                        aa = a_op(p, lambda E: E.activation(junk[:], wstG[:], AF.Abs,
                                                            accum_out=csum[:]),
                                  waits=[("ldw", ldw), ("ve_c", av)],
                                  tag=f"GB{wi}_{rr}_{cc}")
                        av = v_op(p, lambda E, col=col: E.tensor_add(
                            accG[:, col:col + 1], accG[:, col:col + 1], csum[:]),
                            waits=[("act_c", aa), ("ve_c", acc0)],
                            tag=f"GC{wi}_{rr}_{cc}")
                        p.ops["sp"].append((lambda E: None, [("act_c", aa)], None, 1,
                                            "ws"))
            acc_done = p.at("ve_c")
            tm = t_matmul(p, tot_ps[0:1, 0:2], onescol[:], accG[:, 0:2], True, True,
                          waits=[("ve_c", acc_done)], inc="te_c", tag="GT")
            v_op(p, lambda E: E.tensor_copy(scal[0:1, 0:2], tot_ps[0:1, 0:2]),
                 waits=[("te_c", tm)], tag="G0")
            v_op(p, lambda E: E.tensor_scalar(scal[0:1, 0:2], scal[0:1, 0:2],
                                              1.0 / (2048.0 * 5120.0), None, ALU.mult),
                 tag="G1")
            v_op(p, lambda E: E.tensor_scalar_max(scal[0:1, 0:2], scal[0:1, 0:2],
                                                  1e-5), tag="G2")
            v_op(p, lambda E: E.tensor_scalar(scal[0:1, 2:4], scal[0:1, 0:2], 0.7,
                                              None, ALU.mult), tag="G3")
            v_op(p, lambda E: E.tensor_scalar(scal[0:1, 4:6], scal[0:1, 2:4], -1.0,
                                              None, ALU.mult), tag="G4")
            v_op(p, lambda E: E.tensor_copy(perm[0:1, 0:1], scal[0:1, 2:3]), tag="G5")
            v_op(p, lambda E: E.tensor_copy(perm[0:1, 1:2], scal[0:1, 4:5]), tag="G6")
            v_op(p, lambda E: E.tensor_copy(perm[0:1, 2:3], scal[0:1, 0:1]), tag="G7")
            v_op(p, lambda E: E.tensor_copy(perm[0:1, 3:4], scal[0:1, 3:4]), tag="G8")
            v_op(p, lambda E: E.tensor_copy(perm[0:1, 4:5], scal[0:1, 5:6]), tag="G9")
            v_op(p, lambda E: E.tensor_copy(perm[0:1, 5:6], scal[0:1, 1:2]), tag="Gx")
            pm_done = p.at("ve_c")
            bm = t_matmul(p, bc_ps[:, 0:6], ones_row[:], perm[0:1, 0:6], True, True,
                          waits=[("ve_c", pm_done)], inc="te_c", tag="Gy")
            thr_done = v_op(p, lambda E: E.tensor_copy(thrb[:], bc_ps[:, 0:6]),
                            waits=[("te_c", bm)], tag="Gz")

            c1G = p.sb("G_c1", [128, 512], F32)
            c2G = p.sb("G_c2", [128, 512], F32)
            wpan4 = p.sb("G_wp4", [128, 16 * 512], BF16)
            wpanp = p.sb("G_wpp", [128, 40 * 128], BF16)
            h2 = p.sb("G_h2", [128, 40 * 512], BF16)
            r_sbG = p.sb("G_r", [128, 512], BF16)
            g_sbG = p.sb("G_g", [128, 512], BF16)
            oacc = p.sb("G_oacc", [128, 4 * 2048], F32)
            fpsG = p.ps("G_fps", [128, 512], F32)
            ppsG = p.ps("G_pps", [128, 512], F32)
            trbfG = p.ps("G_trbf", [128, 128], BF16)

            def ternz(dst_ap, thr_col, width, tag):
                v_op(p, lambda E: E.tensor_scalar(c1G[:, 0:width], wstG[:, 0:width],
                                                  thrb[:, thr_col:thr_col + 1],
                                                  None, ALU.is_gt),
                     waits=[("ve_c", thr_done)], tag=tag + "1")
                v_op(p, lambda E: E.tensor_scalar(c2G[:, 0:width], wstG[:, 0:width],
                                                  thrb[:, thr_col + 1:thr_col + 2],
                                                  None, ALU.is_lt), tag=tag + "2")
                v_op(p, lambda E: E.tensor_sub(c1G[:, 0:width], c1G[:, 0:width],
                                               c2G[:, 0:width]), tag=tag + "3")
                return v_op(p, lambda E: E.tensor_scalar(
                    dst_ap, c1G[:, 0:width], thrb[:, thr_col + 2:thr_col + 3],
                    None, ALU.mult), tag=tag + "4")

            for tg in range(2):
                for tsub in range(4):
                    t = tg * 4 + tsub
                    p.dma("sp", oacc[:, tsub * 2048:(tsub + 1) * 2048],
                          x1_d[t * 128:(t + 1) * 128, :],
                          waits=[("stG", p.counts.get("stG", 0)),
                                 ("ve_c", p.at("ve_c")), ("stD", x1_done)],
                          stream="ldx1G", tag=f"GX{t}")
                ld_x1 = p.at("ldx1G")
                for hg in range(10):
                    cvp = 0
                    for c in range(16):
                        ldw = p.dma("sp", wstG[:], wfcT[c * 128:(c + 1) * 128,
                                                       hg * 512:(hg + 1) * 512],
                                    stream="ldw", tag=f"GF{tg}_{hg}_{c}")
                        p.ops["ve"].append((lambda E: None,
                                            [("ldw", p.at("ldw"))], None, 1, "wv"))
                        cvp = ternz(wpan4[:, c * 512:(c + 1) * 512], 0, 512,
                                    f"GH{tg}_{hg}_{c}")
                        p.ops["sp"].append((lambda E: None, [("ve_c", cvp)], None, 1,
                                            "ws"))
                    for hsub in range(4):
                        hc = hg * 4 + hsub
                        for c in range(16):
                            t_matmul(p, fpsG[:, 0:512],
                                     wpan4[:, c * 512 + hsub * 128:
                                           c * 512 + (hsub + 1) * 128],
                                     xn1_fm_sb[:, c * 1024 + tg * 512:
                                               c * 1024 + (tg + 1) * 512],
                                     start=(c == 0), stop=(c == 15),
                                     waits=[("ve_c", max(cvp, p.at("ve_c")))] if c == 0 else (),
                                     inc="te_c", tag=f"GM{tg}_{hc}_{c}")
                        te_f = p.at("te_c")
                        ar = a_op(p, lambda E: E.activation(r_sbG[:], fpsG[:, 0:512],
                                                            AF.Relu),
                                  waits=[("te_c", te_f)], tag=f"GR{tg}_{hc}")
                        v_op(p, lambda E, hc=hc: E.tensor_mul(
                            h2[:, hc * 512:(hc + 1) * 512], r_sbG[:], r_sbG[:]),
                            waits=[("act_c", ar)], tag=f"GQ{tg}_{hc}")
                h2_done = p.at("ve_c")
                for og in range(16):
                    cvp = 0
                    for i in range(40):
                        ldw = p.dma("sp", wstG[:, 0:128],
                                    wprT[i * 128:(i + 1) * 128,
                                         og * 128:(og + 1) * 128],
                                    stream="ldw", tag=f"GP{tg}_{og}_{i}")
                        p.ops["ve"].append((lambda E: None,
                                            [("ldw", p.at("ldw"))], None, 1, "wv"))
                        cvp = ternz(wpanp[:, i * 128:(i + 1) * 128], 3, 128,
                                    f"GI{tg}_{og}_{i}")
                        p.ops["sp"].append((lambda E: None, [("ve_c", cvp)], None, 1,
                                            "ws"))
                    for i in range(40):
                        t_matmul(p, ppsG[:, 0:512], wpanp[:, i * 128:(i + 1) * 128],
                                 h2[:, i * 512:(i + 1) * 512],
                                 start=(i == 0), stop=(i == 39),
                                 waits=[("ve_c", max(cvp, h2_done))] if i == 0 else (),
                                 inc="te_c", tag=f"GN{tg}_{og}_{i}")
                    te_p = p.at("te_c")
                    asil = a_op(p, lambda E: E.activation(g_sbG[:], ppsG[:, 0:512],
                                                          AF.Silu),
                                waits=[("te_c", te_p)], tag=f"GS{tg}_{og}")
                    v_op(p, lambda E, og=og, tg=tg: E.tensor_mul(
                        g_sbG[:], g_sbG[:],
                        s1_sb[:, og * 1024 + tg * 512: og * 1024 + (tg + 1) * 512]),
                        waits=[("act_c", asil), ("ve_c", s1_done)], tag=f"GG{tg}_{og}")
                    for tsub in range(4):
                        tt = t_transpose(p, trbfG[:],
                                         g_sbG[:, tsub * 128:(tsub + 1) * 128],
                                         id_b[:], waits=[("ve_c", p.at("ve_c"))],
                                         inc="te_c", tag=f"GU{tg}_{og}_{tsub}")
                        v_op(p, lambda E, tsub=tsub, og=og: E.tensor_add(
                            oacc[:, tsub * 2048 + og * 128:
                                 tsub * 2048 + (og + 1) * 128],
                            oacc[:, tsub * 2048 + og * 128:
                                 tsub * 2048 + (og + 1) * 128],
                            trbfG[:]),
                            waits=[("te_c", tt), ("ldx1G", ld_x1)],
                            tag=f"GO{tg}_{og}_{tsub}")
                oa_done = p.at("ve_c")
                for tsub in range(4):
                    t = tg * 4 + tsub
                    p.dma("sp", out[t * 128:(t + 1) * 128, :],
                          oacc[:, tsub * 2048:(tsub + 1) * 2048],
                          waits=[("ve_c", oa_done)], stream="stG", tag=f"GZ{t}")

    nc = p.finalize(final_waits=[("stG", p.at("stG"))])
    return nc, p


T = 8192
NCORE = 8

def tok_idx(c):
    return np.concatenate([np.arange(128 * j, 128 * (j + 1)) for j in chunks_of(c)])

def _mk_masks(c):
    ch = chunks_of(c)
    m = np.zeros((64, 128, 128), np.float32)
    kk = np.arange(128)[:, None]; qq = np.arange(128)[None, :]
    for kc in range(64):
        j = ch[kc // 8]
        if kc < j:
            m[kc] = 1.0
        elif kc == j:
            m[kc] = (128 * kc + kk <= 128 * j + qq)
    return m.astype(ml_dtypes.bfloat16)

_NC_CACHE = {}

def kernel(**inputs):
    if "nc" not in _NC_CACHE:
        _NC_CACHE["nc"] = build_fused()[0]
    nc = _NC_CACHE["nc"]
    x_full = np.ascontiguousarray(
        np.asarray(inputs["L0_attn_x_q"], np.float32).reshape(T, 2048))
    wq0 = np.ascontiguousarray(np.asarray(inputs["L0_attn_Wq"]).reshape(2048, 2048))
    wk0 = np.ascontiguousarray(np.asarray(inputs["L0_attn_Wk"]).reshape(2048, 512))
    wv0 = np.ascontiguousarray(np.asarray(inputs["L0_attn_Wv"]).reshape(2048, 512))
    gwt = np.ascontiguousarray(np.asarray(inputs["L0_ffn_gate_w"]).T)
    wo0 = np.ascontiguousarray(np.asarray(inputs["L0_attn_Wo"]))
    wg_all = np.asarray(inputs["L0_ffn_w_gate"])
    wu_all = np.asarray(inputs["L0_ffn_w_up"])
    wd_all = np.asarray(inputs["L0_ffn_w_down"])
    wsg = np.ascontiguousarray(np.asarray(inputs["L0_ffn_ws_gate"]))
    wsu = np.ascontiguousarray(np.asarray(inputs["L0_ffn_ws_up"]))
    wsd = np.ascontiguousarray(np.asarray(inputs["L0_ffn_ws_down"]))
    wq1 = np.ascontiguousarray(np.asarray(inputs["L1_attn_Wq"]))
    wk1 = np.ascontiguousarray(np.asarray(inputs["L1_attn_Wk"]))
    wv1 = np.ascontiguousarray(np.asarray(inputs["L1_attn_Wv"]))
    bqT = np.ascontiguousarray(np.asarray(inputs["L1_attn_bq"]).T)
    bkT = np.ascontiguousarray(np.asarray(inputs["L1_attn_bk"]).T)
    bvT = np.ascontiguousarray(np.asarray(inputs["L1_attn_bv"]).T)
    wfcT = np.ascontiguousarray(np.asarray(inputs["L1_ffn_w_fc"]).T)
    wprT = np.ascontiguousarray(np.asarray(inputs["L1_ffn_w_proj"]).T)

    in_maps = []
    for c in range(NCORE):
        ti = tok_idx(c)
        in_maps.append(dict(
            x=np.ascontiguousarray(x_full[ti]),
            wq=wq0, wk=wk0, wv=wv0, gwt=gwt, wo=wo0,
            e0vec=np.full((128, 1), 2.0 * c, np.float32),
            wg=np.ascontiguousarray(wg_all[2 * c:2 * c + 2]),
            wu=np.ascontiguousarray(wu_all[2 * c:2 * c + 2]),
            wd=np.ascontiguousarray(wd_all[2 * c:2 * c + 2]),
            myidx=ti.astype(np.uint32).reshape(1024, 1),
            wsg=wsg, wsu=wsu, wsd=wsd,
            wq1=wq1, bqT=bqT, wk1=wk1, bkT=bkT, wv1=wv1, bvT=bvT,
            masks=_mk_masks(c), wfcT=wfcT, wprT=wprT,
        ))
    res = run_bass_kernel_spmd(nc, in_maps, core_ids=list(range(NCORE)))
    final = np.empty((T, 2048), np.float32)
    for c in range(NCORE):
        final[tok_idx(c)] = np.asarray(res.results[c]["out"])
    return final.reshape(1, T, 2048)


# version 03
# speedup vs baseline: 1.8296x; 1.8296x over previous
import os
"""Program-recorder framework for raw-Bass: write phases in dataflow order,
emit per-engine instruction streams inside one nc.Block().

Engines: "te" (tensor), "ve" (vector/DVE), "act" (scalar), "po" (gpsimd/pool),
"sp" (sync queue, DMA only).

Semaphores are created on demand; wait values are resolved at RECORD time, so a
wait can only reference increments already recorded -> no deadlocks by
construction (as long as the dataflow order is correct).
"""
import numpy as np
import ml_dtypes
from contextlib import ExitStack
import concourse.bass as bass
import concourse.mybir as mybir
from concourse.bass_utils import run_bass_kernel_spmd
from concourse.bass_core_rust import IndirectOffsetOnAxis

F32 = mybir.dt.float32
BF16 = mybir.dt.bfloat16
U32 = mybir.dt.uint32
AF = mybir.ActivationFunctionType
ALU = mybir.AluOpType
AX = mybir.AxisListType

ENGINES = ("te", "ve", "act", "po", "sp")

class Prog:
    def __init__(self, name="prog"):
        self.nc = bass.Bass("TRN2", target_bir_lowering=False)
        self.es = ExitStack()
        self.ops = {e: [] for e in ENGINES}
        self.sems = {}
        self.counts = {}
        self.name = name

    def sem(self, name):
        if name not in self.sems:
            self.sems[name] = self.es.enter_context(self.nc.semaphore(name))
            self.counts[name] = 0
        return self.sems[name]

    def at(self, name):
        self.sem(name)
        return self.counts[name]

    def sb(self, name, shape, dtype):
        es = getattr(self, "_scope_es", None) or self.es
        return es.enter_context(self.nc.sbuf_tensor(name, shape, dtype))

    def fence(self, tag="fence"):
        snap = [(s, v) for s, v in self.counts.items() if v > 0]
        for e in ENGINES:
            self.ops[e].append((lambda E: None, list(snap), None, 1, tag))

    def scope(self):
        from contextlib import ExitStack, contextmanager
        @contextmanager
        def _s():
            old = getattr(self, "_scope_es", None)
            self._scope_es = ExitStack()
            try:
                yield
            finally:
                self._scope_es.close()
                self._scope_es = old
                self.fence("scope_fence")
        return _s()

    def ps(self, name, shape, dtype=F32):
        es = getattr(self, "_scope_es", None) or self.es
        return es.enter_context(self.nc.psum_tensor(name, shape, dtype))

    def din(self, name, shape, dtype):
        return self.nc.dram_tensor(name, shape, dtype, kind="ExternalInput").ap()

    def dout(self, name, shape, dtype):
        return self.nc.dram_tensor(name, shape, dtype, kind="ExternalOutput").ap()

    def dtmp(self, name, shape, dtype):
        return self.nc.dram_tensor(name, shape, dtype).ap()

    AUTOSYNC = ("ve", "act", "po")

    def emit(self, eng, fn, waits=(), inc=None, inc_k=1, tag="", chain=True):
        w = [(s, v) for (s, v) in waits]
        if eng in self.AUTOSYNC and chain:
            ec = eng + "_c"
            if inc is None:
                inc = ec
                inc_k = 1
            if inc == ec:
                w.append((ec, self.at(ec)))
        self.ops[eng].append((fn, w, inc, inc_k, tag))
        if inc is not None:
            self.sem(inc)
            self.counts[inc] += inc_k
            return self.counts[inc]
        return None

    def dma(self, eng, out, in_, waits=(), stream=None, tag="", **kw):
        s = stream or (eng + "_dma")
        return self.emit(eng, lambda E, o=out, i=in_: E.dma_start(out=o, in_=i, **kw),
                         waits=waits, inc=s, inc_k=16, tag=tag)

    def finalize(self, final_waits=()):
        for (s, v) in final_waits:
            self.emit("sp", lambda E: None, waits=[(s, v)], tag="final_wait")
        nc = self.nc
        with nc.Block() as blk:
            def mk_body(ename):
                def body(E):
                    waited = {}
                    for fn, waits, inc, inc_k, tag in self.ops[ename]:
                        for (s, v) in waits:
                            if v <= 0 or waited.get(s, -1) >= v:
                                continue
                            E.wait_ge(self.sems[s], v)
                            waited[s] = v
                        instr = fn(E)
                        if inc is not None:
                            assert instr is not None, f"op with inc returned None: {tag}"
                            instr.then_inc(self.sems[inc], inc_k)
                return body
            if self.ops["sp"]:
                blk.sync(mk_body("sp"))
            if self.ops["te"]:
                blk.tensor(mk_body("te"))
            if self.ops["ve"]:
                blk.vector(mk_body("ve"))
            if self.ops["act"]:
                blk.scalar(mk_body("act"))
            if self.ops["po"]:
                blk.gpsimd(mk_body("po"))
        return nc

    def close(self):
        self.es.close()


def t_matmul(p, out_ap, lhsT, rhs, start, stop, waits=(), inc=None, tag=""):
    return p.emit("te", lambda E, o=out_ap, l=lhsT, r=rhs, st=start, sp=stop:
                  E.matmul(o, l, r, start=st, stop=sp),
                  waits=waits, inc=inc, tag=tag)

def t_transpose(p, out_ap, in_ap, ident, waits=(), inc=None, tag=""):
    return p.emit("te", lambda E, o=out_ap, i=in_ap, d=ident: E.transpose(o, i, d),
                  waits=waits, inc=inc, tag=tag)

def v_op(p, fn, waits=(), inc=None, tag=""):
    return p.emit("ve", fn, waits=waits, inc=inc, tag=tag)

def a_op(p, fn, waits=(), inc=None, tag=""):
    return p.emit("act", fn, waits=waits, inc=inc, tag=tag)

def g_op(p, fn, waits=(), inc=None, inc_k=1, tag=""):
    return p.emit("po", fn, waits=waits, inc=inc, inc_k=inc_k, tag=tag)

def bcast_ap(t, P, n, m, col=0):
    ap = t[:, col:col+n]
    a = ap.ap
    return bass.AP(ap.tensor, ap.offset, [list(a[0]), list(a[1]), [0, m]])


"""Shared MoE routing emitter.

Computes, replicated from full-sequence logits [8192,16] fp32 (DRAM, token-major):
  - per tile j (64 tiles of 128 tokens): softmax probs, top-2 (w1,w2,e1,e2)
  - global exclusive cumsum positions pos1/pos2 per token
Results left in SBUF arrays (fp32):
  warr1/warr2 [128,64], earr1/earr2 [128,64], pos1/pos2 [128,64]
Caller decides what to do with them (scatter gather-lists / compute slotids).
"""

def emit_routing(p, lg_dram, own_tiles=None, lgt_filler=None):
    if own_tiles is None:
        own_tiles = list(range(64))
    lgt = p.sb("r_lgt", [128, 64 * 16], F32)
    exs = p.sb("r_exs", [128, 16], F32)
    m1 = p.sb("r_m1", [128, 1], F32)
    nm = p.sb("r_nm", [128, 1], F32)
    ssum = p.sb("r_ssum", [128, 1], F32)
    rs = p.sb("r_rs", [128, 1], F32)
    probs = p.sb("r_probs", [128, 16], F32)
    m8 = p.sb("r_m8", [128, 8], F32)
    i8 = p.sb("r_i8", [128, 8], U32)
    i1f = p.sb("r_i1f", [128, 1], F32)
    i2f = p.sb("r_i2f", [128, 1], F32)
    oh1 = p.sb("r_oh1", [128, 64 * 16], BF16)
    oh2 = p.sb("r_oh2", [128, 64 * 16], BF16)
    ohs = p.sb("r_ohs", [128, 16], F32)
    iota16 = p.sb("r_iota16", [128, 16], F32)
    nem = p.sb("r_nem", [16, 8192], BF16)
    ones16 = p.sb("r_ones16", [16, 1], F32)
    sincl = p.sb("r_sincl", [16, 8192], F32)
    sext = p.sb("r_sext", [128, 16], F32)
    tmp16 = p.sb("r_tmp16", [128, 16], F32)
    warr1 = p.sb("r_w1", [128, 64], F32)
    warr2 = p.sb("r_w2", [128, 64], F32)
    earr1 = p.sb("r_e1", [128, 64], F32)
    earr2 = p.sb("r_e2", [128, 64], F32)
    pos1 = p.sb("r_pos1", [128, 64], F32)
    pos2 = p.sb("r_pos2", [128, 64], F32)
    trp16 = p.ps("r_trp16", [16, 128], F32)
    trp128 = p.ps("r_trp128", [128, 16], F32)

    lg_ve = 0
    if lgt_filler is None:
        p.dma("sp", lgt[:], bass.AP(lg_dram.tensor, 0, [[16, 128], [2048, 64], [1, 16]]),
              stream="r_ld", tag="lgt")
    else:
        lg_ve = lgt_filler(p, lgt)
    lg_done = p.at("r_ld")

    g_op(p, lambda E: E.iota(iota16[:], [[1, 16]], base=0, channel_multiplier=0,
                             allow_small_or_imprecise_dtypes=True), tag="iota16")
    io_done = p.at("po_c")
    v_op(p, lambda E: E.memset(ones16[:], 1.0), tag="ones16")

    id_done = p.at("ve_c")
    id_f = p.id_f

    for j in range(64):
        lj = lgt[:, j * 16:(j + 1) * 16]
        v_op(p, lambda E, lj=lj: E.reduce_max(m1[:], lj, axis=AX.X),
             waits=[("r_ld", lg_done), ("ve_c", max(lg_ve, p.at("ve_c")))], tag=f"rm{j}")
        v_op(p, lambda E: E.tensor_scalar(nm[:], m1[:], -1.0, None, ALU.mult), tag=f"rn{j}")
        nm_ve = p.at("ve_c")
        ae = a_op(p, lambda E, lj=lj: E.activation(exs[:], lj, AF.Exp, bias=nm[:], scale=1.0),
                  waits=[("ve_c", nm_ve)], tag=f"re{j}")
        v_op(p, lambda E: E.reduce_sum(ssum[:], exs[:], axis=AX.X),
             waits=[("act_c", ae)], tag=f"rs{j}")
        v_op(p, lambda E: E.reciprocal(rs[:], ssum[:]), tag=f"rr{j}")
        v_op(p, lambda E: E.tensor_scalar(probs[:], exs[:], rs[:], None, ALU.mult), tag=f"rp{j}")
        v_op(p, lambda E: E.max(m8[:], probs[:]), tag=f"rx{j}")
        v_op(p, lambda E: E.max_index(i8[:], m8[:], probs[:]), tag=f"ri{j}")
        v_op(p, lambda E, j=j: E.tensor_copy(warr1[:, j:j + 1], m8[:, 0:1]), tag=f"rw1{j}")
        v_op(p, lambda E, j=j: E.tensor_copy(warr2[:, j:j + 1], m8[:, 1:2]), tag=f"rw2{j}")
        v_op(p, lambda E: E.tensor_copy(i1f[:], i8[:, 0:1]), tag=f"rc1{j}")
        v_op(p, lambda E: E.tensor_copy(i2f[:], i8[:, 1:2]), tag=f"rc2{j}")
        v_op(p, lambda E, j=j: E.tensor_copy(earr1[:, j:j + 1], i1f[:]), tag=f"re1{j}")
        v_op(p, lambda E, j=j: E.tensor_copy(earr2[:, j:j + 1], i2f[:]), tag=f"re2{j}")
        v_op(p, lambda E, j=j: E.tensor_scalar(oh1[:, j * 16:(j + 1) * 16], iota16[:],
                                               i1f[:], None, ALU.is_equal),
             waits=[("po_c", io_done)], tag=f"ro1{j}")
        v_op(p, lambda E, j=j: E.tensor_scalar(oh2[:, j * 16:(j + 1) * 16], iota16[:],
                                               i2f[:], None, ALU.is_equal), tag=f"ro2{j}")
        v_op(p, lambda E, j=j: E.tensor_add(ohs[:], oh1[:, j * 16:(j + 1) * 16],
                                            oh2[:, j * 16:(j + 1) * 16]), tag=f"roa{j}")
        ve_ohs = p.at("ve_c")
        tt = t_transpose(p, trp16[:], ohs[:], id_f[:],
                         waits=[("ve_c", ve_ohs)], inc="te_c", tag=f"rt{j}")
        v_op(p, lambda E, j=j: E.tensor_copy(nem[:, j * 128:(j + 1) * 128], trp16[:]),
             waits=[("te_c", tt)], tag=f"rte{j}")
    nem_done = p.at("ve_c")
    ones_b = bass.AP(ones16[:].tensor, 0, [list(ones16[:].ap[0]), [0, 8192]])
    v_op(p, lambda E: E.tensor_tensor_scan(sincl[:], ones_b, nem[:], 0.0,
                                           ALU.mult, ALU.add),
         waits=[("ve_c", nem_done)], tag="rscan")
    v_op(p, lambda E: E.tensor_sub(sincl[:], sincl[:], nem[:]), tag="rsub")
    sx_done = p.at("ve_c")
    for j in own_tiles:
        tt = t_transpose(p, trp128[:], sincl[:, j * 128:(j + 1) * 128], id_f[0:16, 0:16],
                         waits=[("ve_c", sx_done), ("ve_c", p.at("ve_c"))],
                         inc="te_c", tag=f"rpt{j}")
        ve0 = v_op(p, lambda E: E.tensor_copy(sext[:], trp128[:]),
                   waits=[("te_c", tt)], tag=f"rpe{j}")
        v_op(p, lambda E, j=j: E.tensor_mul(tmp16[:], oh1[:, j * 16:(j + 1) * 16], sext[:]),
             tag=f"rq1{j}")
        v_op(p, lambda E, j=j: E.reduce_sum(pos1[:, j:j + 1], tmp16[:], axis=AX.X),
             tag=f"rr1{j}")
        v_op(p, lambda E, j=j: E.tensor_mul(tmp16[:], oh2[:, j * 16:(j + 1) * 16], sext[:]),
             tag=f"rq2{j}")
        v_op(p, lambda E, j=j: E.reduce_sum(pos2[:, j:j + 1], tmp16[:], axis=AX.X),
             tag=f"rr2{j}")
    done = p.at("ve_c")
    return dict(warr1=warr1, warr2=warr2, earr1=earr1, earr2=earr2,
                pos1=pos1, pos2=pos2, done=done)

"""Single-launch fused megakernel: both layers, 3 AllGathers, activations
resident in SBUF between phases (x1 bounced via local DRAM).

Inputs per core: x [1024,2048] f32, wq [2048,2048] f32, wk/wv [2048,512],
gwt [2048,16], wo [2048,2048], e0vec [128,1] f32, wg/wu [2,2048,1408],
wd [2,1408,2048], myidx [1024,1] u32 (GLOBAL token ids of own tokens),
wsg/wsu [2048,2816], wsd [2816,2048], wq1 [16,2048,128], bqT [128,16],
wk1/wv1 [4,2048,128], bkT/bvT [128,4], masks [64,128,128] bf16,
wfcT [2048,5120], wprT [5120,2048].   Output: out [1024,2048] f32.

AG row convention: gathered full-sequence tensors are rank-major: row
r*1024+i = token tok_idx(r)[i]. The MoE token list stores AG-row ids
(base agbase(j) per global tile). rodram slot/weight arrays are indexed by
GLOBAL token id (tile-major j*128+p), gathered with myidx.
L1 causality: iterate k-chunks in GLOBAL order, index AG-ordered buffers at
agcol(kc)."""

def chunks_of(c):
    return sorted([c, 15 - c, 16 + c, 31 - c, 32 + c, 47 - c, 48 + c, 63 - c])

_OWNER = {}
for _c in range(8):
    for _m, _j in enumerate(chunks_of(_c)):
        _OWNER[_j] = (_c, _m)

def agcol(j):
    r, m = _OWNER[j]
    return 8 * r + m

def agbase(j):
    r, m = _OWNER[j]
    return r * 1024 + m * 128

SCALE = 1.0 / (128.0 ** 0.5)


def build_fused():
    p = Prog("FUSED")
    x_in = p.din("x", [1024, 2048], F32)
    wq = p.din("wq", [2048, 2048], F32)
    wk = p.din("wk", [2048, 512], F32)
    wv = p.din("wv", [2048, 512], F32)
    gwt = p.din("gwt", [2048, 16], F32)
    wo = p.din("wo", [2048, 2048], F32)
    e0vec = p.din("e0vec", [128, 1], F32)
    wg = p.din("wg", [2, 2048, 1408], F32)
    wu = p.din("wu", [2, 2048, 1408], F32)
    wd = p.din("wd", [2, 1408, 2048], F32)
    myidx = p.din("myidx", [1024, 1], U32)
    wsg = p.din("wsg", [2048, 2816], F32)
    wsu = p.din("wsu", [2048, 2816], F32)
    wsd = p.din("wsd", [2816, 2048], F32)
    wq1 = p.din("wq1", [16, 2048, 128], F32)
    bqT = p.din("bqT", [128, 16], F32)
    wk1 = p.din("wk1", [4, 2048, 128], F32)
    bkT = p.din("bkT", [128, 4], F32)
    wv1 = p.din("wv1", [4, 2048, 128], F32)
    bvT = p.din("bvT", [128, 4], F32)
    masks = p.din("masks", [64, 128, 128], BF16)
    wfcT = p.din("wfcT", [2048, 5120], F32)
    wprT = p.din("wprT", [5120, 2048], F32)
    out = p.dout("out", [1024, 2048], F32)

    xn_c = p.dtmp("xn_c", [1024, 2048], BF16)
    kh_c = p.dtmp("kh_c", [512, 1024], BF16)
    v_c = p.dtmp("v_c", [1024, 512], BF16)
    lg_c = p.dtmp("lg_c", [16, 1024], F32)
    xn_ag = p.dtmp("xn_ag", [8192, 2048], BF16)
    kh_ag = p.dtmp("kh_ag", [4096, 1024], BF16)
    v_ag = p.dtmp("v_ag", [8192, 512], BF16)
    lg_ag = p.dtmp("lg_ag", [128, 1024], F32)
    attn_d = p.dtmp("attn_d", [2048, 1024], BF16)
    listd = p.dtmp("listd", [144, 128], U32)
    out_c = p.dtmp("out_c", [2048, 2048], BF16)
    outf_ag = p.dtmp("outf_ag", [16384, 2048], BF16)
    rodram = p.dtmp("rodram", [4, 64, 128], F32)
    x1_d = p.dtmp("x1_d", [1024, 2048], F32)
    k1_c = p.dtmp("k1_c", [512, 1024], BF16)
    v1_c = p.dtmp("v1_c", [1024, 512], BF16)
    k1_ag = p.dtmp("k1_ag", [4096, 1024], BF16)
    v1_ag = p.dtmp("v1_ag", [8192, 512], BF16)

    idw = p.sb("idw", [128, 128], F32)
    id_f = p.sb("id_f", [128, 128], F32)
    id_b = p.sb("id_b", [128, 128], BF16)
    eps5 = p.sb("eps5", [128, 1], F32)
    eps6 = p.sb("eps6", [128, 1], F32)
    ones_b = p.sb("ones_b", [128, 1], BF16)
    ones_row = p.sb("ones_row", [1, 128], F32)
    onescol = p.sb("onescol", [128, 1], F32)
    iota_p = p.sb("iota_p", [128, 1], F32)
    g_op(p, lambda E: E.iota(idw[:], [[-1, 128]], base=0, channel_multiplier=1,
                             allow_small_or_imprecise_dtypes=True), tag="iota")
    g_op(p, lambda E: E.iota(iota_p[:], [[0, 1]], base=0, channel_multiplier=1,
                             allow_small_or_imprecise_dtypes=True), tag="iop")
    io_done = p.at("po_c")
    v_op(p, lambda E: E.tensor_scalar(id_f[:], idw[:], 0.0, None, ALU.is_equal),
         waits=[("po_c", io_done)], tag="idf")
    v_op(p, lambda E: E.tensor_copy(id_b[:], id_f[:]), tag="idb")
    v_op(p, lambda E: E.memset(eps5[:], 1e-5), tag="eps5")
    v_op(p, lambda E: E.memset(eps6[:], 1e-6), tag="eps6")
    v_op(p, lambda E: E.memset(ones_b[:], 1.0), tag="ones_b")
    v_op(p, lambda E: E.memset(ones_row[:], 1.0), tag="ones_row")
    v_op(p, lambda E: E.memset(onescol[:], 1.0), tag="onescol")
    p.id_f = id_f
    const_done = p.at("ve_c")

    def ln_tiles(pfx, x_dram, dst_fm_sb, xn_tm_dram=None, logits=None,
                 x_ready=()):
        xt_b = [p.sb(pfx + f"xt{i}", [128, 2048], F32) for i in range(2)]
        xnf32 = p.sb(pfx + "xnf32", [128, 2048], F32)
        xn_tm_bf = p.sb(pfx + "xntm", [128, 2048], BF16)
        sumx = p.sb(pfx + "sumx", [128, 8], F32)
        ssq = p.sb(pfx + "ssq", [128, 8], F32)
        meanv = p.sb(pfx + "mean", [128, 8], F32)
        varv = p.sb(pfx + "var", [128, 8], F32)
        m2 = p.sb(pfx + "m2", [128, 8], F32)
        rstd = p.sb(pfx + "rstd", [128, 8], F32)
        sq_scr = p.sb(pfx + "sq", [128, 2048], F32)
        lgscr = p.sb(pfx + "lg", [128, 2048], F32)
        trps = p.ps(pfx + "trps", [128, 128], F32)
        if logits is not None:
            gwt_sb, lgT_sb, lgps, gwt_ld = logits
        prev_tile_te = p.at("te_c")
        last_xn_ve = [0, 0]
        for t in range(8):
            par = t % 2
            ldx = p.dma("sp", xt_b[par][:], x_dram[t * 128:(t + 1) * 128, :],
                        waits=[("ve_c", last_xn_ve[par])] + list(x_ready),
                        stream="ld" + pfx + str(par), tag=f"{pfx}x{t}")
            xt = xt_b[par][:]
            v_op(p, lambda E, xt=xt, t=t: E.reduce_sum(sumx[:, t:t + 1], xt, axis=AX.X),
                 waits=[("ld" + pfx + str(par), ldx)], tag=f"{pfx}sum{t}")
            a_op(p, lambda E, xt=xt, t=t: E.activation(sq_scr[:], xt, AF.Square,
                                                       accum_out=ssq[:, t:t + 1]),
                 waits=[("ld" + pfx + str(par), ldx), ("ve_c", p.at("ve_c"))],
                 tag=f"{pfx}ssq{t}")
            act_ssq = p.at("act_c")
            v_op(p, lambda E, t=t: E.tensor_scalar(meanv[:, t:t + 1], sumx[:, t:t + 1],
                                                   1.0 / 2048, None, ALU.mult),
                 tag=f"{pfx}mn{t}")
            v_op(p, lambda E, t=t: E.tensor_scalar(varv[:, t:t + 1], ssq[:, t:t + 1],
                                                   1.0 / 2048, None, ALU.mult),
                 waits=[("act_c", act_ssq)], tag=f"{pfx}vr{t}")
            v_op(p, lambda E, t=t: E.scalar_tensor_tensor(m2[:, t:t + 1],
                                                          meanv[:, t:t + 1], 1.0,
                                                          meanv[:, t:t + 1],
                                                          ALU.mult, ALU.mult),
                 tag=f"{pfx}m2{t}")
            v_op(p, lambda E, t=t: E.tensor_sub(varv[:, t:t + 1], varv[:, t:t + 1],
                                                m2[:, t:t + 1]), tag=f"{pfx}v2{t}")
            ve_var = p.at("ve_c")
            a_op(p, lambda E, t=t: E.activation(rstd[:, t:t + 1], varv[:, t:t + 1],
                                                AF.Ln, bias=eps5[:], scale=1.0),
                 waits=[("ve_c", max(ve_var, const_done))], tag=f"{pfx}ln{t}")
            a_op(p, lambda E, t=t: E.activation(rstd[:, t:t + 1], rstd[:, t:t + 1],
                                                AF.Exp, bias=0.0, scale=-0.5),
                 tag=f"{pfx}ex{t}")
            act_rstd = p.at("act_c")
            xn_ve = v_op(p, lambda E, xt=xt, t=t: E.tensor_scalar(
                xnf32[:], xt, meanv[:, t:t + 1], rstd[:, t:t + 1],
                ALU.subtract, ALU.mult),
                waits=[("act_c", act_rstd), ("te_c", prev_tile_te)], tag=f"{pfx}xn{t}")
            last_xn_ve[par] = xn_ve
            if xn_tm_dram is not None:
                cvt = v_op(p, lambda E: E.tensor_copy(xn_tm_bf[:], xnf32[:]),
                           waits=[("st" + pfx, p.counts.get("st" + pfx, 0))],
                           tag=f"{pfx}xtm{t}")
                p.dma("sp", xn_tm_dram[t * 128:(t + 1) * 128, :], xn_tm_bf[:],
                      waits=[("ve_c", cvt)], stream="st" + pfx, tag=f"{pfx}oxtm{t}")
            for c in range(16):
                tt = t_transpose(p, trps[:], xnf32[:, c * 128:(c + 1) * 128], id_f[:],
                                 waits=[("ve_c", p.at("ve_c")), ("ve_c", const_done),
                                        ("act_c", p.at("act_c"))],
                                 inc="te_c", tag=f"{pfx}tr{t}_{c}")
                a_ev = a_op(p, lambda E, c=c: E.activation(
                    lgscr[:, c * 128:(c + 1) * 128], trps[:], AF.Identity),
                    waits=[("te_c", tt)], tag=f"{pfx}lgev{t}_{c}")
                v_op(p, lambda E, c=c, t=t: E.tensor_copy(
                    dst_fm_sb[:, c * 1024 + t * 128: c * 1024 + (t + 1) * 128],
                    lgscr[:, c * 128:(c + 1) * 128]),
                    waits=[("act_c", a_ev)], tag=f"{pfx}fm{t}_{c}")
                if logits is not None:
                    t_matmul(p, lgps[:, 0:128], gwt_sb[:, c * 16:(c + 1) * 16],
                             lgscr[:, c * 128:(c + 1) * 128],
                             start=(c == 0), stop=(c == 15),
                             waits=[("act_c", p.at("act_c")), ("ve_c", p.at("ve_c")),
                                    ("ld" + pfx + "g", gwt_ld)],
                             inc="te_c", tag=f"{pfx}lgmm{t}_{c}")
            prev_tile_te = p.at("te_c")
            if logits is not None:
                v_op(p, lambda E, t=t: E.tensor_copy(lgT_sb[:, t * 128:(t + 1) * 128],
                                                     lgps[:, 0:128]),
                     waits=[("te_c", prev_tile_te)], tag=f"{pfx}lge2{t}")

    class PanelStream:
        def __init__(self, name, sub=2048):
            self.stg = [p.sb(name + "s0", [128, sub], F32),
                        p.sb(name + "s1", [128, sub], F32)]
            self.sub = sub
            self.prev = [0, 0]
            self.i = 0

        def load(self, tag, subs, conv=None, te_guard=None):
            cv = 0
            for idx, (src, ncols, dst_ap) in enumerate(subs):
                par = self.i % 2
                self.i += 1
                ld = p.dma("sp", self.stg[par][:, 0:ncols], src,
                           waits=[("ve_c", self.prev[par])], stream=f"ldw{par}",
                           tag=f"{tag}L{idx}")
                w = [(f"ldw{par}", ld),
                     ("te_c", p.at("te_c") if te_guard is None else te_guard)]
                if conv is None:
                    cv = v_op(p, lambda E, par=par, ncols=ncols, dst_ap=dst_ap:
                              E.tensor_copy(dst_ap, self.stg[par][:, 0:ncols]),
                              waits=w, tag=f"{tag}C{idx}")
                else:
                    cv = conv(self.stg[par][:, 0:ncols], dst_ap, w)
                self.prev[par] = cv
            return cv

    def wsub3(wt, row_pitch, c0, nc_, col0, ncols):
        return bass.AP(wt.tensor, (c0 * 128) * row_pitch + col0,
                       [[row_pitch, 128], [128 * row_pitch, nc_], [1, ncols]])

    with p.scope():
        xn_fm_sb = p.sb("XN0", [128, 16 * 1024], BF16)
        s_tm_sb = p.sb("S0", [128, 8 * 2048], BF16)
        with p.scope():
            qh_sb = p.sb("QH", [128, 16 * 1024], BF16)
            with p.scope():
                gwt_sb = p.sb("A_gwt", [128, 256], F32)
                lgT_sb = p.sb("A_lgT", [16, 1024], F32)
                lgps = p.ps("A_lgps", [16, 128], F32)
                p.dma("sp", gwt_sb[:],
                      bass.AP(gwt.tensor, 0, [[16, 128], [2048, 16], [1, 16]]),
                      stream="ldAg", tag="Agwt")
                gwt_ld = p.at("ldAg")
                ln_tiles("A", x_in, xn_fm_sb, xn_tm_dram=xn_c,
                         logits=(gwt_sb, lgT_sb, lgps, gwt_ld))
                ve_lg = p.at("ve_c")
                p.dma("sp", lg_c, lgT_sb[:], waits=[("ve_c", ve_lg)], stream="stA",
                      tag="AolgT")

                psA = PanelStream("A_ps")
                wpan2 = [p.sb("A_wpan0", [128, 16 * 512], BF16),
                         p.sb("A_wpan1", [128, 16 * 512], BF16)]
                sqq = p.sb("A_sqq", [128, 512], F32)
                ssqq = p.sb("A_ssqq", [128, 4], F32)
                rstd4 = p.sb("A_rstd4", [128, 4], F32)
                qtm_sb = p.sb("A_qtm", [128, 512], BF16)
                qev = p.sb("A_qev", [128, 128], BF16)
                qps = p.ps("A_qps", [128, 512], F32)
                trbf = p.ps("A_trbf", [128, 128], BF16)

                teA = {}
                def projA(wdram, pitch, ngrp, dest_kind, wtag):
                    for grp in range(ngrp):
                        par = (wtag, grp % 2)
                        wpan = wpan2[grp % 2]
                        cv = psA.load(
                            "A" + wtag + str(grp),
                            [(wsub3(wdram, pitch, 4 * s, 4, grp * 512, 512), 2048,
                              wpan[:, s * 4 * 512:(s + 1) * 4 * 512])
                             for s in range(4)],
                            te_guard=teA.get(grp % 2, 0))
                        prev_cv = 0
                        for t in range(8):
                            for c in range(16):
                                t_matmul(p, qps[:, 0:512],
                                         xn_fm_sb[:, c * 1024 + t * 128:
                                                  c * 1024 + (t + 1) * 128],
                                         wpan[:, c * 512:(c + 1) * 512],
                                         start=(c == 0), stop=(c == 15),
                                         waits=[("ve_c", max(cv, prev_cv)),
                                                ("act_c", p.at("act_c"))] if c == 0 else (),
                                         inc="te_c", tag=f"A{wtag}m{grp}_{t}_{c}")
                            te_q = p.at("te_c")
                            if dest_kind[0] == "vtm":
                                cvv = v_op(p, lambda E: E.tensor_copy(qtm_sb[:],
                                                                      qps[:, 0:512]),
                                           waits=[("te_c", te_q),
                                                  ("stA", p.counts.get("stA", 0))],
                                           tag=f"A{wtag}ev{grp}_{t}")
                                prev_cv = cvv
                                p.dma("sp", dest_kind[1][t * 128:(t + 1) * 128, :],
                                      qtm_sb[:], waits=[("ve_c", cvv)], stream="stA",
                                      tag=f"A{wtag}o{t}")
                                continue
                            a1 = a_op(p, lambda E: E.activation(sqq[:], qps[:, 0:512],
                                                                AF.Square),
                                      waits=[("te_c", te_q)], tag=f"A{wtag}s{grp}_{t}")
                            r1 = v_op(p, lambda E: E.tensor_reduce(
                                ssqq[:], sqq[:].rearrange("p (h d) -> p h d", h=4),
                                AX.X, ALU.add), waits=[("act_c", a1)],
                                tag=f"A{wtag}r{grp}_{t}")
                            a2 = a_op(p, lambda E: E.activation(rstd4[:], ssqq[:],
                                                                AF.Ln, bias=eps6[:],
                                                                scale=1.0 / 128),
                                      waits=[("ve_c", max(r1, const_done))],
                                      tag=f"A{wtag}n{grp}_{t}")
                            a3 = a_op(p, lambda E: E.activation(rstd4[:], rstd4[:],
                                                                AF.Exp, bias=0.0,
                                                                scale=-0.5),
                                      tag=f"A{wtag}x{grp}_{t}")
                            cvq = v_op(p, lambda E: E.tensor_tensor(
                                qtm_sb[:].rearrange("p (h d) -> p h d", h=4),
                                qps[:, 0:512].rearrange("p (h d) -> p h d", h=4),
                                bcast_ap(rstd4, 128, 4, 128), ALU.mult),
                                waits=[("act_c", a3)], tag=f"A{wtag}q{grp}_{t}")
                            prev_cv = cvq
                            for k in range(4):
                                tt = t_transpose(p, trbf[:],
                                                 qtm_sb[:, k * 128:(k + 1) * 128],
                                                 id_b[:],
                                                 waits=[("ve_c", p.at("ve_c"))],
                                                 inc="te_c", tag=f"A{wtag}t{grp}_{t}_{k}")
                                h = grp * 4 + k
                                if dest_kind[0] == "qh":
                                    v_op(p, lambda E, h=h, t=t: E.tensor_copy(
                                        qh_sb[:, h * 1024 + t * 128:
                                              h * 1024 + (t + 1) * 128], trbf[:]),
                                        waits=[("te_c", tt)], tag=f"Aqh{grp}_{t}_{k}")
                                else:
                                    ev = v_op(p, lambda E: E.tensor_copy(qev[:], trbf[:]),
                                              waits=[("te_c", tt),
                                                     ("stA", p.counts.get("stA", 0))],
                                              tag=f"A{wtag}e{grp}_{t}_{k}")
                                    p.dma("sp",
                                          dest_kind[1][h * 128:(h + 1) * 128,
                                                       t * 128:(t + 1) * 128], qev[:],
                                          waits=[("ve_c", ev)], stream="stA",
                                          tag=f"A{wtag}o{grp}_{t}_{k}")
                        teA[grp % 2] = p.at("te_c")

                projA(wq, 2048, 4, ("qh",), "q")
                projA(wk, 512, 1, ("dram", kh_c), "k")
                projA(wv, 512, 1, ("vtm", v_c), "v")
            for nm, (ci, co) in {"cc1a": (xn_c, xn_ag), "cc1b": (kh_c, kh_ag),
                                 "cc1c": (v_c, v_ag), "cc1d": (lg_c, lg_ag)}.items():
                g_op(p, lambda E, ci=ci, co=co: E.collective_compute(
                    "AllGather", ALU.bypass, replica_groups=[list(range(8))],
                    ins=[ci.opt()], outs=[co.opt()]),
                    waits=[("stA", p.at("stA"))] if nm == "cc1a" else (),
                    inc=nm, inc_k=1, tag="AG1" + nm)
                p.ops["po"].append((lambda E: None, [(nm, 1)], None, 1, "AG1w"))

            with p.scope():
                kbufB = p.sb("B_k", [128, 8192], BF16)
                vbufB = p.sb("B_v", [128, 64 * 128], BF16)
                esB2 = [p.sb("B_es0", [128, 1024], BF16),
                        p.sb("B_es1", [128, 1024], BF16)]
                daccB = p.sb("B_dacc", [128, 1024], F32)
                rcB = p.sb("B_rc", [1, 1024], F32)
                bcB = p.sb("B_bc", [128, 1024], F32)
                aoB = p.sb("B_ao", [128, 1024], BF16)
                lgpB2 = [p.ps("B_lgp0", [128, 1024], F32),
                         p.ps("B_lgp1", [128, 1024], F32)]
                avpB = p.ps("B_avp", [128, 1024], F32)
                bcpB = p.ps("B_bcp", [128, 1024], F32)
                for h in range(16):
                    g = h // 4
                    prev_sweep = p.at("te_c")
                    for r_ in range(8):
                        p.dma("sp", kbufB[:, r_ * 1024:(r_ + 1) * 1024],
                              kh_ag[512 * r_ + 128 * g: 512 * r_ + 128 * (g + 1), :],
                              waits=[("te_c", prev_sweep), ("cc1b", 1), ("cc1c", 1)] if r_ == 0 else (),
                              stream="Bldk", tag=f"Bk{h}_{r_}")
                    p.dma("sp", vbufB[:],
                          bass.AP(v_ag.tensor, g * 128,
                                  [[512, 128], [128 * 512, 64], [1, 128]]),
                          stream="Bldk", tag=f"Bv{h}")
                    ldk = p.at("Bldk")
                    last_exp = [p.at("act_c"), p.at("act_c")]
                    last_cons = [p.at("ve_c"), p.at("ve_c")]
                    last_av = [p.at("te_c"), p.at("te_c")]
                    first_ve = p.at("ve_c")
                    for kc in range(64):
                        par = kc % 2
                        w0 = [("Bldk", ldk), ("act_c", last_exp[par])]
                        if kc == 0:
                            w0.append(("ve_c", first_ve))
                        t_matmul(p, lgpB2[par][:, 0:512],
                                 kbufB[:, kc * 128:(kc + 1) * 128],
                                 qh_sb[:, h * 1024: h * 1024 + 512], True, True,
                                 waits=w0, inc="te_c", tag=f"Bl{h}_{kc}a")
                        t_matmul(p, lgpB2[par][:, 512:1024],
                                 kbufB[:, kc * 128:(kc + 1) * 128],
                                 qh_sb[:, h * 1024 + 512: h * 1024 + 1024], True, True,
                                 inc="te_c", tag=f"Bl{h}_{kc}b")
                        te_lg = p.at("te_c")
                        a_exp = a_op(p, lambda E, par=par: E.activation(
                            esB2[par][:], lgpB2[par][:], AF.Exp, bias=0.0, scale=SCALE),
                            waits=[("te_c", te_lg), ("te_c", last_av[par]),
                                   ("ve_c", last_cons[par])], tag=f"Be{h}_{kc}")
                        last_exp[par] = a_exp
                        if kc == 0:
                            dv = v_op(p, lambda E, par=par: E.tensor_copy(
                                daccB[:], esB2[par][:]),
                                waits=[("act_c", a_exp), ("te_c", p.at("te_c"))],
                                tag=f"Bq{h}_{kc}")
                        else:
                            dv = v_op(p, lambda E, par=par: E.tensor_add(
                                daccB[:], daccB[:], esB2[par][:]),
                                waits=[("act_c", a_exp)], tag=f"Bq{h}_{kc}")
                        last_cons[par] = dv
                        st0 = (kc == 0); sp0 = (kc == 63)
                        t_matmul(p, avpB[:, 0:512], vbufB[:, kc * 128:(kc + 1) * 128],
                                 esB2[par][:, 0:512], st0, sp0,
                                 waits=[("act_c", a_exp)], inc="te_c",
                                 tag=f"Ba{h}_{kc}a")
                        t_matmul(p, avpB[:, 512:1024], vbufB[:, kc * 128:(kc + 1) * 128],
                                 esB2[par][:, 512:1024], st0, sp0, inc="te_c",
                                 tag=f"Ba{h}_{kc}b")
                        last_av[par] = p.at("te_c")
                    dacc_done = p.at("ve_c")
                    t_matmul(p, bcpB[0:1, 0:512], onescol[:], daccB[:, 0:512],
                             True, True, waits=[("ve_c", dacc_done)], inc="te_c",
                             tag=f"Bn{h}a")
                    dn_mm = t_matmul(p, bcpB[0:1, 512:1024], onescol[:],
                                     daccB[:, 512:1024], True, True, inc="te_c",
                                     tag=f"Bn{h}b")
                    rce = v_op(p, lambda E: E.tensor_copy(rcB[:], bcpB[0:1, :]),
                               waits=[("te_c", dn_mm)], tag=f"Bre{h}")
                    rc = v_op(p, lambda E: E.reciprocal(rcB[:], rcB[:]),
                              waits=[("ve_c", rce)], tag=f"Br{h}")
                    t_matmul(p, bcpB[:, 0:512], ones_row[:], rcB[0:1, 0:512], True, True,
                             waits=[("ve_c", rc)], inc="te_c", tag=f"Bb{h}a")
                    bc_mm = t_matmul(p, bcpB[:, 512:1024], ones_row[:],
                                     rcB[0:1, 512:1024], True, True, inc="te_c",
                                     tag=f"Bb{h}b")
                    bce = v_op(p, lambda E: E.tensor_copy(bcB[:], bcpB[:]),
                               waits=[("te_c", bc_mm)], tag=f"Bc{h}")
                    ao = v_op(p, lambda E: E.scalar_tensor_tensor(
                        aoB[:], avpB[:], 1.0, bcB[:], ALU.mult, ALU.mult),
                        waits=[("ve_c", bce), ("stB", p.counts.get("stB", 0))],
                        tag=f"Bo{h}")
                    p.dma("sp", attn_d[h * 128:(h + 1) * 128, :], aoB[:],
                          waits=[("ve_c", ao)], stream="stB", tag=f"Bso{h}")
                attn_done_st = p.at("stB")

                psB = PanelStream("B_ps")
                wpanB2 = [p.sb("B_wpan0", [128, 16 * 512], BF16),
                          p.sb("B_wpan1", [128, 16 * 512], BF16)]
                atc = [p.sb(f"B_atc{i}", [128, 2048], BF16) for i in range(2)]
                teB = {}
                for grp in range(4):
                    wpanB = wpanB2[grp % 2]
                    cv = psB.load("Bw" + str(grp),
                                  [(wsub3(wo, 2048, 4 * s, 4, grp * 512, 512), 2048,
                                    wpanB[:, s * 4 * 512:(s + 1) * 4 * 512])
                                   for s in range(4)],
                                  te_guard=teB.get(grp % 2, 0))
                    prev_ev = 0
                    for t in range(8):
                        par = t % 2
                        lda = p.dma("sp", atc[par][:],
                                    bass.AP(attn_d.tensor, t * 128,
                                            [[1024, 128], [128 * 1024, 16], [1, 128]]),
                                    waits=[("stB", attn_done_st),
                                           ("te_c", p.at("te_c"))],
                                    stream="Blda", tag=f"Bla{grp}_{t}")
                        for c in range(16):
                            t_matmul(p, bcpB[:, 0:512],
                                     atc[par][:, c * 128:(c + 1) * 128],
                                     wpanB[:, c * 512:(c + 1) * 512],
                                     start=(c == 0), stop=(c == 15),
                                     waits=[("ve_c", max(cv, prev_ev)),
                                            ("Blda", lda)] if c == 0 else (),
                                     inc="te_c", tag=f"Bm{grp}_{t}_{c}")
                        te_w = p.at("te_c")
                        ev = v_op(p, lambda E, t=t, grp=grp: E.tensor_copy(
                            s_tm_sb[:, t * 2048 + grp * 512: t * 2048 + (grp + 1) * 512],
                            bcpB[:, 0:512]), waits=[("te_c", te_w)], tag=f"Bv{grp}_{t}")
                        prev_ev = ev
                    teB[grp % 2] = p.at("te_c")
        s_done = p.at("ve_c")

        e0_sb = p.sb("e0_sb", [128, 1], F32)
        p.dma("sp", e0_sb[:], e0vec, stream="ld_e0", tag="e0")
        zer = p.sb("zer", [128, 128], U32)
        ze = v_op(p, lambda E: E.memset(zer[:], 0), tag="zer")
        p.dma("sp", listd[0:128, :], zer[:], waits=[("ve_c", ze)], stream="lz", tag="lz0")
        p.dma("sp", listd[128:144, :], zer[0:16, :], waits=[("ve_c", ze)], stream="lz",
              tag="lz1")
        lz_done = p.at("lz")

        with p.scope():
            lgrow = p.sb("C_lgrow", [16, 128], F32)
            trp_lg = p.ps("C_trplg", [128, 16], F32)

            def lg_filler(p2, lgt):
                last = 0
                for j in range(64):
                    r_, m_ = _OWNER[j]
                    l1 = p.dma("sp", lgrow[:],
                               lg_ag[16 * r_:16 * (r_ + 1), 128 * m_:128 * (m_ + 1)],
                               waits=[("cc1d", 1), ("ve_c", last)],
                               stream="Clgl", tag=f"Clg{j}")
                    tt = t_transpose(p, trp_lg[:], lgrow[:], id_f[0:16, 0:16],
                                     waits=[("Clgl", l1), ("ve_c", p.at("ve_c"))],
                                     inc="te_c", tag=f"Clt{j}")
                    last = v_op(p, lambda E, j=j: E.tensor_copy(
                        lgt[:, j * 16:(j + 1) * 16], trp_lg[:]),
                        waits=[("te_c", tt)], tag=f"Cle{j}")
                return last

            r = emit_routing(p, None, own_tiles=list(range(64)), lgt_filler=lg_filler)

            slot = p.sb("C_slot", [128, 1], F32)
            kept = p.sb("C_kept", [128, 1], F32)
            erel = p.sb("C_erel", [128, 1], F32)
            cge = p.sb("C_cge", [128, 1], F32)
            dstf = p.sb("C_dstf", [128, 1], F32)
            dstu = p.sb("C_dstu", [128, 1], U32)
            tknu = p.sb("C_tknu", [128, 1], U32)
            tknf = p.sb("C_tknf", [128, 1], F32)
            for j in range(64):
                for k in (1, 2):
                    pos = (r["pos1"] if k == 1 else r["pos2"])[:, j:j + 1]
                    ear = (r["earr1"] if k == 1 else r["earr2"])[:, j:j + 1]
                    v_op(p, lambda E, pos=pos: E.tensor_scalar_min(slot[:], pos, 1023.0),
                         waits=[("ve_c", r["done"])] if (j == 0 and k == 1) else (),
                         tag=f"Cs{j}_{k}")
                    v_op(p, lambda E, pos=pos: E.tensor_scalar(kept[:], pos, 1023.5,
                                                               None, ALU.is_le),
                         tag=f"Ck{j}_{k}")
                    v_op(p, lambda E, ear=ear: E.tensor_sub(erel[:], ear, e0_sb[:]),
                         waits=[("ld_e0", p.at("ld_e0"))] if (j == 0 and k == 1) else (),
                         tag=f"Ce{j}_{k}")
                    v_op(p, lambda E: E.tensor_scalar_add(erel[:], erel[:], 16.0),
                         tag=f"Ca{j}_{k}")
                    v_op(p, lambda E: E.tensor_scalar(cge[:], erel[:], 15.5, None,
                                                      ALU.is_gt), tag=f"Cg{j}_{k}")
                    v_op(p, lambda E: E.scalar_tensor_tensor(erel[:], cge[:], -16.0,
                                                             erel[:], ALU.mult,
                                                             ALU.add), tag=f"Cm{j}_{k}")
                    v_op(p, lambda E: E.scalar_tensor_tensor(dstf[:], erel[:], 1024.0,
                                                             slot[:], ALU.mult,
                                                             ALU.add), tag=f"C0{j}_{k}")
                    tb_ = 16384.0 + (k - 1) * 128
                    v_op(p, lambda E, tb_=tb_: E.tensor_scalar(cge[:], iota_p[:], 1.0,
                                                               tb_, ALU.mult, ALU.add),
                         tag=f"Ct{j}_{k}")
                    v_op(p, lambda E: E.tensor_sub(dstf[:], dstf[:], cge[:]),
                         tag=f"C1{j}_{k}")
                    v_op(p, lambda E: E.scalar_tensor_tensor(dstf[:], dstf[:], 1.0,
                                                             kept[:], ALU.mult,
                                                             ALU.mult), tag=f"C2{j}_{k}")
                    v_op(p, lambda E: E.tensor_add(dstf[:], dstf[:], cge[:]),
                         tag=f"C3{j}_{k}")
                    v_op(p, lambda E: E.tensor_copy(dstu[:], dstf[:]), tag=f"C4{j}_{k}")
                    base = float(agbase(j))
                    v_op(p, lambda E, base=base: E.tensor_scalar(tknf[:], iota_p[:],
                                                                 1.0, base, ALU.mult,
                                                                 ALU.add),
                         tag=f"C5{j}_{k}")
                    cvt = v_op(p, lambda E: E.tensor_copy(tknu[:], tknf[:]),
                               tag=f"C6{j}_{k}")
                    g_op(p, lambda E: E.indirect_dma_start(
                        out=bass.AP(listd.tensor, 0, [[1, 144 * 128], [1, 1]]),
                        out_offset=IndirectOffsetOnAxis(dstu[:], 0),
                        in_=tknu[:], in_offset=None),
                        waits=[("ve_c", cvt), ("lz", lz_done)], inc="sc_dma",
                        inc_k=16, tag=f"C7{j}_{k}")
                    p.ops["ve"].append((lambda E: None, [("sc_dma", p.at("sc_dma"))],
                                        None, 1, "scsync"))
            scat_done = p.at("sc_dma")

            slq = p.sb("C_slq", [128, 64], F32)
            tmpq = p.sb("C_tmpq", [128, 64], F32)
            sl64 = p.sb("C_sl64", [64, 128], F32)
            trq = p.ps("C_trq", [64, 128], F32)
            for k in (1, 2):
                posk = r["pos1"] if k == 1 else r["pos2"]
                eark = r["earr1"] if k == 1 else r["earr2"]
                v_op(p, lambda E, posk=posk: E.tensor_scalar_min(tmpq[:], posk[:],
                                                                 1023.0),
                     waits=[("ve_c", r["done"])], tag=f"Cq{k}")
                v_op(p, lambda E, eark=eark: E.scalar_tensor_tensor(
                    slq[:], eark[:], 1024.0, tmpq[:], ALU.mult, ALU.add),
                    tag=f"Cw{k}")
                tt = t_transpose(p, trq[:], slq[:, 0:64], id_f[:],
                                 waits=[("ve_c", p.at("ve_c"))], inc="te_c",
                                 tag=f"Cx{k}")
                e1 = v_op(p, lambda E: E.tensor_copy(sl64[:], trq[0:64, :]),
                          waits=[("te_c", tt), ("rod", p.counts.get("rod", 0))],
                          tag=f"Cy{k}")
                p.dma("sp", bass.AP(rodram.tensor, (k - 1) * 64 * 128,
                                    [[128, 64], [1, 128]]),
                      sl64[:], waits=[("ve_c", e1)], stream="rod", tag=f"Cz{k}")
                wk_ = r["warr1"] if k == 1 else r["warr2"]
                tw = t_transpose(p, trq[:], wk_[:, 0:64], id_f[:],
                                 waits=[("ve_c", p.at("ve_c"))], inc="te_c",
                                 tag=f"Cu{k}")
                e2 = v_op(p, lambda E: E.tensor_copy(sl64[:], trq[0:64, :]),
                          waits=[("te_c", tw), ("rod", p.at("rod"))], tag=f"Cv{k}")
                p.dma("sp", bass.AP(rodram.tensor, (1 + k) * 64 * 128,
                                    [[128, 64], [1, 128]]),
                      sl64[:], waits=[("ve_c", e2)], stream="rod", tag=f"Cr{k}")
            rod_done = p.at("rod")
        bar_ve = p.at("ve_c"); bar_te = p.at("te_c"); bar_act = p.at("act_c")
        bar_po = p.at("po_c")

        with p.scope():
            lrow = p.sb("C_lrow", [128, 1], U32)
            xg_tm = p.sb("C_xgtm", [128, 2048], BF16)
            xg_fm = p.sb("C_xgfm", [128, 16 * 1024], BF16)
            trbfC = p.ps("C_trbf", [128, 128], BF16)
            gpsC = p.ps("C_gps", [128, 512], F32)
            upsC = p.ps("C_ups", [128, 512], F32)
            dpsC = p.ps("C_dps", [128, 512], F32)
            g_sbC = p.sb("C_g", [128, 1024], BF16)
            u_sbC = p.sb("C_u", [128, 1024], BF16)
            gu_fm = p.sb("C_gu", [128, 11 * 1024], BF16)
            psC = PanelStream("C_ps")
            wpang2 = [p.sb("C_wpg0", [128, 16 * 128], BF16),
                      p.sb("C_wpg1", [128, 16 * 128], BF16)]
            wpanu2 = [p.sb("C_wpu0", [128, 16 * 128], BF16),
                      p.sb("C_wpu1", [128, 16 * 128], BF16)]
            wpand = p.sb("C_wpd", [128, 11 * 512], BF16)
            teCg = {}
            ev_sbC = p.sb("C_ev", [128, 512], BF16)

            for le in range(2):
                for st in range(8):
                    row = le * 8 + st
                    ldl = p.dma("sp", lrow[:],
                                listd[row:row + 1, :].rearrange("a b -> b a"),
                                waits=[("sc_dma", scat_done),
                                       ("gt_dma", p.counts.get("gt_dma", 0)),
                                       ("ve_c", bar_ve), ("act_c", bar_act),
                                       ("te_c", bar_te), ("po_c", bar_po)],
                                stream="ldl", tag=f"CL{le}_{st}")
                    g_op(p, lambda E: E.indirect_dma_start(
                        out=xg_tm[:], out_offset=None, in_=xn_ag,
                        in_offset=IndirectOffsetOnAxis(lrow[:], 0)),
                        waits=[("ldl", ldl), ("te_c", p.at("te_c")), ("cc1a", 1)],
                        inc="gt_dma", inc_k=16, tag=f"CG{le}_{st}")
                    gt = p.at("gt_dma")
                    for c in range(16):
                        tt = t_transpose(p, trbfC[:], xg_tm[:, c * 128:(c + 1) * 128],
                                         id_b[:],
                                         waits=[("gt_dma", gt), ("ve_c", p.at("ve_c"))],
                                         inc="te_c", tag=f"CT{le}_{st}_{c}")
                        v_op(p, lambda E, c=c, st=st: E.tensor_copy(
                            xg_fm[:, c * 1024 + st * 128: c * 1024 + (st + 1) * 128],
                            trbfC[:]), waits=[("te_c", tt)], tag=f"CE{le}_{st}_{c}")
                xg_done = p.at("ve_c")
                for i in range(11):
                    wpang = wpang2[i % 2]
                    wpanu = wpanu2[i % 2]
                    base = le * 2048 * 1408 + i * 128
                    psC.load(f"CA{le}_{i}",
                             [(bass.AP(wg.tensor, base,
                                       [[1408, 128], [128 * 1408, 16], [1, 128]]),
                               2048, wpang[:, 0:2048])],
                             te_guard=teCg.get(i % 2, 0))
                    cvu = psC.load(f"CB{le}_{i}",
                                   [(bass.AP(wu.tensor, base,
                                             [[1408, 128], [128 * 1408, 16], [1, 128]]),
                                     2048, wpanu[:, 0:2048])],
                                   te_guard=teCg.get(i % 2, 0))
                    for sg_ in range(2):
                        for c in range(16):
                            t_matmul(p, gpsC[:, 0:512], wpang[:, c * 128:(c + 1) * 128],
                                     xg_fm[:, c * 1024 + sg_ * 512:
                                           c * 1024 + (sg_ + 1) * 512],
                                     start=(c == 0), stop=(c == 15),
                                     waits=[("ve_c", max(cvu, xg_done, p.at("ve_c")))] if c == 0 else (),
                                     inc="te_c", tag=f"CM{le}_{i}_{sg_}_{c}")
                        te_g = p.at("te_c")
                        ag_ = a_op(p, lambda E, sg_=sg_: E.activation(
                            g_sbC[:, sg_ * 512:(sg_ + 1) * 512], gpsC[:, 0:512],
                            AF.Silu), waits=[("te_c", te_g)], tag=f"CS{le}_{i}_{sg_}")
                        for c in range(16):
                            t_matmul(p, upsC[:, 0:512], wpanu[:, c * 128:(c + 1) * 128],
                                     xg_fm[:, c * 1024 + sg_ * 512:
                                           c * 1024 + (sg_ + 1) * 512],
                                     start=(c == 0), stop=(c == 15),
                                     inc="te_c", tag=f"CU{le}_{i}_{sg_}_{c}")
                        te_u = p.at("te_c")
                        v_op(p, lambda E, sg_=sg_: E.tensor_copy(
                            u_sbC[:, sg_ * 512:(sg_ + 1) * 512], upsC[:, 0:512]),
                            waits=[("te_c", te_u)], tag=f"CV{le}_{i}_{sg_}")
                        v_op(p, lambda E, i=i, sg_=sg_: E.tensor_mul(
                            gu_fm[:, i * 1024 + sg_ * 512: i * 1024 + (sg_ + 1) * 512],
                            g_sbC[:, sg_ * 512:(sg_ + 1) * 512],
                            u_sbC[:, sg_ * 512:(sg_ + 1) * 512]),
                            waits=[("act_c", ag_)], tag=f"CW{le}_{i}_{sg_}")
                    teCg[i % 2] = p.at("te_c")
                gu_done = p.at("ve_c")
                for og in range(4):
                    cvd = psC.load(
                        f"CD{le}_{og}",
                        [(bass.AP(wd.tensor,
                                  le * 1408 * 2048 + (i0 * 128) * 2048 + og * 512,
                                  [[2048, 128], [128 * 2048, ni], [1, 512]]),
                          ni * 512, wpand[:, i0 * 512:(i0 + ni) * 512])
                         for (i0, ni) in ((0, 4), (4, 4), (8, 3))])
                    for st in range(8):
                        for i in range(11):
                            t_matmul(p, dpsC[:, 0:512],
                                     gu_fm[:, i * 1024 + st * 128:
                                           i * 1024 + (st + 1) * 128],
                                     wpand[:, i * 512:(i + 1) * 512],
                                     start=(i == 0), stop=(i == 10),
                                     waits=[("ve_c", max(cvd, gu_done, p.at("ve_c")))] if i == 0 else (),
                                     inc="te_c", tag=f"CP{le}_{og}_{st}_{i}")
                        te_d = p.at("te_c")
                        ev = v_op(p, lambda E: E.tensor_copy(ev_sbC[:], dpsC[:, 0:512]),
                                  waits=[("te_c", te_d),
                                         ("stC", p.counts.get("stC", 0))],
                                  tag=f"CQ{le}_{og}_{st}")
                        p.dma("sp", out_c[le * 1024 + st * 128:
                                          le * 1024 + (st + 1) * 128,
                                          og * 512:(og + 1) * 512], ev_sbC[:],
                              waits=[("ve_c", ev)], stream="stC",
                              tag=f"CO{le}_{og}_{st}")

        g_op(p, lambda E: E.collective_compute(
            "AllGather", ALU.bypass, replica_groups=[list(range(8))],
            ins=[out_c.opt()], outs=[outf_ag.opt()]),
            waits=[("stC", p.at("stC"))], inc="cc2", inc_k=1, tag="AG2")
        p.ops["po"].append((lambda E: None, [("cc2", 1)], None, 1, "AG2w"))

        with p.scope():
            myi = p.sb("D_myi", [128, 1], U32)
            sidf = p.sb("D_sidf", [128, 2], F32)
            sidu = p.sb("D_sidu", [128, 1], U32)
            wvD = p.sb("D_wv", [128, 2], F32)
            gbuf = p.sb("D_gbuf", [128, 2048], BF16)
            y_all = p.sb("D_yall", [128, 8 * 2048], BF16)
            yacc = p.sb("D_yacc", [128, 2048], F32)
            for t in range(8):
                lm = p.dma("sp", myi[:], myidx[t * 128:(t + 1) * 128, :],
                           waits=[("gt_dma", p.at("gt_dma")), ("ve_c", p.at("ve_c")),
                                  ("rod", rod_done)],
                           stream="ldm", tag=f"Dm{t}")
                for k in (0, 1):
                    g_op(p, lambda E, k=k: E.indirect_dma_start(
                        out=sidf[:, k:k + 1], out_offset=None,
                        in_=bass.AP(rodram.tensor, 0, [[1, 32768], [1, 1]]),
                        in_offset=IndirectOffsetOnAxis(myi[:], 0),
                        element_offset=k * 64 * 128),
                        waits=[("ldm", p.at("ldm")), ("ve_c", p.at("ve_c"))],
                        inc="gt_dma", inc_k=16, tag=f"DS{t}_{k}")
                    g_op(p, lambda E, k=k: E.indirect_dma_start(
                        out=wvD[:, k:k + 1], out_offset=None,
                        in_=bass.AP(rodram.tensor, 0, [[1, 32768], [1, 1]]),
                        in_offset=IndirectOffsetOnAxis(myi[:], 0),
                        element_offset=(2 + k) * 64 * 128),
                        inc="gt_dma", inc_k=16, tag=f"DW{t}_{k}")
                gt0 = p.at("gt_dma")
                for k in (0, 1):
                    cu = v_op(p, lambda E, k=k: E.tensor_copy(sidu[:], sidf[:, k:k + 1]),
                              waits=[("gt_dma", gt0)], tag=f"DC{t}_{k}")
                    g_op(p, lambda E: E.indirect_dma_start(
                        out=gbuf[:], out_offset=None, in_=outf_ag,
                        in_offset=IndirectOffsetOnAxis(sidu[:], 0)),
                        waits=[("ve_c", cu), ("cc2", 1), ("ve_c", p.at("ve_c"))],
                        inc="gt_dma", inc_k=16, tag=f"DG{t}_{k}")
                    gtk = p.at("gt_dma")
                    if k == 0:
                        v_op(p, lambda E: E.tensor_scalar(yacc[:], gbuf[:], wvD[:, 0:1],
                                                          None, ALU.mult),
                             waits=[("gt_dma", gtk)], tag=f"D0{t}")
                    else:
                        v_op(p, lambda E: E.scalar_tensor_tensor(
                            yacc[:], gbuf[:], wvD[:, 1:2], yacc[:], ALU.mult, ALU.add),
                            waits=[("gt_dma", gtk)], tag=f"D1{t}")
                        v_op(p, lambda E, t=t: E.tensor_copy(
                            y_all[:, t * 2048:(t + 1) * 2048], yacc[:]), tag=f"D2{t}")
            y_done = p.at("ve_c")

            guD = p.sb("D_gu", [128, 22 * 512], BF16)
            psD = PanelStream("D_ps")
            wpgD2 = [p.sb("D_wpg0", [128, 16 * 128], BF16),
                     p.sb("D_wpg1", [128, 16 * 128], BF16)]
            wpuD2 = [p.sb("D_wpu0", [128, 16 * 128], BF16),
                     p.sb("D_wpu1", [128, 16 * 128], BF16)]
            wpdD = p.sb("D_wpd", [128, 22 * 512], BF16)
            teDg = {}
            g_sbD = p.sb("D_g", [128, 512], BF16)
            u_sbD = p.sb("D_u", [128, 512], BF16)
            ysD = p.sb("D_ys", [128, 512], F32)
            silD = p.sb("D_sil", [128, 512], BF16)
            xslD = p.sb("D_xsl", [128, 512], F32)
            o_sbD = p.sb("D_o", [128, 512], F32)
            gpsD = p.ps("D_gps", [128, 512], F32)
            upsD = p.ps("D_ups", [128, 512], F32)
            dpsD = p.ps("D_dps", [128, 512], F32)
            for tg in range(2):
                for i in range(22):
                    wpgD = wpgD2[i % 2]
                    wpuD = wpuD2[i % 2]
                    psD.load(f"DA{tg}_{i}",
                             [(bass.AP(wsg.tensor, i * 128,
                                       [[2816, 128], [128 * 2816, 16], [1, 128]]),
                               2048, wpgD[:, 0:2048])], te_guard=teDg.get(i % 2, 0))
                    psD.load(f"DB{tg}_{i}",
                             [(bass.AP(wsu.tensor, i * 128,
                                       [[2816, 128], [128 * 2816, 16], [1, 128]]),
                               2048, wpuD[:, 0:2048])], te_guard=teDg.get(i % 2, 0))
                    for c in range(16):
                        t_matmul(p, gpsD[:, 0:512], wpgD[:, c * 128:(c + 1) * 128],
                                 xn_fm_sb[:, c * 1024 + tg * 512:
                                          c * 1024 + (tg + 1) * 512],
                                 start=(c == 0), stop=(c == 15),
                                 waits=[("ve_c", p.at("ve_c"))] if c == 0 else (),
                                 inc="te_c", tag=f"DM{tg}_{i}_{c}")
                    te_g = p.at("te_c")
                    agD = a_op(p, lambda E: E.activation(g_sbD[:], gpsD[:, 0:512],
                                                         AF.Silu),
                               waits=[("te_c", te_g)], tag=f"DI{tg}_{i}")
                    for c in range(16):
                        t_matmul(p, upsD[:, 0:512], wpuD[:, c * 128:(c + 1) * 128],
                                 xn_fm_sb[:, c * 1024 + tg * 512:
                                          c * 1024 + (tg + 1) * 512],
                                 start=(c == 0), stop=(c == 15),
                                 inc="te_c", tag=f"DU{tg}_{i}_{c}")
                    te_u = p.at("te_c")
                    v_op(p, lambda E: E.tensor_copy(u_sbD[:], upsD[:, 0:512]),
                         waits=[("te_c", te_u)], tag=f"DV{tg}_{i}")
                    v_op(p, lambda E, i=i: E.tensor_mul(guD[:, i * 512:(i + 1) * 512],
                                                        g_sbD[:], u_sbD[:]),
                         waits=[("act_c", agD)], tag=f"DX{tg}_{i}")
                    teDg[i % 2] = p.at("te_c")
                gu_doneD = p.at("ve_c")
                for og in range(4):
                    cvd = psD.load(
                        f"DD{tg}_{og}",
                        [(bass.AP(wsd.tensor, (i0 * 128) * 2048 + og * 512,
                                  [[2048, 128], [128 * 2048, ni], [1, 512]]),
                          ni * 512, wpdD[:, i0 * 512:(i0 + ni) * 512])
                         for (i0, ni) in ((0, 4), (4, 4), (8, 4), (12, 4), (16, 4),
                                          (20, 2))])
                    for tt_ in range(4):
                        t = tg * 4 + tt_
                        for i in range(22):
                            t_matmul(p, dpsD[:, 0:512],
                                     guD[:, i * 512 + tt_ * 128:
                                         i * 512 + (tt_ + 1) * 128],
                                     wpdD[:, i * 512:(i + 1) * 512],
                                     start=(i == 0), stop=(i == 21),
                                     waits=[("ve_c", max(cvd, gu_doneD, p.at("ve_c")))] if i == 0 else (),
                                     inc="te_c", tag=f"DP{tg}_{og}_{tt_}_{i}")
                        te_d = p.at("te_c")
                        lx = p.dma("sp", xslD[:], x_in[t * 128:(t + 1) * 128,
                                                       og * 512:(og + 1) * 512],
                                   waits=[("ve_c", p.at("ve_c"))], stream="ldsD",
                                   tag=f"DL{t}_{og}")
                        sv = v_op(p, lambda E, t=t, og=og: E.scalar_tensor_tensor(
                            ysD[:], dpsD[:, 0:512], 1.0,
                            y_all[:, t * 2048 + og * 512: t * 2048 + (og + 1) * 512],
                            ALU.mult, ALU.add),
                            waits=[("te_c", te_d), ("ve_c", y_done)], tag=f"DY{t}_{og}")
                        asil = a_op(p, lambda E: E.activation(silD[:], ysD[:], AF.Silu),
                                    waits=[("ve_c", sv)], tag=f"DF{t}_{og}")
                        v_op(p, lambda E, t=t, og=og: E.tensor_mul(
                            ysD[:], silD[:],
                            s_tm_sb[:, t * 2048 + og * 512: t * 2048 + (og + 1) * 512]),
                            waits=[("act_c", asil), ("ve_c", s_done)], tag=f"DZ{t}_{og}")
                        ov = v_op(p, lambda E: E.tensor_add(o_sbD[:], ysD[:], xslD[:]),
                                  waits=[("stD", p.counts.get("stD", 0)),
                                         ("ldsD", lx)], tag=f"DO{t}_{og}")
                        p.dma("sp", x1_d[t * 128:(t + 1) * 128,
                                         og * 512:(og + 1) * 512], o_sbD[:],
                              waits=[("ve_c", ov)], stream="stD", tag=f"DR{t}_{og}")
    x1_done = p.at("stD")

    with p.scope():
        xn1_fm_sb = p.sb("XN1", [128, 16 * 1024], BF16)
        s1_sb = p.sb("S1", [128, 16 * 1024], BF16)
        with p.scope():
            q1_sb = p.sb("Q1", [128, 16 * 1024], BF16)
            with p.scope():
                ln_tiles("E", x1_d, xn1_fm_sb, xn_tm_dram=None, logits=None,
                         x_ready=[("stD", x1_done)])
                xn1_done = p.at("ve_c")
                bq_sb = p.sb("E_bq", [128, 16], F32)
                bk_sb = p.sb("E_bk", [128, 4], F32)
                bv_sb = p.sb("E_bv", [128, 4], F32)
                p.dma("sp", bq_sb[:], bqT, stream="ldEb", tag="Ebq")
                p.dma("sp", bk_sb[:], bkT, stream="ldEb", tag="Ebk")
                p.dma("sp", bv_sb[:], bvT, stream="ldEb", tag="Ebv")
                ld_b = p.at("ldEb")
                psE = PanelStream("E_ps")
                wpanE2 = [p.sb("E_wpan0", [128, 16 * 128], BF16),
                          p.sb("E_wpan1", [128, 16 * 128], BF16)]
                teE = {}
                qeE = p.sb("E_qe", [128, 1024], BF16)
                vtmE = p.sb("E_vtm", [128, 128], BF16)
                qpsE = p.ps("E_qps", [128, 512], F32)
                trbfE = p.ps("E_trbf", [128, 128], BF16)

                def projE(wdram, bias_sb, nheads, dest, wtag, to_tm=False):
                    for h in range(nheads):
                        wpanE = wpanE2[h % 2]
                        cv = psE.load(f"E{wtag}{h}",
                                      [(bass.AP(wdram.tensor, h * 2048 * 128,
                                                [[128, 128], [128 * 128, 16], [1, 128]]),
                                        2048, wpanE[:, 0:2048])],
                                      te_guard=teE.get(h % 2, 0))
                        for grp in range(2):
                            for c in range(16):
                                t_matmul(p, qpsE[:, 0:512],
                                         wpanE[:, c * 128:(c + 1) * 128],
                                         xn1_fm_sb[:, c * 1024 + grp * 512:
                                                   c * 1024 + (grp + 1) * 512],
                                         start=(c == 0), stop=(c == 15),
                                         waits=[("ve_c", max(cv, xn1_done)),
                                                ("act_c", p.at("act_c"))] if c == 0 else (),
                                         inc="te_c", tag=f"E{wtag}m{h}_{grp}_{c}")
                            te_q = p.at("te_c")
                            a_op(p, lambda E, h=h, grp=grp: E.activation(
                                qeE[:, grp * 512:(grp + 1) * 512], qpsE[:, 0:512],
                                AF.Identity, bias=bias_sb[:, h:h + 1], scale=1.0),
                                waits=[("te_c", te_q), ("ldEb", ld_b),
                                       ("stE", p.counts.get("stE", 0)),
                                       ("ve_c", p.at("ve_c"))],
                                tag=f"E{wtag}e{h}_{grp}")
                        ae_all = p.at("act_c")
                        if not to_tm:
                            if dest is None:
                                v_op(p, lambda E, h=h: E.tensor_copy(
                                    q1_sb[:, h * 1024:(h + 1) * 1024], qeE[:]),
                                    waits=[("act_c", ae_all)], tag=f"Eq{h}")
                            else:
                                p.dma("sp", dest[h * 128:(h + 1) * 128, :], qeE[:],
                                      waits=[("act_c", ae_all)], stream="stE",
                                      tag=f"E{wtag}o{h}")
                        else:
                            for t in range(8):
                                tt = t_transpose(p, trbfE[:],
                                                 qeE[:, t * 128:(t + 1) * 128], id_b[:],
                                                 waits=[("act_c", ae_all),
                                                        ("ve_c", p.at("ve_c"))],
                                                 inc="te_c", tag=f"E{wtag}t{h}_{t}")
                                ev = v_op(p, lambda E: E.tensor_copy(vtmE[:], trbfE[:]),
                                          waits=[("te_c", tt),
                                                 ("stE", p.counts.get("stE", 0))],
                                          tag=f"E{wtag}v{h}_{t}")
                                p.dma("sp", dest[t * 128:(t + 1) * 128,
                                                 h * 128:(h + 1) * 128], vtmE[:],
                                      waits=[("ve_c", ev)], stream="stE",
                                      tag=f"E{wtag}o{h}_{t}")
                        teE[h % 2] = p.at("te_c")

                projE(wq1, bq_sb, 16, None, "q")
                projE(wk1, bk_sb, 4, k1_c, "k")
                projE(wv1, bv_sb, 4, v1_c, "v", to_tm=True)

            for nm, (ci, co) in {"cc3a": (k1_c, k1_ag), "cc3b": (v1_c, v1_ag)}.items():
                g_op(p, lambda E, ci=ci, co=co: E.collective_compute(
                    "AllGather", ALU.bypass, replica_groups=[list(range(8))],
                    ins=[ci.opt()], outs=[co.opt()]),
                    waits=[("stE", p.at("stE"))] if nm == "cc3a" else (),
                    inc=nm, inc_k=1, tag="AG3" + nm)
                p.ops["po"].append((lambda E: None, [(nm, 1)], None, 1, "AG3w"))

            with p.scope():
                kbufF = p.sb("F_k", [128, 8192], BF16)
                vbufF = p.sb("F_v", [128, 64 * 128], BF16)
                mk_sb = p.sb("F_mk", [128, 64 * 128], BF16)
                esF2 = [p.sb("F_es0", [128, 1024], BF16),
                        p.sb("F_es1", [128, 1024], BF16)]
                daccF = p.sb("F_dacc", [128, 1024], F32)
                rcF = p.sb("F_rc", [1, 1024], F32)
                bcF = p.sb("F_bc", [128, 1024], F32)
                lgpF2 = [p.ps("F_lgp0", [128, 1024], F32),
                         p.ps("F_lgp1", [128, 1024], F32)]
                avpF = p.ps("F_avp", [128, 1024], F32)
                bcpF = p.ps("F_bcp", [128, 1024], F32)
                p.dma("sp", mk_sb[:],
                      bass.AP(masks.tensor, 0, [[128, 128], [16384, 64], [1, 128]]),
                      stream="ldF", tag="Fmk")
                ld_mk = p.at("ldF")
                for h in range(16):
                    g = h // 4
                    prev_sweep = p.at("te_c")
                    for r_ in range(8):
                        p.dma("sp", kbufF[:, r_ * 1024:(r_ + 1) * 1024],
                              k1_ag[512 * r_ + 128 * g: 512 * r_ + 128 * (g + 1), :],
                              waits=[("te_c", prev_sweep), ("cc3a", 1), ("cc3b", 1)] if r_ == 0 else (),
                              stream="ldkF", tag=f"Fk{h}_{r_}")
                    p.dma("sp", vbufF[:],
                          bass.AP(v1_ag.tensor, g * 128,
                                  [[512, 128], [128 * 512, 64], [1, 128]]),
                          stream="ldkF", tag=f"Fv{h}")
                    ldkF = p.at("ldkF")
                    last_exp = [p.at("act_c"), p.at("act_c")]
                    last_cons = [p.at("ve_c"), p.at("ve_c")]
                    last_av = [p.at("te_c"), p.at("te_c")]
                    first_ve = p.at("ve_c")
                    for kc in range(64):
                        par = kc % 2
                        ac = agcol(kc)
                        m0 = kc // 8
                        s = m0 * 128
                        pieces = []
                        if s < 512:
                            pieces.append((s, 512))
                        pieces.append((max(s, 512), 1024))
                        w0 = [("ldkF", ldkF), ("act_c", last_exp[par])]
                        if kc == 0:
                            w0.append(("ve_c", first_ve))
                        first = True
                        for (a, b) in pieces:
                            t_matmul(p, lgpF2[par][:, a:b],
                                     kbufF[:, ac * 128:(ac + 1) * 128],
                                     q1_sb[:, h * 1024 + a: h * 1024 + b], True, True,
                                     waits=w0 if first else (), inc="te_c",
                                     tag=f"Fl{h}_{kc}_{a}")
                            first = False
                        te_lg = p.at("te_c")
                        a_exp = a_op(p, lambda E, s=s, par=par: E.activation(
                            esF2[par][:, s:1024], lgpF2[par][:, s:1024], AF.Exp,
                            bias=0.0, scale=SCALE),
                            waits=[("te_c", te_lg), ("te_c", last_av[par]),
                                   ("ve_c", last_cons[par])], tag=f"Fe{h}_{kc}")
                        last_exp[par] = a_exp
                        mm = v_op(p, lambda E, s=s, kc=kc, par=par: E.tensor_mul(
                            esF2[par][:, s:s + 128], esF2[par][:, s:s + 128],
                            mk_sb[:, kc * 128:(kc + 1) * 128]),
                            waits=[("act_c", a_exp), ("ldF", ld_mk)], tag=f"Fm{h}_{kc}")
                        if kc == 0:
                            dv = v_op(p, lambda E, par=par: E.tensor_copy(
                                daccF[:], esF2[par][:]),
                                waits=[("ve_c", mm), ("te_c", p.at("te_c"))],
                                tag=f"Fq{h}_{kc}")
                        else:
                            dv = v_op(p, lambda E, par=par, s=s: E.tensor_add(
                                daccF[:, s:1024], daccF[:, s:1024],
                                esF2[par][:, s:1024]),
                                waits=[("ve_c", mm)], tag=f"Fq{h}_{kc}")
                        last_cons[par] = dv
                        for (a, b) in pieces:
                            st_ = (kc == 0)
                            sp_ = (kc == 31) if b == 512 else (kc == 63)
                            t_matmul(p, avpF[:, a:b], vbufF[:, ac * 128:(ac + 1) * 128],
                                     esF2[par][:, a:b], st_, sp_,
                                     waits=[("ve_c", mm)], inc="te_c",
                                     tag=f"Fa{h}_{kc}_{a}")
                        last_av[par] = p.at("te_c")
                    dacc_done = p.at("ve_c")
                    t_matmul(p, bcpF[0:1, 0:512], onescol[:], daccF[:, 0:512],
                             True, True, waits=[("ve_c", dacc_done)], inc="te_c",
                             tag=f"Fn{h}a")
                    dn_mm = t_matmul(p, bcpF[0:1, 512:1024], onescol[:],
                                     daccF[:, 512:1024], True, True, inc="te_c",
                                     tag=f"Fn{h}b")
                    rce = v_op(p, lambda E: E.tensor_copy(rcF[:], bcpF[0:1, :]),
                               waits=[("te_c", dn_mm)], tag=f"Fre{h}")
                    rcv = v_op(p, lambda E: E.reciprocal(rcF[:], rcF[:]),
                               waits=[("ve_c", rce)], tag=f"Fr{h}")
                    t_matmul(p, bcpF[:, 0:512], ones_row[:], rcF[0:1, 0:512], True,
                             True, waits=[("ve_c", rcv)], inc="te_c", tag=f"Fb{h}a")
                    bc_mm = t_matmul(p, bcpF[:, 512:1024], ones_row[:],
                                     rcF[0:1, 512:1024], True, True, inc="te_c",
                                     tag=f"Fb{h}b")
                    bce = v_op(p, lambda E: E.tensor_copy(bcF[:], bcpF[:]),
                               waits=[("te_c", bc_mm)], tag=f"Fc{h}")
                    v_op(p, lambda E, h=h: E.scalar_tensor_tensor(
                        s1_sb[:, h * 1024:(h + 1) * 1024], avpF[:], 1.0, bcF[:],
                        ALU.mult, ALU.mult), waits=[("ve_c", bce)], tag=f"Fo{h}")
        if os.environ.get("DBG_S1"):
            s1dbg = p.dtmp("s1dbg", [2048, 1024], BF16)
            _sv = p.at("ve_c")
            for _h in range(16):
                p.dma("sp", s1dbg[_h * 128:(_h + 1) * 128, :],
                      s1_sb[:, _h * 1024:(_h + 1) * 1024],
                      waits=[("ve_c", _sv)], stream="s1dbg", tag=f"s1d{_h}")
        s1_done = p.at("ve_c")

        with p.scope():
            junk = p.sb("G_junk", [128, 2048], F32)
            csum = p.sb("G_csum", [128, 1], F32)
            accG = p.sb("G_acc", [128, 2], F32)
            scal = p.sb("G_scal", [1, 8], F32)
            perm = p.sb("G_perm", [1, 6], F32)
            thrb = p.sb("G_thrb", [128, 6], F32)
            tot_ps = p.ps("G_tot", [1, 8], F32)
            bc_ps = p.ps("G_bcp", [128, 6], F32)
            v_op(p, lambda E: E.memset(accG[:], 0.0), tag="Ga0")
            acc0 = p.at("ve_c")
            av = 0
            absst = [p.sb("G_ab0", [128, 2048], F32), p.sb("G_ab1", [128, 2048], F32)]
            prev_abs = [0, 0]
            ii = 0
            for wi, (wdr, nr, nc_, col) in enumerate([(wfcT, 2048, 5120, 0),
                                                      (wprT, 5120, 2048, 1)]):
                for rr in range(nr // 128):
                    for cc0 in range(0, nc_, 2048):
                        cw = min(2048, nc_ - cc0)
                        par = ii % 2
                        ii += 1
                        ldw = p.dma("sp", absst[par][:, 0:cw],
                                    wdr[rr * 128:(rr + 1) * 128, cc0:cc0 + cw],
                                    waits=[("act_c", prev_abs[par])],
                                    stream=f"ldab{par}", tag=f"GA{wi}_{rr}_{cc0}")
                        aa = a_op(p, lambda E, par=par, cw=cw: E.activation(
                            junk[:, 0:cw], absst[par][:, 0:cw], AF.Abs,
                            accum_out=csum[:]),
                            waits=[(f"ldab{par}", ldw), ("ve_c", av)],
                            tag=f"GB{wi}_{rr}_{cc0}")
                        prev_abs[par] = aa
                        av = v_op(p, lambda E, col=col: E.tensor_add(
                            accG[:, col:col + 1], accG[:, col:col + 1], csum[:]),
                            waits=[("act_c", aa), ("ve_c", acc0)],
                            tag=f"GC{wi}_{rr}_{cc0}")
            acc_done = p.at("ve_c")
            tm = t_matmul(p, tot_ps[0:1, 0:2], onescol[:], accG[:, 0:2], True, True,
                          waits=[("ve_c", acc_done)], inc="te_c", tag="GT")
            v_op(p, lambda E: E.tensor_copy(scal[0:1, 0:2], tot_ps[0:1, 0:2]),
                 waits=[("te_c", tm)], tag="G0")
            v_op(p, lambda E: E.tensor_scalar(scal[0:1, 0:2], scal[0:1, 0:2],
                                              1.0 / (2048.0 * 5120.0), None, ALU.mult),
                 tag="G1")
            v_op(p, lambda E: E.tensor_scalar_max(scal[0:1, 0:2], scal[0:1, 0:2],
                                                  1e-5), tag="G2")
            v_op(p, lambda E: E.tensor_scalar(scal[0:1, 2:4], scal[0:1, 0:2], 0.7,
                                              None, ALU.mult), tag="G3")
            v_op(p, lambda E: E.tensor_scalar(scal[0:1, 4:6], scal[0:1, 2:4], -1.0,
                                              None, ALU.mult), tag="G4")
            v_op(p, lambda E: E.tensor_copy(perm[0:1, 0:1], scal[0:1, 2:3]), tag="G5")
            v_op(p, lambda E: E.tensor_copy(perm[0:1, 1:2], scal[0:1, 4:5]), tag="G6")
            v_op(p, lambda E: E.tensor_copy(perm[0:1, 2:3], scal[0:1, 0:1]), tag="G7")
            v_op(p, lambda E: E.tensor_copy(perm[0:1, 3:4], scal[0:1, 3:4]), tag="G8")
            v_op(p, lambda E: E.tensor_copy(perm[0:1, 4:5], scal[0:1, 5:6]), tag="G9")
            v_op(p, lambda E: E.tensor_copy(perm[0:1, 5:6], scal[0:1, 1:2]), tag="Gx")
            pm_done = p.at("ve_c")
            bm = t_matmul(p, bc_ps[:, 0:6], ones_row[:], perm[0:1, 0:6], True, True,
                          waits=[("ve_c", pm_done)], inc="te_c", tag="Gy")
            thr_done = v_op(p, lambda E: E.tensor_copy(thrb[:], bc_ps[:, 0:6]),
                            waits=[("te_c", bm)], tag="Gz")

            c1G = p.sb("G_c1", [128, 2048], F32)
            c2G = p.sb("G_c2", [128, 2048], F32)
            psG = PanelStream("G_ps")
            wpan4 = p.sb("G_wp4", [128, 16 * 512], BF16)
            wpanp = p.sb("G_wpp", [128, 40 * 128], BF16)
            h2 = p.sb("G_h2", [128, 40 * 512], BF16)
            r_sbG = p.sb("G_r", [128, 512], BF16)
            g_sbG = p.sb("G_g", [128, 512], BF16)
            x1sl = p.sb("G_x1sl", [128, 128], F32)
            obuf = p.sb("G_obuf", [128, 128], F32)
            fpsG = p.ps("G_fps", [128, 512], F32)
            ppsG = p.ps("G_pps", [128, 512], F32)
            trbfG = p.ps("G_trbf", [128, 128], BF16)

            def mk_ternz(thr_col):
                def conv(stage_ap, dst_ap, waits):
                    width = stage_ap.ap[-1][1] * (stage_ap.ap[-2][1]
                                                  if len(stage_ap.ap) > 2 else 1)
                    v_op(p, lambda E: E.tensor_scalar(c1G[:, 0:width], stage_ap,
                                                      thrb[:, thr_col:thr_col + 1],
                                                      None, ALU.is_gt),
                         waits=list(waits) + [("ve_c", thr_done)], tag="tz1")
                    v_op(p, lambda E: E.tensor_scalar(c2G[:, 0:width], stage_ap,
                                                      thrb[:, thr_col + 1:thr_col + 2],
                                                      None, ALU.is_lt), tag="tz2")
                    v_op(p, lambda E: E.tensor_sub(c1G[:, 0:width], c1G[:, 0:width],
                                                   c2G[:, 0:width]), tag="tz3")
                    return v_op(p, lambda E: E.tensor_scalar(
                        dst_ap, c1G[:, 0:width],
                        thrb[:, thr_col + 2:thr_col + 3], None, ALU.mult), tag="tz4")
                return conv
            ternz_fc = mk_ternz(0)
            ternz_pr = mk_ternz(3)

            for tg in range(2):
                for hg in range(10):
                    cvp = psG.load(
                        f"GF{tg}_{hg}",
                        [(wsub3(wfcT, 5120, 4 * s, 4, hg * 512, 512), 2048,
                          wpan4[:, s * 4 * 512:(s + 1) * 4 * 512])
                         for s in range(4)], conv=ternz_fc)
                    for hsub in range(4):
                        hc = hg * 4 + hsub
                        for c in range(16):
                            t_matmul(p, fpsG[:, 0:512],
                                     wpan4[:, c * 512 + hsub * 128:
                                           c * 512 + (hsub + 1) * 128],
                                     xn1_fm_sb[:, c * 1024 + tg * 512:
                                               c * 1024 + (tg + 1) * 512],
                                     start=(c == 0), stop=(c == 15),
                                     waits=[("ve_c", max(cvp, p.at("ve_c")))] if c == 0 else (),
                                     inc="te_c", tag=f"GM{tg}_{hc}_{c}")
                        te_f = p.at("te_c")
                        ar = a_op(p, lambda E: E.activation(r_sbG[:], fpsG[:, 0:512],
                                                            AF.Relu),
                                  waits=[("te_c", te_f)], tag=f"GR{tg}_{hc}")
                        v_op(p, lambda E, hc=hc: E.tensor_mul(
                            h2[:, hc * 512:(hc + 1) * 512], r_sbG[:], r_sbG[:]),
                            waits=[("act_c", ar)], tag=f"GQ{tg}_{hc}")
                h2_done = p.at("ve_c")
                for og in range(16):
                    cvp = psG.load(
                        f"GP{tg}_{og}",
                        [(bass.AP(wprT.tensor, (i0 * 128) * 2048 + og * 128,
                                  [[2048, 128], [128 * 2048, ni], [1, 128]]),
                          ni * 128, wpanp[:, i0 * 128:(i0 + ni) * 128])
                         for (i0, ni) in ((0, 16), (16, 16), (32, 8))],
                        conv=ternz_pr)
                    for i in range(40):
                        t_matmul(p, ppsG[:, 0:512], wpanp[:, i * 128:(i + 1) * 128],
                                 h2[:, i * 512:(i + 1) * 512],
                                 start=(i == 0), stop=(i == 39),
                                 waits=[("ve_c", max(cvp, h2_done))] if i == 0 else (),
                                 inc="te_c", tag=f"GN{tg}_{og}_{i}")
                    te_p = p.at("te_c")
                    asil = a_op(p, lambda E: E.activation(g_sbG[:], ppsG[:, 0:512],
                                                          AF.Silu),
                                waits=[("te_c", te_p)], tag=f"GS{tg}_{og}")
                    v_op(p, lambda E, og=og, tg=tg: E.tensor_mul(
                        g_sbG[:], g_sbG[:],
                        s1_sb[:, og * 1024 + tg * 512: og * 1024 + (tg + 1) * 512]),
                        waits=[("act_c", asil), ("ve_c", s1_done)], tag=f"GG{tg}_{og}")
                    for tsub in range(4):
                        t = tg * 4 + tsub
                        lx = p.dma("sp", x1sl[:],
                                   x1_d[t * 128:(t + 1) * 128,
                                        og * 128:(og + 1) * 128],
                                   waits=[("ve_c", p.at("ve_c")), ("stD", x1_done)],
                                   stream="ldx1G", tag=f"GX{t}_{og}")
                        tt = t_transpose(p, trbfG[:],
                                         g_sbG[:, tsub * 128:(tsub + 1) * 128],
                                         id_b[:], waits=[("ve_c", p.at("ve_c"))],
                                         inc="te_c", tag=f"GU{tg}_{og}_{tsub}")
                        ov = v_op(p, lambda E: E.tensor_add(obuf[:], x1sl[:], trbfG[:]),
                                  waits=[("te_c", tt), ("ldx1G", lx),
                                         ("stG", p.counts.get("stG", 0))],
                                  tag=f"GO{tg}_{og}_{tsub}")
                        p.dma("sp", out[t * 128:(t + 1) * 128,
                                        og * 128:(og + 1) * 128], obuf[:],
                              waits=[("ve_c", ov)], stream="stG", tag=f"GZ{t}_{og}")

    nc = p.finalize(final_waits=[("stG", p.at("stG"))])
    return nc, p


T = 8192
NCORE = 8

def tok_idx(c):
    return np.concatenate([np.arange(128 * j, 128 * (j + 1)) for j in chunks_of(c)])

def _mk_masks(c):
    ch = chunks_of(c)
    m = np.zeros((64, 128, 128), np.float32)
    kk = np.arange(128)[:, None]; qq = np.arange(128)[None, :]
    for kc in range(64):
        j = ch[kc // 8]
        if kc < j:
            m[kc] = 1.0
        elif kc == j:
            m[kc] = (128 * kc + kk <= 128 * j + qq)
    return m.astype(ml_dtypes.bfloat16)

_NC_CACHE = {}

def kernel(**inputs):
    if "nc" not in _NC_CACHE:
        _NC_CACHE["nc"] = build_fused()[0]
    nc = _NC_CACHE["nc"]
    x_full = np.ascontiguousarray(
        np.asarray(inputs["L0_attn_x_q"], np.float32).reshape(T, 2048))
    wq0 = np.ascontiguousarray(np.asarray(inputs["L0_attn_Wq"]).reshape(2048, 2048))
    wk0 = np.ascontiguousarray(np.asarray(inputs["L0_attn_Wk"]).reshape(2048, 512))
    wv0 = np.ascontiguousarray(np.asarray(inputs["L0_attn_Wv"]).reshape(2048, 512))
    gwt = np.ascontiguousarray(np.asarray(inputs["L0_ffn_gate_w"]).T)
    wo0 = np.ascontiguousarray(np.asarray(inputs["L0_attn_Wo"]))
    wg_all = np.asarray(inputs["L0_ffn_w_gate"])
    wu_all = np.asarray(inputs["L0_ffn_w_up"])
    wd_all = np.asarray(inputs["L0_ffn_w_down"])
    wsg = np.ascontiguousarray(np.asarray(inputs["L0_ffn_ws_gate"]))
    wsu = np.ascontiguousarray(np.asarray(inputs["L0_ffn_ws_up"]))
    wsd = np.ascontiguousarray(np.asarray(inputs["L0_ffn_ws_down"]))
    wq1 = np.ascontiguousarray(np.asarray(inputs["L1_attn_Wq"]))
    wk1 = np.ascontiguousarray(np.asarray(inputs["L1_attn_Wk"]))
    wv1 = np.ascontiguousarray(np.asarray(inputs["L1_attn_Wv"]))
    bqT = np.ascontiguousarray(np.asarray(inputs["L1_attn_bq"]).T)
    bkT = np.ascontiguousarray(np.asarray(inputs["L1_attn_bk"]).T)
    bvT = np.ascontiguousarray(np.asarray(inputs["L1_attn_bv"]).T)
    wfcT = np.ascontiguousarray(np.asarray(inputs["L1_ffn_w_fc"]).T)
    wprT = np.ascontiguousarray(np.asarray(inputs["L1_ffn_w_proj"]).T)

    in_maps = []
    for c in range(NCORE):
        ti = tok_idx(c)
        in_maps.append(dict(
            x=np.ascontiguousarray(x_full[ti]),
            wq=wq0, wk=wk0, wv=wv0, gwt=gwt, wo=wo0,
            e0vec=np.full((128, 1), 2.0 * c, np.float32),
            wg=np.ascontiguousarray(wg_all[2 * c:2 * c + 2]),
            wu=np.ascontiguousarray(wu_all[2 * c:2 * c + 2]),
            wd=np.ascontiguousarray(wd_all[2 * c:2 * c + 2]),
            myidx=ti.astype(np.uint32).reshape(1024, 1),
            wsg=wsg, wsu=wsu, wsd=wsd,
            wq1=wq1, bqT=bqT, wk1=wk1, bkT=bkT, wv1=wv1, bvT=bvT,
            masks=_mk_masks(c), wfcT=wfcT, wprT=wprT,
        ))
    res = run_bass_kernel_spmd(nc, in_maps, core_ids=list(range(NCORE)))
    final = np.empty((T, 2048), np.float32)
    for c in range(NCORE):
        final[tok_idx(c)] = np.asarray(res.results[c]["out"])
    return final.reshape(1, T, 2048)


# version 04 (final)
# speedup vs baseline: 1.9529x; 1.0674x over previous
import os
"""Program-recorder framework for raw-Bass: write phases in dataflow order,
emit per-engine instruction streams inside one nc.Block().

Engines: "te" (tensor), "ve" (vector/DVE), "act" (scalar), "po" (gpsimd/pool),
"sp" (sync queue, DMA only).

Semaphores are created on demand; wait values are resolved at RECORD time, so a
wait can only reference increments already recorded -> no deadlocks by
construction (as long as the dataflow order is correct).
"""
import numpy as np
import ml_dtypes
from contextlib import ExitStack
import concourse.bass as bass
import concourse.mybir as mybir
from concourse.bass_utils import run_bass_kernel_spmd
from concourse.bass_core_rust import IndirectOffsetOnAxis

F32 = mybir.dt.float32
BF16 = mybir.dt.bfloat16
U32 = mybir.dt.uint32
AF = mybir.ActivationFunctionType
ALU = mybir.AluOpType
AX = mybir.AxisListType

ENGINES = ("te", "ve", "act", "po", "sp")

class Prog:
    def __init__(self, name="prog"):
        self.nc = bass.Bass("TRN2", target_bir_lowering=False)
        self.es = ExitStack()
        self.ops = {e: [] for e in ENGINES}
        self.sems = {}
        self.counts = {}
        self.name = name

    def sem(self, name):
        if name not in self.sems:
            self.sems[name] = self.es.enter_context(self.nc.semaphore(name))
            self.counts[name] = 0
        return self.sems[name]

    def at(self, name):
        self.sem(name)
        return self.counts[name]

    def sb(self, name, shape, dtype):
        es = getattr(self, "_scope_es", None) or self.es
        return es.enter_context(self.nc.sbuf_tensor(name, shape, dtype))

    def fence(self, tag="fence"):
        snap = [(s, v) for s, v in self.counts.items() if v > 0]
        for e in ENGINES:
            self.ops[e].append((lambda E: None, list(snap), None, 1, tag))

    def scope(self):
        from contextlib import ExitStack, contextmanager
        @contextmanager
        def _s():
            old = getattr(self, "_scope_es", None)
            self._scope_es = ExitStack()
            try:
                yield
            finally:
                self._scope_es.close()
                self._scope_es = old
                self.fence("scope_fence")
        return _s()

    def ps(self, name, shape, dtype=F32):
        es = getattr(self, "_scope_es", None) or self.es
        return es.enter_context(self.nc.psum_tensor(name, shape, dtype))

    def din(self, name, shape, dtype):
        return self.nc.dram_tensor(name, shape, dtype, kind="ExternalInput").ap()

    def dout(self, name, shape, dtype):
        return self.nc.dram_tensor(name, shape, dtype, kind="ExternalOutput").ap()

    def dtmp(self, name, shape, dtype):
        return self.nc.dram_tensor(name, shape, dtype).ap()

    AUTOSYNC = ("ve", "act", "po")

    def emit(self, eng, fn, waits=(), inc=None, inc_k=1, tag="", chain=True):
        w = [(s, v) for (s, v) in waits]
        if eng in self.AUTOSYNC and chain:
            ec = eng + "_c"
            if inc is None:
                inc = ec
                inc_k = 1
            if inc == ec:
                w.append((ec, self.at(ec)))
        self.ops[eng].append((fn, w, inc, inc_k, tag))
        if inc is not None:
            self.sem(inc)
            self.counts[inc] += inc_k
            return self.counts[inc]
        return None

    def dma(self, eng, out, in_, waits=(), stream=None, tag="", **kw):
        s = stream or (eng + "_dma")
        return self.emit(eng, lambda E, o=out, i=in_: E.dma_start(out=o, in_=i, **kw),
                         waits=waits, inc=s, inc_k=16, tag=tag)

    def finalize(self, final_waits=()):
        for (s, v) in final_waits:
            self.emit("sp", lambda E: None, waits=[(s, v)], tag="final_wait")
        nc = self.nc
        with nc.Block() as blk:
            def mk_body(ename):
                def body(E):
                    waited = {}
                    for fn, waits, inc, inc_k, tag in self.ops[ename]:
                        for (s, v) in waits:
                            if v <= 0 or waited.get(s, -1) >= v:
                                continue
                            E.wait_ge(self.sems[s], v)
                            waited[s] = v
                        instr = fn(E)
                        if inc is not None:
                            assert instr is not None, f"op with inc returned None: {tag}"
                            instr.then_inc(self.sems[inc], inc_k)
                return body
            if self.ops["sp"]:
                blk.sync(mk_body("sp"))
            if self.ops["te"]:
                blk.tensor(mk_body("te"))
            if self.ops["ve"]:
                blk.vector(mk_body("ve"))
            if self.ops["act"]:
                blk.scalar(mk_body("act"))
            if self.ops["po"]:
                blk.gpsimd(mk_body("po"))
        return nc

    def close(self):
        self.es.close()


def t_matmul(p, out_ap, lhsT, rhs, start, stop, waits=(), inc=None, tag=""):
    return p.emit("te", lambda E, o=out_ap, l=lhsT, r=rhs, st=start, sp=stop:
                  E.matmul(o, l, r, start=st, stop=sp),
                  waits=waits, inc=inc, tag=tag)

def t_transpose(p, out_ap, in_ap, ident, waits=(), inc=None, tag=""):
    return p.emit("te", lambda E, o=out_ap, i=in_ap, d=ident: E.transpose(o, i, d),
                  waits=waits, inc=inc, tag=tag)

def v_op(p, fn, waits=(), inc=None, tag=""):
    return p.emit("ve", fn, waits=waits, inc=inc, tag=tag)

def a_op(p, fn, waits=(), inc=None, tag=""):
    return p.emit("act", fn, waits=waits, inc=inc, tag=tag)

def g_op(p, fn, waits=(), inc=None, inc_k=1, tag=""):
    return p.emit("po", fn, waits=waits, inc=inc, inc_k=inc_k, tag=tag)

def bcast_ap(t, P, n, m, col=0):
    ap = t[:, col:col+n]
    a = ap.ap
    return bass.AP(ap.tensor, ap.offset, [list(a[0]), list(a[1]), [0, m]])


"""Shared MoE routing emitter.

Computes, replicated from full-sequence logits [8192,16] fp32 (DRAM, token-major):
  - per tile j (64 tiles of 128 tokens): softmax probs, top-2 (w1,w2,e1,e2)
  - global exclusive cumsum positions pos1/pos2 per token
Results left in SBUF arrays (fp32):
  warr1/warr2 [128,64], earr1/earr2 [128,64], pos1/pos2 [128,64]
Caller decides what to do with them (scatter gather-lists / compute slotids).
"""

def emit_routing(p, lg_dram, own_tiles=None, lgt_filler=None):
    if own_tiles is None:
        own_tiles = list(range(64))
    lgt = p.sb("r_lgt", [128, 64 * 16], F32)
    exs = p.sb("r_exs", [128, 16], F32)
    m1 = p.sb("r_m1", [128, 1], F32)
    nm = p.sb("r_nm", [128, 1], F32)
    ssum = p.sb("r_ssum", [128, 1], F32)
    rs = p.sb("r_rs", [128, 1], F32)
    probs = p.sb("r_probs", [128, 16], F32)
    m8 = p.sb("r_m8", [128, 8], F32)
    i8 = p.sb("r_i8", [128, 8], U32)
    i1f = p.sb("r_i1f", [128, 1], F32)
    i2f = p.sb("r_i2f", [128, 1], F32)
    oh1 = p.sb("r_oh1", [128, 64 * 16], BF16)
    oh2 = p.sb("r_oh2", [128, 64 * 16], BF16)
    ohs = p.sb("r_ohs", [128, 16], F32)
    iota16 = p.sb("r_iota16", [128, 16], F32)
    nem = p.sb("r_nem", [16, 8192], BF16)
    ones16 = p.sb("r_ones16", [16, 1], F32)
    sincl = p.sb("r_sincl", [16, 8192], F32)
    sext = p.sb("r_sext", [128, 16], F32)
    tmp16 = p.sb("r_tmp16", [128, 16], F32)
    warr1 = p.sb("r_w1", [128, 64], F32)
    warr2 = p.sb("r_w2", [128, 64], F32)
    earr1 = p.sb("r_e1", [128, 64], F32)
    earr2 = p.sb("r_e2", [128, 64], F32)
    pos1 = p.sb("r_pos1", [128, 64], F32)
    pos2 = p.sb("r_pos2", [128, 64], F32)
    trp16 = p.ps("r_trp16", [16, 128], F32)
    trp128 = p.ps("r_trp128", [128, 16], F32)

    lg_ve = 0
    if lgt_filler is None:
        p.dma("sp", lgt[:], bass.AP(lg_dram.tensor, 0, [[16, 128], [2048, 64], [1, 16]]),
              stream="r_ld", tag="lgt")
    else:
        lg_ve = lgt_filler(p, lgt)
    lg_done = p.at("r_ld")

    g_op(p, lambda E: E.iota(iota16[:], [[1, 16]], base=0, channel_multiplier=0,
                             allow_small_or_imprecise_dtypes=True), tag="iota16")
    io_done = p.at("po_c")
    v_op(p, lambda E: E.memset(ones16[:], 1.0), tag="ones16")

    id_done = p.at("ve_c")
    id_f = p.id_f

    for j in range(64):
        lj = lgt[:, j * 16:(j + 1) * 16]
        v_op(p, lambda E, lj=lj: E.reduce_max(m1[:], lj, axis=AX.X),
             waits=[("r_ld", lg_done), ("ve_c", max(lg_ve, p.at("ve_c")))], tag=f"rm{j}")
        v_op(p, lambda E: E.tensor_scalar(nm[:], m1[:], -1.0, None, ALU.mult), tag=f"rn{j}")
        nm_ve = p.at("ve_c")
        ae = a_op(p, lambda E, lj=lj: E.activation(exs[:], lj, AF.Exp, bias=nm[:], scale=1.0),
                  waits=[("ve_c", nm_ve)], tag=f"re{j}")
        v_op(p, lambda E: E.reduce_sum(ssum[:], exs[:], axis=AX.X),
             waits=[("act_c", ae)], tag=f"rs{j}")
        v_op(p, lambda E: E.reciprocal(rs[:], ssum[:]), tag=f"rr{j}")
        v_op(p, lambda E: E.tensor_scalar(probs[:], exs[:], rs[:], None, ALU.mult), tag=f"rp{j}")
        v_op(p, lambda E: E.max(m8[:], probs[:]), tag=f"rx{j}")
        v_op(p, lambda E: E.max_index(i8[:], m8[:], probs[:]), tag=f"ri{j}")
        v_op(p, lambda E, j=j: E.tensor_copy(warr1[:, j:j + 1], m8[:, 0:1]), tag=f"rw1{j}")
        v_op(p, lambda E, j=j: E.tensor_copy(warr2[:, j:j + 1], m8[:, 1:2]), tag=f"rw2{j}")
        v_op(p, lambda E: E.tensor_copy(i1f[:], i8[:, 0:1]), tag=f"rc1{j}")
        v_op(p, lambda E: E.tensor_copy(i2f[:], i8[:, 1:2]), tag=f"rc2{j}")
        v_op(p, lambda E, j=j: E.tensor_copy(earr1[:, j:j + 1], i1f[:]), tag=f"re1{j}")
        v_op(p, lambda E, j=j: E.tensor_copy(earr2[:, j:j + 1], i2f[:]), tag=f"re2{j}")
        v_op(p, lambda E, j=j: E.tensor_scalar(oh1[:, j * 16:(j + 1) * 16], iota16[:],
                                               i1f[:], None, ALU.is_equal),
             waits=[("po_c", io_done)], tag=f"ro1{j}")
        v_op(p, lambda E, j=j: E.tensor_scalar(oh2[:, j * 16:(j + 1) * 16], iota16[:],
                                               i2f[:], None, ALU.is_equal), tag=f"ro2{j}")
        v_op(p, lambda E, j=j: E.tensor_add(ohs[:], oh1[:, j * 16:(j + 1) * 16],
                                            oh2[:, j * 16:(j + 1) * 16]), tag=f"roa{j}")
        ve_ohs = p.at("ve_c")
        tt = t_transpose(p, trp16[:], ohs[:], id_f[:],
                         waits=[("ve_c", ve_ohs)], inc="te_c", tag=f"rt{j}")
        v_op(p, lambda E, j=j: E.tensor_copy(nem[:, j * 128:(j + 1) * 128], trp16[:]),
             waits=[("te_c", tt)], tag=f"rte{j}")
    nem_done = p.at("ve_c")
    ones_b = bass.AP(ones16[:].tensor, 0, [list(ones16[:].ap[0]), [0, 8192]])
    v_op(p, lambda E: E.tensor_tensor_scan(sincl[:], ones_b, nem[:], 0.0,
                                           ALU.mult, ALU.add),
         waits=[("ve_c", nem_done)], tag="rscan")
    v_op(p, lambda E: E.tensor_sub(sincl[:], sincl[:], nem[:]), tag="rsub")
    sx_done = p.at("ve_c")
    for j in own_tiles:
        tt = t_transpose(p, trp128[:], sincl[:, j * 128:(j + 1) * 128], id_f[0:16, 0:16],
                         waits=[("ve_c", sx_done), ("ve_c", p.at("ve_c"))],
                         inc="te_c", tag=f"rpt{j}")
        ve0 = v_op(p, lambda E: E.tensor_copy(sext[:], trp128[:]),
                   waits=[("te_c", tt)], tag=f"rpe{j}")
        v_op(p, lambda E, j=j: E.tensor_mul(tmp16[:], oh1[:, j * 16:(j + 1) * 16], sext[:]),
             tag=f"rq1{j}")
        v_op(p, lambda E, j=j: E.reduce_sum(pos1[:, j:j + 1], tmp16[:], axis=AX.X),
             tag=f"rr1{j}")
        v_op(p, lambda E, j=j: E.tensor_mul(tmp16[:], oh2[:, j * 16:(j + 1) * 16], sext[:]),
             tag=f"rq2{j}")
        v_op(p, lambda E, j=j: E.reduce_sum(pos2[:, j:j + 1], tmp16[:], axis=AX.X),
             tag=f"rr2{j}")
    done = p.at("ve_c")
    return dict(warr1=warr1, warr2=warr2, earr1=earr1, earr2=earr2,
                pos1=pos1, pos2=pos2, done=done)

"""Single-launch fused megakernel: both layers, 3 AllGathers, activations
resident in SBUF between phases (x1 bounced via local DRAM).

Inputs per core: x [1024,2048] f32, wq [2048,2048] f32, wk/wv [2048,512],
gwt [2048,16], wo [2048,2048], e0vec [128,1] f32, wg/wu [2,2048,1408],
wd [2,1408,2048], myidx [1024,1] u32 (GLOBAL token ids of own tokens),
wsg/wsu [2048,2816], wsd [2816,2048], wq1 [16,2048,128], bqT [128,16],
wk1/wv1 [4,2048,128], bkT/bvT [128,4], masks [64,128,128] bf16,
wfcT [2048,5120], wprT [5120,2048].   Output: out [1024,2048] f32.

AG row convention: gathered full-sequence tensors are rank-major: row
r*1024+i = token tok_idx(r)[i]. The MoE token list stores AG-row ids
(base agbase(j) per global tile). rodram slot/weight arrays are indexed by
GLOBAL token id (tile-major j*128+p), gathered with myidx.
L1 causality: iterate k-chunks in GLOBAL order, index AG-ordered buffers at
agcol(kc)."""

def chunks_of(c):
    return sorted([c, 15 - c, 16 + c, 31 - c, 32 + c, 47 - c, 48 + c, 63 - c])

_OWNER = {}
for _c in range(8):
    for _m, _j in enumerate(chunks_of(_c)):
        _OWNER[_j] = (_c, _m)

def agcol(j):
    r, m = _OWNER[j]
    return 8 * r + m

def agbase(j):
    r, m = _OWNER[j]
    return r * 1024 + m * 128

SCALE = 1.0 / (128.0 ** 0.5)


def build_fused():
    p = Prog("FUSED")
    p.dtmp_shared = lambda name, shape, dtype: p.nc.dram_tensor(
        name, shape, dtype, addr_space="Shared").ap()
    x_in = p.din("x", [1024, 2048], F32)
    wq = p.din("wq", [2048, 2048], F32)
    wk = p.din("wk", [2048, 512], F32)
    wv = p.din("wv", [2048, 512], F32)
    gwt = p.din("gwt", [2048, 16], F32)
    wo = p.din("wo", [2048, 2048], F32)
    e0vec = p.din("e0vec", [128, 1], F32)
    wg = p.din("wg", [2, 2048, 1408], F32)
    wu = p.din("wu", [2, 2048, 1408], F32)
    wd = p.din("wd", [2, 1408, 2048], F32)
    myidx = p.din("myidx", [1024, 1], U32)
    wsg = p.din("wsg", [2048, 2816], F32)
    wsu = p.din("wsu", [2048, 2816], F32)
    wsd = p.din("wsd", [2816, 2048], F32)
    wq1 = p.din("wq1", [16, 2048, 128], F32)
    bqT = p.din("bqT", [128, 16], F32)
    wk1 = p.din("wk1", [4, 2048, 128], F32)
    bkT = p.din("bkT", [128, 4], F32)
    wv1 = p.din("wv1", [4, 2048, 128], F32)
    bvT = p.din("bvT", [128, 4], F32)
    masks = p.din("masks", [64, 128, 128], BF16)
    wfcT = p.din("wfcT", [2048, 5120], F32)
    wprT = p.din("wprT", [5120, 2048], F32)
    out = p.dout("out", [1024, 2048], F32)

    xn_c = p.dtmp("xn_c", [1024, 2048], BF16)
    kh_c = p.dtmp("kh_c", [512, 1024], BF16)
    v_c = p.dtmp("v_c", [1024, 512], BF16)
    lg_c = p.dtmp("lg_c", [16, 1024], F32)
    xn_ag = p.dtmp_shared("xn_ag", [8192, 2048], BF16)
    kh_ag = p.dtmp_shared("kh_ag", [4096, 1024], BF16)
    v_ag = p.dtmp_shared("v_ag", [8192, 512], BF16)
    lg_ag = p.dtmp_shared("lg_ag", [128, 1024], F32)
    attn_d = p.dtmp("attn_d", [2048, 1024], BF16)
    listd = p.dtmp("listd", [144, 128], U32)
    out_c = p.dtmp("out_c", [2048, 2048], BF16)
    outf_ag = p.dtmp_shared("outf_ag", [16384, 2048], BF16)
    rodram = p.dtmp("rodram", [4, 64, 128], F32)
    x1_d = p.dtmp("x1_d", [1024, 2048], F32)
    k1_c = p.dtmp("k1_c", [512, 1024], BF16)
    v1_c = p.dtmp("v1_c", [1024, 512], BF16)
    k1_ag = p.dtmp_shared("k1_ag", [4096, 1024], BF16)
    v1_ag = p.dtmp_shared("v1_ag", [8192, 512], BF16)

    idw = p.sb("idw", [128, 128], F32)
    id_f = p.sb("id_f", [128, 128], F32)
    id_b = p.sb("id_b", [128, 128], BF16)
    eps5 = p.sb("eps5", [128, 1], F32)
    eps6 = p.sb("eps6", [128, 1], F32)
    ones_b = p.sb("ones_b", [128, 1], BF16)
    ones_row = p.sb("ones_row", [1, 128], F32)
    onescol = p.sb("onescol", [128, 1], F32)
    iota_p = p.sb("iota_p", [128, 1], F32)
    g_op(p, lambda E: E.iota(idw[:], [[-1, 128]], base=0, channel_multiplier=1,
                             allow_small_or_imprecise_dtypes=True), tag="iota")
    g_op(p, lambda E: E.iota(iota_p[:], [[0, 1]], base=0, channel_multiplier=1,
                             allow_small_or_imprecise_dtypes=True), tag="iop")
    io_done = p.at("po_c")
    v_op(p, lambda E: E.tensor_scalar(id_f[:], idw[:], 0.0, None, ALU.is_equal),
         waits=[("po_c", io_done)], tag="idf")
    v_op(p, lambda E: E.tensor_copy(id_b[:], id_f[:]), tag="idb")
    v_op(p, lambda E: E.memset(eps5[:], 1e-5), tag="eps5")
    v_op(p, lambda E: E.memset(eps6[:], 1e-6), tag="eps6")
    v_op(p, lambda E: E.memset(ones_b[:], 1.0), tag="ones_b")
    v_op(p, lambda E: E.memset(ones_row[:], 1.0), tag="ones_row")
    v_op(p, lambda E: E.memset(onescol[:], 1.0), tag="onescol")
    p.id_f = id_f
    const_done = p.at("ve_c")

    def ln_tiles(pfx, x_dram, dst_fm_sb, xn_tm_dram=None, logits=None,
                 x_ready=()):
        xt_b = [p.sb(pfx + f"xt{i}", [128, 2048], F32) for i in range(2)]
        xnf32 = p.sb(pfx + "xnf32", [128, 2048], F32)
        xn_tm_bf = p.sb(pfx + "xntm", [128, 2048], BF16)
        sumx = p.sb(pfx + "sumx", [128, 8], F32)
        ssq = p.sb(pfx + "ssq", [128, 8], F32)
        meanv = p.sb(pfx + "mean", [128, 8], F32)
        varv = p.sb(pfx + "var", [128, 8], F32)
        m2 = p.sb(pfx + "m2", [128, 8], F32)
        rstd = p.sb(pfx + "rstd", [128, 8], F32)
        sq_scr = p.sb(pfx + "sq", [128, 2048], F32)
        lgscr = p.sb(pfx + "lg", [128, 2048], F32)
        trps = p.ps(pfx + "trps", [128, 128], F32)
        if logits is not None:
            gwt_sb, lgT_sb, lgps, gwt_ld = logits
        prev_tile_te = p.at("te_c")
        last_xn_ve = [0, 0]
        for t in range(8):
            par = t % 2
            ldx = p.dma("sp", xt_b[par][:], x_dram[t * 128:(t + 1) * 128, :],
                        waits=[("ve_c", last_xn_ve[par])] + list(x_ready),
                        stream="ld" + pfx + str(par), tag=f"{pfx}x{t}")
            xt = xt_b[par][:]
            v_op(p, lambda E, xt=xt, t=t: E.reduce_sum(sumx[:, t:t + 1], xt, axis=AX.X),
                 waits=[("ld" + pfx + str(par), ldx)], tag=f"{pfx}sum{t}")
            a_op(p, lambda E, xt=xt, t=t: E.activation(sq_scr[:], xt, AF.Square,
                                                       accum_out=ssq[:, t:t + 1]),
                 waits=[("ld" + pfx + str(par), ldx), ("ve_c", p.at("ve_c"))],
                 tag=f"{pfx}ssq{t}")
            act_ssq = p.at("act_c")
            v_op(p, lambda E, t=t: E.tensor_scalar(meanv[:, t:t + 1], sumx[:, t:t + 1],
                                                   1.0 / 2048, None, ALU.mult),
                 tag=f"{pfx}mn{t}")
            v_op(p, lambda E, t=t: E.tensor_scalar(varv[:, t:t + 1], ssq[:, t:t + 1],
                                                   1.0 / 2048, None, ALU.mult),
                 waits=[("act_c", act_ssq)], tag=f"{pfx}vr{t}")
            v_op(p, lambda E, t=t: E.scalar_tensor_tensor(m2[:, t:t + 1],
                                                          meanv[:, t:t + 1], 1.0,
                                                          meanv[:, t:t + 1],
                                                          ALU.mult, ALU.mult),
                 tag=f"{pfx}m2{t}")
            v_op(p, lambda E, t=t: E.tensor_sub(varv[:, t:t + 1], varv[:, t:t + 1],
                                                m2[:, t:t + 1]), tag=f"{pfx}v2{t}")
            ve_var = p.at("ve_c")
            a_op(p, lambda E, t=t: E.activation(rstd[:, t:t + 1], varv[:, t:t + 1],
                                                AF.Ln, bias=eps5[:], scale=1.0),
                 waits=[("ve_c", max(ve_var, const_done))], tag=f"{pfx}ln{t}")
            a_op(p, lambda E, t=t: E.activation(rstd[:, t:t + 1], rstd[:, t:t + 1],
                                                AF.Exp, bias=0.0, scale=-0.5),
                 tag=f"{pfx}ex{t}")
            act_rstd = p.at("act_c")
            xn_ve = v_op(p, lambda E, xt=xt, t=t: E.tensor_scalar(
                xnf32[:], xt, meanv[:, t:t + 1], rstd[:, t:t + 1],
                ALU.subtract, ALU.mult),
                waits=[("act_c", act_rstd), ("te_c", prev_tile_te)], tag=f"{pfx}xn{t}")
            last_xn_ve[par] = xn_ve
            if xn_tm_dram is not None:
                cvt = v_op(p, lambda E: E.tensor_copy(xn_tm_bf[:], xnf32[:]),
                           waits=[("st" + pfx, p.counts.get("st" + pfx, 0))],
                           tag=f"{pfx}xtm{t}")
                p.dma("sp", xn_tm_dram[t * 128:(t + 1) * 128, :], xn_tm_bf[:],
                      waits=[("ve_c", cvt)], stream="st" + pfx, tag=f"{pfx}oxtm{t}")
            for c in range(16):
                tt = t_transpose(p, trps[:], xnf32[:, c * 128:(c + 1) * 128], id_f[:],
                                 waits=[("ve_c", p.at("ve_c")), ("ve_c", const_done),
                                        ("act_c", p.at("act_c"))],
                                 inc="te_c", tag=f"{pfx}tr{t}_{c}")
                a_ev = a_op(p, lambda E, c=c: E.activation(
                    lgscr[:, c * 128:(c + 1) * 128], trps[:], AF.Identity),
                    waits=[("te_c", tt)], tag=f"{pfx}lgev{t}_{c}")
                v_op(p, lambda E, c=c, t=t: E.tensor_copy(
                    dst_fm_sb[:, c * 1024 + t * 128: c * 1024 + (t + 1) * 128],
                    lgscr[:, c * 128:(c + 1) * 128]),
                    waits=[("act_c", a_ev)], tag=f"{pfx}fm{t}_{c}")
                if logits is not None:
                    t_matmul(p, lgps[:, 0:128], gwt_sb[:, c * 16:(c + 1) * 16],
                             lgscr[:, c * 128:(c + 1) * 128],
                             start=(c == 0), stop=(c == 15),
                             waits=[("act_c", p.at("act_c")), ("ve_c", p.at("ve_c")),
                                    ("ld" + pfx + "g", gwt_ld)],
                             inc="te_c", tag=f"{pfx}lgmm{t}_{c}")
            prev_tile_te = p.at("te_c")
            if logits is not None:
                v_op(p, lambda E, t=t: E.tensor_copy(lgT_sb[:, t * 128:(t + 1) * 128],
                                                     lgps[:, 0:128]),
                     waits=[("te_c", prev_tile_te)], tag=f"{pfx}lge2{t}")

    class PanelStream:
        def __init__(self, name, sub=2048):
            self.stg = [p.sb(name + "s0", [128, sub], F32),
                        p.sb(name + "s1", [128, sub], F32)]
            self.sub = sub
            self.prev = [0, 0]
            self.i = 0

        def load(self, tag, subs, conv=None, te_guard=None):
            cv = 0
            for idx, (src, ncols, dst_ap) in enumerate(subs):
                par = self.i % 2
                self.i += 1
                ld = p.dma("sp", self.stg[par][:, 0:ncols], src,
                           waits=[("ve_c", self.prev[par])], stream=f"ldw{par}",
                           tag=f"{tag}L{idx}")
                w = [(f"ldw{par}", ld),
                     ("te_c", p.at("te_c") if te_guard is None else te_guard)]
                if conv is None:
                    cv = v_op(p, lambda E, par=par, ncols=ncols, dst_ap=dst_ap:
                              E.tensor_copy(dst_ap, self.stg[par][:, 0:ncols]),
                              waits=w, tag=f"{tag}C{idx}")
                else:
                    cv = conv(self.stg[par][:, 0:ncols], dst_ap, w)
                self.prev[par] = cv
            return cv

    def wsub3(wt, row_pitch, c0, nc_, col0, ncols):
        return bass.AP(wt.tensor, (c0 * 128) * row_pitch + col0,
                       [[row_pitch, 128], [128 * row_pitch, nc_], [1, ncols]])

    with p.scope():
        xn_fm_sb = p.sb("XN0", [128, 16 * 1024], BF16)
        s_tm_sb = p.sb("S0", [128, 8 * 2048], BF16)
        with p.scope():
            qh_sb = p.sb("QH", [128, 16 * 1024], BF16)
            with p.scope():
                gwt_sb = p.sb("A_gwt", [128, 256], F32)
                lgT_sb = p.sb("A_lgT", [16, 1024], F32)
                lgps = p.ps("A_lgps", [16, 128], F32)
                p.dma("sp", gwt_sb[:],
                      bass.AP(gwt.tensor, 0, [[16, 128], [2048, 16], [1, 16]]),
                      stream="ldAg", tag="Agwt")
                gwt_ld = p.at("ldAg")
                ln_tiles("A", x_in, xn_fm_sb, xn_tm_dram=xn_c,
                         logits=(gwt_sb, lgT_sb, lgps, gwt_ld))
                ve_lg = p.at("ve_c")
                p.dma("sp", lg_c, lgT_sb[:], waits=[("ve_c", ve_lg)], stream="stA",
                      tag="AolgT")
                for nm, (ci, co) in {"cc1a": (xn_c, xn_ag),
                                     "cc1d": (lg_c, lg_ag)}.items():
                    g_op(p, lambda E, ci=ci, co=co: E.collective_compute(
                        "AllGather", ALU.bypass, replica_groups=[list(range(8))],
                        ins=[ci.opt()], outs=[co.opt()]),
                        waits=[("stA", p.at("stA"))], inc=nm, inc_k=1, tag=nm)
                    p.ops["po"].append((lambda E: None, [(nm, 1)], None, 1, nm + "w"))

                psA = PanelStream("A_ps")
                wpan2 = [p.sb("A_wpan0", [128, 16 * 512], BF16),
                         p.sb("A_wpan1", [128, 16 * 512], BF16)]
                sqq = p.sb("A_sqq", [128, 512], F32)
                ssqq = p.sb("A_ssqq", [128, 4], F32)
                rstd4 = p.sb("A_rstd4", [128, 4], F32)
                qtm_sb = p.sb("A_qtm", [128, 512], BF16)
                qev = p.sb("A_qev", [128, 128], BF16)
                qps = p.ps("A_qps", [128, 512], F32)
                trbf = p.ps("A_trbf", [128, 128], BF16)

                teA = {}
                def projA(wdram, pitch, ngrp, dest_kind, wtag):
                    for grp in range(ngrp):
                        par = (wtag, grp % 2)
                        wpan = wpan2[grp % 2]
                        cv = psA.load(
                            "A" + wtag + str(grp),
                            [(wsub3(wdram, pitch, 4 * s, 4, grp * 512, 512), 2048,
                              wpan[:, s * 4 * 512:(s + 1) * 4 * 512])
                             for s in range(4)],
                            te_guard=teA.get(grp % 2, 0))
                        prev_cv = 0
                        for t in range(8):
                            for c in range(16):
                                t_matmul(p, qps[:, 0:512],
                                         xn_fm_sb[:, c * 1024 + t * 128:
                                                  c * 1024 + (t + 1) * 128],
                                         wpan[:, c * 512:(c + 1) * 512],
                                         start=(c == 0), stop=(c == 15),
                                         waits=[("ve_c", max(cv, prev_cv)),
                                                ("act_c", p.at("act_c"))] if c == 0 else (),
                                         inc="te_c", tag=f"A{wtag}m{grp}_{t}_{c}")
                            te_q = p.at("te_c")
                            if dest_kind[0] == "vtm":
                                cvv = v_op(p, lambda E: E.tensor_copy(qtm_sb[:],
                                                                      qps[:, 0:512]),
                                           waits=[("te_c", te_q),
                                                  ("stA", p.counts.get("stA", 0))],
                                           tag=f"A{wtag}ev{grp}_{t}")
                                prev_cv = cvv
                                p.dma("sp", dest_kind[1][t * 128:(t + 1) * 128, :],
                                      qtm_sb[:], waits=[("ve_c", cvv)], stream="stA",
                                      tag=f"A{wtag}o{t}")
                                continue
                            a1 = a_op(p, lambda E: E.activation(sqq[:], qps[:, 0:512],
                                                                AF.Square),
                                      waits=[("te_c", te_q)], tag=f"A{wtag}s{grp}_{t}")
                            r1 = v_op(p, lambda E: E.tensor_reduce(
                                ssqq[:], sqq[:].rearrange("p (h d) -> p h d", h=4),
                                AX.X, ALU.add), waits=[("act_c", a1)],
                                tag=f"A{wtag}r{grp}_{t}")
                            a2 = a_op(p, lambda E: E.activation(rstd4[:], ssqq[:],
                                                                AF.Ln, bias=eps6[:],
                                                                scale=1.0 / 128),
                                      waits=[("ve_c", max(r1, const_done))],
                                      tag=f"A{wtag}n{grp}_{t}")
                            a3 = a_op(p, lambda E: E.activation(rstd4[:], rstd4[:],
                                                                AF.Exp, bias=0.0,
                                                                scale=-0.5),
                                      tag=f"A{wtag}x{grp}_{t}")
                            cvq = v_op(p, lambda E: E.tensor_tensor(
                                qtm_sb[:].rearrange("p (h d) -> p h d", h=4),
                                qps[:, 0:512].rearrange("p (h d) -> p h d", h=4),
                                bcast_ap(rstd4, 128, 4, 128), ALU.mult),
                                waits=[("act_c", a3)], tag=f"A{wtag}q{grp}_{t}")
                            prev_cv = cvq
                            for k in range(4):
                                tt = t_transpose(p, trbf[:],
                                                 qtm_sb[:, k * 128:(k + 1) * 128],
                                                 id_b[:],
                                                 waits=[("ve_c", p.at("ve_c"))],
                                                 inc="te_c", tag=f"A{wtag}t{grp}_{t}_{k}")
                                h = grp * 4 + k
                                if dest_kind[0] == "qh":
                                    v_op(p, lambda E, h=h, t=t: E.tensor_copy(
                                        qh_sb[:, h * 1024 + t * 128:
                                              h * 1024 + (t + 1) * 128], trbf[:]),
                                        waits=[("te_c", tt)], tag=f"Aqh{grp}_{t}_{k}")
                                else:
                                    ev = v_op(p, lambda E: E.tensor_copy(qev[:], trbf[:]),
                                              waits=[("te_c", tt),
                                                     ("stA", p.counts.get("stA", 0))],
                                              tag=f"A{wtag}e{grp}_{t}_{k}")
                                    p.dma("sp",
                                          dest_kind[1][h * 128:(h + 1) * 128,
                                                       t * 128:(t + 1) * 128], qev[:],
                                          waits=[("ve_c", ev)], stream="stA",
                                          tag=f"A{wtag}o{grp}_{t}_{k}")
                        teA[grp % 2] = p.at("te_c")

                projA(wq, 2048, 4, ("qh",), "q")
                projA(wk, 512, 1, ("dram", kh_c), "k")
                g_op(p, lambda E: E.collective_compute(
                    "AllGather", ALU.bypass, replica_groups=[list(range(8))],
                    ins=[kh_c.opt()], outs=[kh_ag.opt()]),
                    waits=[("stA", p.at("stA"))], inc="cc1b", inc_k=1, tag="cc1b")
                p.ops["po"].append((lambda E: None, [("cc1b", 1)], None, 1, "cc1bw"))
                projA(wv, 512, 1, ("vtm", v_c), "v")
                g_op(p, lambda E: E.collective_compute(
                    "AllGather", ALU.bypass, replica_groups=[list(range(8))],
                    ins=[v_c.opt()], outs=[v_ag.opt()]),
                    waits=[("stA", p.at("stA"))], inc="cc1c", inc_k=1, tag="cc1c")
                p.ops["po"].append((lambda E: None, [("cc1c", 1)], None, 1, "cc1cw"))
            with p.scope():
                kbufB = p.sb("B_k", [128, 8192], BF16)
                vbufB = p.sb("B_v", [128, 64 * 128], BF16)
                esB2 = [p.sb("B_es0", [128, 1024], BF16),
                        p.sb("B_es1", [128, 1024], BF16),
                        p.sb("B_es2", [128, 1024], BF16)]
                daccB = p.sb("B_dacc", [128, 1024], F32)
                rcB = p.sb("B_rc", [1, 1024], F32)
                bcB = p.sb("B_bc", [128, 1024], F32)
                aoB = p.sb("B_ao", [128, 1024], BF16)
                lgpB2 = [p.ps("B_lgp0", [128, 1024], F32),
                         p.ps("B_lgp1", [128, 1024], F32),
                         p.ps("B_lgp2", [128, 1024], F32)]
                avpB = p.ps("B_avp", [128, 1024], F32)
                for h in range(16):
                    g = h // 4
                    prev_sweep = p.at("te_c")
                    for r_ in range(8):
                        p.dma("sp", kbufB[:, r_ * 1024:(r_ + 1) * 1024],
                              kh_ag[512 * r_ + 128 * g: 512 * r_ + 128 * (g + 1), :],
                              waits=[("te_c", prev_sweep), ("cc1b", 1), ("cc1c", 1)] if r_ == 0 else (),
                              stream="Bldk", tag=f"Bk{h}_{r_}")
                    p.dma("sp", vbufB[:],
                          bass.AP(v_ag.tensor, g * 128,
                                  [[512, 128], [128 * 512, 64], [1, 128]]),
                          stream="Bldk", tag=f"Bv{h}")
                    ldk = p.at("Bldk")
                    last_exp = [p.at("act_c")] * 3
                    last_cons = [p.at("ve_c")] * 3
                    last_av = [p.at("te_c")] * 3
                    first_ve = p.at("ve_c")
                    for kc in range(64):
                        par = kc % 3
                        w0 = [("Bldk", ldk), ("act_c", last_exp[par])]
                        if kc < 3:
                            w0.append(("ve_c", first_ve))
                        t_matmul(p, lgpB2[par][:, 0:512],
                                 kbufB[:, kc * 128:(kc + 1) * 128],
                                 qh_sb[:, h * 1024: h * 1024 + 512], True, True,
                                 waits=w0, inc="te_c", tag=f"Bl{h}_{kc}a")
                        t_matmul(p, lgpB2[par][:, 512:1024],
                                 kbufB[:, kc * 128:(kc + 1) * 128],
                                 qh_sb[:, h * 1024 + 512: h * 1024 + 1024], True, True,
                                 inc="te_c", tag=f"Bl{h}_{kc}b")
                        te_lg = p.at("te_c")
                        a_exp = a_op(p, lambda E, par=par: E.activation(
                            esB2[par][:], lgpB2[par][:], AF.Exp, bias=0.0, scale=SCALE),
                            waits=[("te_c", te_lg), ("te_c", last_av[par]),
                                   ("ve_c", last_cons[par])], tag=f"Be{h}_{kc}")
                        last_exp[par] = a_exp
                        if kc == 0:
                            dv = v_op(p, lambda E, par=par: E.tensor_copy(
                                daccB[:], esB2[par][:]),
                                waits=[("act_c", a_exp), ("te_c", p.at("te_c"))],
                                tag=f"Bq{h}_{kc}")
                        else:
                            dv = v_op(p, lambda E, par=par: E.tensor_add(
                                daccB[:], daccB[:], esB2[par][:]),
                                waits=[("act_c", a_exp)], tag=f"Bq{h}_{kc}")
                        last_cons[par] = dv
                        st0 = (kc == 0); sp0 = (kc == 63)
                        t_matmul(p, avpB[:, 0:512], vbufB[:, kc * 128:(kc + 1) * 128],
                                 esB2[par][:, 0:512], st0, sp0,
                                 waits=[("act_c", a_exp)], inc="te_c",
                                 tag=f"Ba{h}_{kc}a")
                        t_matmul(p, avpB[:, 512:1024], vbufB[:, kc * 128:(kc + 1) * 128],
                                 esB2[par][:, 512:1024], st0, sp0, inc="te_c",
                                 tag=f"Ba{h}_{kc}b")
                        last_av[par] = p.at("te_c")
                    dacc_done = p.at("ve_c")
                    t_matmul(p, lgpB2[0][0:1, 0:512], onescol[:], daccB[:, 0:512],
                             True, True,
                             waits=[("ve_c", dacc_done), ("act_c", p.at("act_c"))],
                             inc="te_c", tag=f"Bn{h}a")
                    dn_mm = t_matmul(p, lgpB2[0][0:1, 512:1024], onescol[:],
                                     daccB[:, 512:1024], True, True, inc="te_c",
                                     tag=f"Bn{h}b")
                    rce = v_op(p, lambda E: E.tensor_copy(rcB[:], lgpB2[0][0:1, :]),
                               waits=[("te_c", dn_mm)], tag=f"Bre{h}")
                    rc = v_op(p, lambda E: E.reciprocal(rcB[:], rcB[:]),
                              waits=[("ve_c", rce)], tag=f"Br{h}")
                    t_matmul(p, lgpB2[1][:, 0:512], ones_row[:], rcB[0:1, 0:512],
                             True, True,
                             waits=[("ve_c", rc), ("act_c", p.at("act_c"))],
                             inc="te_c", tag=f"Bb{h}a")
                    bc_mm = t_matmul(p, lgpB2[1][:, 512:1024], ones_row[:],
                                     rcB[0:1, 512:1024], True, True, inc="te_c",
                                     tag=f"Bb{h}b")
                    bce = v_op(p, lambda E: E.tensor_copy(bcB[:], lgpB2[1][:]),
                               waits=[("te_c", bc_mm)], tag=f"Bc{h}")
                    ao = v_op(p, lambda E: E.scalar_tensor_tensor(
                        aoB[:], avpB[:], 1.0, bcB[:], ALU.mult, ALU.mult),
                        waits=[("ve_c", bce), ("stB", p.counts.get("stB", 0))],
                        tag=f"Bo{h}")
                    p.dma("sp", attn_d[h * 128:(h + 1) * 128, :], aoB[:],
                          waits=[("ve_c", ao)], stream="stB", tag=f"Bso{h}")
                attn_done_st = p.at("stB")

                psB = PanelStream("B_ps")
                wpanB2 = [p.sb("B_wpan0", [128, 16 * 512], BF16),
                          p.sb("B_wpan1", [128, 16 * 512], BF16)]
                atc = [p.sb(f"B_atc{i}", [128, 2048], BF16) for i in range(2)]
                teB = {}
                for grp in range(4):
                    wpanB = wpanB2[grp % 2]
                    cv = psB.load("Bw" + str(grp),
                                  [(wsub3(wo, 2048, 4 * s, 4, grp * 512, 512), 2048,
                                    wpanB[:, s * 4 * 512:(s + 1) * 4 * 512])
                                   for s in range(4)],
                                  te_guard=teB.get(grp % 2, 0))
                    prev_ev = 0
                    for t in range(8):
                        par = t % 2
                        lda = p.dma("sp", atc[par][:],
                                    bass.AP(attn_d.tensor, t * 128,
                                            [[1024, 128], [128 * 1024, 16], [1, 128]]),
                                    waits=[("stB", attn_done_st),
                                           ("te_c", p.at("te_c"))],
                                    stream="Blda", tag=f"Bla{grp}_{t}")
                        for c in range(16):
                            t_matmul(p, lgpB2[2][:, 0:512],
                                     atc[par][:, c * 128:(c + 1) * 128],
                                     wpanB[:, c * 512:(c + 1) * 512],
                                     start=(c == 0), stop=(c == 15),
                                     waits=[("ve_c", max(cv, prev_ev)),
                                            ("Blda", lda)] if c == 0 else (),
                                     inc="te_c", tag=f"Bm{grp}_{t}_{c}")
                        te_w = p.at("te_c")
                        ev = v_op(p, lambda E, t=t, grp=grp: E.tensor_copy(
                            s_tm_sb[:, t * 2048 + grp * 512: t * 2048 + (grp + 1) * 512],
                            lgpB2[2][:, 0:512]), waits=[("te_c", te_w)], tag=f"Bv{grp}_{t}")
                        prev_ev = ev
                    teB[grp % 2] = p.at("te_c")
        s_done = p.at("ve_c")

        e0_sb = p.sb("e0_sb", [128, 1], F32)
        p.dma("sp", e0_sb[:], e0vec, stream="ld_e0", tag="e0")
        zer = p.sb("zer", [128, 128], U32)
        ze = v_op(p, lambda E: E.memset(zer[:], 0), tag="zer")
        p.dma("sp", listd[0:128, :], zer[:], waits=[("ve_c", ze)], stream="lz", tag="lz0")
        p.dma("sp", listd[128:144, :], zer[0:16, :], waits=[("ve_c", ze)], stream="lz",
              tag="lz1")
        lz_done = p.at("lz")

        with p.scope():
            lgrow = p.sb("C_lgrow", [16, 128], F32)
            trp_lg = p.ps("C_trplg", [128, 16], F32)

            def lg_filler(p2, lgt):
                last = 0
                for j in range(64):
                    r_, m_ = _OWNER[j]
                    l1 = p.dma("sp", lgrow[:],
                               lg_ag[16 * r_:16 * (r_ + 1), 128 * m_:128 * (m_ + 1)],
                               waits=[("cc1d", 1), ("ve_c", last)],
                               stream="Clgl", tag=f"Clg{j}")
                    tt = t_transpose(p, trp_lg[:], lgrow[:], id_f[0:16, 0:16],
                                     waits=[("Clgl", l1), ("ve_c", p.at("ve_c"))],
                                     inc="te_c", tag=f"Clt{j}")
                    last = v_op(p, lambda E, j=j: E.tensor_copy(
                        lgt[:, j * 16:(j + 1) * 16], trp_lg[:]),
                        waits=[("te_c", tt)], tag=f"Cle{j}")
                return last

            r = emit_routing(p, None, own_tiles=list(range(64)), lgt_filler=lg_filler)

            slot = p.sb("C_slot", [128, 1], F32)
            kept = p.sb("C_kept", [128, 1], F32)
            erel = p.sb("C_erel", [128, 1], F32)
            cge = p.sb("C_cge", [128, 1], F32)
            dstf = p.sb("C_dstf", [128, 1], F32)
            dstu2 = [p.sb("C_dstu0", [128, 1], U32), p.sb("C_dstu1", [128, 1], U32)]
            tknu2 = [p.sb("C_tknu0", [128, 1], U32), p.sb("C_tknu1", [128, 1], U32)]
            tknf = p.sb("C_tknf", [128, 1], F32)
            last_scat = [0, 0]
            sc_i = 0
            for j in range(64):
                for k in (1, 2):
                    spar = sc_i % 2
                    sc_i += 1
                    dstu = dstu2[spar]
                    tknu = tknu2[spar]
                    pos = (r["pos1"] if k == 1 else r["pos2"])[:, j:j + 1]
                    ear = (r["earr1"] if k == 1 else r["earr2"])[:, j:j + 1]
                    v_op(p, lambda E, pos=pos: E.tensor_scalar_min(slot[:], pos, 1023.0),
                         waits=[("ve_c", r["done"])] if (j == 0 and k == 1) else (),
                         tag=f"Cs{j}_{k}")
                    v_op(p, lambda E, pos=pos: E.tensor_scalar(kept[:], pos, 1023.5,
                                                               None, ALU.is_le),
                         tag=f"Ck{j}_{k}")
                    v_op(p, lambda E, ear=ear: E.tensor_sub(erel[:], ear, e0_sb[:]),
                         waits=[("ld_e0", p.at("ld_e0"))] if (j == 0 and k == 1) else (),
                         tag=f"Ce{j}_{k}")
                    v_op(p, lambda E: E.tensor_scalar_add(erel[:], erel[:], 16.0),
                         tag=f"Ca{j}_{k}")
                    v_op(p, lambda E: E.tensor_scalar(cge[:], erel[:], 15.5, None,
                                                      ALU.is_gt), tag=f"Cg{j}_{k}")
                    v_op(p, lambda E: E.scalar_tensor_tensor(erel[:], cge[:], -16.0,
                                                             erel[:], ALU.mult,
                                                             ALU.add), tag=f"Cm{j}_{k}")
                    v_op(p, lambda E: E.scalar_tensor_tensor(dstf[:], erel[:], 1024.0,
                                                             slot[:], ALU.mult,
                                                             ALU.add), tag=f"C0{j}_{k}")
                    tb_ = 16384.0 + (k - 1) * 128
                    v_op(p, lambda E, tb_=tb_: E.tensor_scalar(cge[:], iota_p[:], 1.0,
                                                               tb_, ALU.mult, ALU.add),
                         tag=f"Ct{j}_{k}")
                    v_op(p, lambda E: E.tensor_sub(dstf[:], dstf[:], cge[:]),
                         tag=f"C1{j}_{k}")
                    v_op(p, lambda E: E.scalar_tensor_tensor(dstf[:], dstf[:], 1.0,
                                                             kept[:], ALU.mult,
                                                             ALU.mult), tag=f"C2{j}_{k}")
                    v_op(p, lambda E: E.tensor_add(dstf[:], dstf[:], cge[:]),
                         tag=f"C3{j}_{k}")
                    v_op(p, lambda E, dstu=dstu: E.tensor_copy(dstu[:], dstf[:]),
                         waits=[(f"sc{spar}", last_scat[spar])], tag=f"C4{j}_{k}")
                    base = float(agbase(j))
                    v_op(p, lambda E, base=base: E.tensor_scalar(tknf[:], iota_p[:],
                                                                 1.0, base, ALU.mult,
                                                                 ALU.add),
                         tag=f"C5{j}_{k}")
                    cvt = v_op(p, lambda E, tknu=tknu: E.tensor_copy(tknu[:], tknf[:]),
                               tag=f"C6{j}_{k}")
                    g_op(p, lambda E, dstu=dstu, tknu=tknu: E.indirect_dma_start(
                        out=bass.AP(listd.tensor, 0, [[1, 144 * 128], [1, 1]]),
                        out_offset=IndirectOffsetOnAxis(dstu[:], 0),
                        in_=tknu[:], in_offset=None),
                        waits=[("ve_c", cvt), ("lz", lz_done)], inc=f"sc{spar}",
                        inc_k=16, tag=f"C7{j}_{k}")
                    last_scat[spar] = p.at(f"sc{spar}")
            scat_done = (p.at("sc0"), p.at("sc1"))

            slq = p.sb("C_slq", [128, 64], F32)
            tmpq = p.sb("C_tmpq", [128, 64], F32)
            sl64 = p.sb("C_sl64", [64, 128], F32)
            trq = p.ps("C_trq", [64, 128], F32)
            for k in (1, 2):
                posk = r["pos1"] if k == 1 else r["pos2"]
                eark = r["earr1"] if k == 1 else r["earr2"]
                v_op(p, lambda E, posk=posk: E.tensor_scalar_min(tmpq[:], posk[:],
                                                                 1023.0),
                     waits=[("ve_c", r["done"])], tag=f"Cq{k}")
                v_op(p, lambda E, eark=eark: E.scalar_tensor_tensor(
                    slq[:], eark[:], 1024.0, tmpq[:], ALU.mult, ALU.add),
                    tag=f"Cw{k}")
                tt = t_transpose(p, trq[:], slq[:, 0:64], id_f[:],
                                 waits=[("ve_c", p.at("ve_c"))], inc="te_c",
                                 tag=f"Cx{k}")
                e1 = v_op(p, lambda E: E.tensor_copy(sl64[:], trq[0:64, :]),
                          waits=[("te_c", tt), ("rod", p.counts.get("rod", 0))],
                          tag=f"Cy{k}")
                p.dma("sp", bass.AP(rodram.tensor, (k - 1) * 64 * 128,
                                    [[128, 64], [1, 128]]),
                      sl64[:], waits=[("ve_c", e1)], stream="rod", tag=f"Cz{k}")
                wk_ = r["warr1"] if k == 1 else r["warr2"]
                tw = t_transpose(p, trq[:], wk_[:, 0:64], id_f[:],
                                 waits=[("ve_c", p.at("ve_c"))], inc="te_c",
                                 tag=f"Cu{k}")
                e2 = v_op(p, lambda E: E.tensor_copy(sl64[:], trq[0:64, :]),
                          waits=[("te_c", tw), ("rod", p.at("rod"))], tag=f"Cv{k}")
                p.dma("sp", bass.AP(rodram.tensor, (1 + k) * 64 * 128,
                                    [[128, 64], [1, 128]]),
                      sl64[:], waits=[("ve_c", e2)], stream="rod", tag=f"Cr{k}")
            rod_done = p.at("rod")
        bar_ve = p.at("ve_c"); bar_te = p.at("te_c"); bar_act = p.at("act_c")
        bar_po = p.at("po_c")

        with p.scope():
            lrow = p.sb("C_lrow", [128, 1], U32)
            xg_tm = p.sb("C_xgtm", [128, 2048], BF16)
            xg_fm = p.sb("C_xgfm", [128, 16 * 1024], BF16)
            trbfC = p.ps("C_trbf", [128, 128], BF16)
            gpsC = p.ps("C_gps", [128, 512], F32)
            upsC = p.ps("C_ups", [128, 512], F32)
            dpsC = p.ps("C_dps", [128, 512], F32)
            g_sbC = p.sb("C_g", [128, 1024], BF16)
            u_sbC = p.sb("C_u", [128, 1024], BF16)
            gu_fm = p.sb("C_gu", [128, 11 * 1024], BF16)
            psC = PanelStream("C_ps")
            wpang2 = [p.sb("C_wpg0", [128, 16 * 128], BF16),
                      p.sb("C_wpg1", [128, 16 * 128], BF16)]
            wpanu2 = [p.sb("C_wpu0", [128, 16 * 128], BF16),
                      p.sb("C_wpu1", [128, 16 * 128], BF16)]
            wpand = p.sb("C_wpd", [128, 11 * 512], BF16)
            teCg = {}
            ev_sbC = p.sb("C_ev", [128, 512], BF16)

            for le in range(2):
                for st in range(8):
                    row = le * 8 + st
                    ldl = p.dma("sp", lrow[:],
                                listd[row:row + 1, :].rearrange("a b -> b a"),
                                waits=[("sc0", scat_done[0]), ("sc1", scat_done[1]),
                                       ("gt_dma", p.counts.get("gt_dma", 0)),
                                       ("ve_c", bar_ve), ("act_c", bar_act),
                                       ("te_c", bar_te), ("po_c", bar_po)],
                                stream="ldl", tag=f"CL{le}_{st}")
                    g_op(p, lambda E: E.indirect_dma_start(
                        out=xg_tm[:], out_offset=None, in_=xn_ag,
                        in_offset=IndirectOffsetOnAxis(lrow[:], 0)),
                        waits=[("ldl", ldl), ("te_c", p.at("te_c")), ("cc1a", 1)],
                        inc="gt_dma", inc_k=16, tag=f"CG{le}_{st}")
                    gt = p.at("gt_dma")
                    for c in range(16):
                        tt = t_transpose(p, trbfC[:], xg_tm[:, c * 128:(c + 1) * 128],
                                         id_b[:],
                                         waits=[("gt_dma", gt), ("ve_c", p.at("ve_c"))],
                                         inc="te_c", tag=f"CT{le}_{st}_{c}")
                        v_op(p, lambda E, c=c, st=st: E.tensor_copy(
                            xg_fm[:, c * 1024 + st * 128: c * 1024 + (st + 1) * 128],
                            trbfC[:]), waits=[("te_c", tt)], tag=f"CE{le}_{st}_{c}")
                xg_done = p.at("ve_c")
                for i in range(11):
                    wpang = wpang2[i % 2]
                    wpanu = wpanu2[i % 2]
                    base = le * 2048 * 1408 + i * 128
                    psC.load(f"CA{le}_{i}",
                             [(bass.AP(wg.tensor, base,
                                       [[1408, 128], [128 * 1408, 16], [1, 128]]),
                               2048, wpang[:, 0:2048])],
                             te_guard=teCg.get(i % 2, 0))
                    cvu = psC.load(f"CB{le}_{i}",
                                   [(bass.AP(wu.tensor, base,
                                             [[1408, 128], [128 * 1408, 16], [1, 128]]),
                                     2048, wpanu[:, 0:2048])],
                                   te_guard=teCg.get(i % 2, 0))
                    for sg_ in range(2):
                        for c in range(16):
                            t_matmul(p, gpsC[:, 0:512], wpang[:, c * 128:(c + 1) * 128],
                                     xg_fm[:, c * 1024 + sg_ * 512:
                                           c * 1024 + (sg_ + 1) * 512],
                                     start=(c == 0), stop=(c == 15),
                                     waits=[("ve_c", max(cvu, xg_done, p.at("ve_c")))] if c == 0 else (),
                                     inc="te_c", tag=f"CM{le}_{i}_{sg_}_{c}")
                        te_g = p.at("te_c")
                        ag_ = a_op(p, lambda E, sg_=sg_: E.activation(
                            g_sbC[:, sg_ * 512:(sg_ + 1) * 512], gpsC[:, 0:512],
                            AF.Silu), waits=[("te_c", te_g)], tag=f"CS{le}_{i}_{sg_}")
                        for c in range(16):
                            t_matmul(p, upsC[:, 0:512], wpanu[:, c * 128:(c + 1) * 128],
                                     xg_fm[:, c * 1024 + sg_ * 512:
                                           c * 1024 + (sg_ + 1) * 512],
                                     start=(c == 0), stop=(c == 15),
                                     inc="te_c", tag=f"CU{le}_{i}_{sg_}_{c}")
                        te_u = p.at("te_c")
                        v_op(p, lambda E, sg_=sg_: E.tensor_copy(
                            u_sbC[:, sg_ * 512:(sg_ + 1) * 512], upsC[:, 0:512]),
                            waits=[("te_c", te_u)], tag=f"CV{le}_{i}_{sg_}")
                        v_op(p, lambda E, i=i, sg_=sg_: E.tensor_mul(
                            gu_fm[:, i * 1024 + sg_ * 512: i * 1024 + (sg_ + 1) * 512],
                            g_sbC[:, sg_ * 512:(sg_ + 1) * 512],
                            u_sbC[:, sg_ * 512:(sg_ + 1) * 512]),
                            waits=[("act_c", ag_)], tag=f"CW{le}_{i}_{sg_}")
                    teCg[i % 2] = p.at("te_c")
                gu_done = p.at("ve_c")
                for og in range(4):
                    cvd = psC.load(
                        f"CD{le}_{og}",
                        [(bass.AP(wd.tensor,
                                  le * 1408 * 2048 + (i0 * 128) * 2048 + og * 512,
                                  [[2048, 128], [128 * 2048, ni], [1, 512]]),
                          ni * 512, wpand[:, i0 * 512:(i0 + ni) * 512])
                         for (i0, ni) in ((0, 4), (4, 4), (8, 3))])
                    for st in range(8):
                        for i in range(11):
                            t_matmul(p, dpsC[:, 0:512],
                                     gu_fm[:, i * 1024 + st * 128:
                                           i * 1024 + (st + 1) * 128],
                                     wpand[:, i * 512:(i + 1) * 512],
                                     start=(i == 0), stop=(i == 10),
                                     waits=[("ve_c", max(cvd, gu_done, p.at("ve_c")))] if i == 0 else (),
                                     inc="te_c", tag=f"CP{le}_{og}_{st}_{i}")
                        te_d = p.at("te_c")
                        ev = v_op(p, lambda E: E.tensor_copy(ev_sbC[:], dpsC[:, 0:512]),
                                  waits=[("te_c", te_d),
                                         ("stC", p.counts.get("stC", 0))],
                                  tag=f"CQ{le}_{og}_{st}")
                        p.dma("sp", out_c[le * 1024 + st * 128:
                                          le * 1024 + (st + 1) * 128,
                                          og * 512:(og + 1) * 512], ev_sbC[:],
                              waits=[("ve_c", ev)], stream="stC",
                              tag=f"CO{le}_{og}_{st}")

        g_op(p, lambda E: E.collective_compute(
            "AllGather", ALU.bypass, replica_groups=[list(range(8))],
            ins=[out_c.opt()], outs=[outf_ag.opt()]),
            waits=[("stC", p.at("stC"))], inc="cc2", inc_k=1, tag="AG2")
        p.ops["po"].append((lambda E: None, [("cc2", 1)], None, 1, "AG2w"))

        with p.scope():
            myi = p.sb("D_myi", [128, 1], U32)
            sidf = p.sb("D_sidf", [128, 2], F32)
            sidu = p.sb("D_sidu", [128, 1], U32)
            wvD = p.sb("D_wv", [128, 2], F32)
            gbuf = p.sb("D_gbuf", [128, 2048], BF16)
            y_all = p.sb("D_yall", [128, 8 * 2048], BF16)
            yacc = p.sb("D_yacc", [128, 2048], F32)
            for t in range(8):
                lm = p.dma("sp", myi[:], myidx[t * 128:(t + 1) * 128, :],
                           waits=[("gt_dma", p.at("gt_dma")), ("ve_c", p.at("ve_c")),
                                  ("rod", rod_done)],
                           stream="ldm", tag=f"Dm{t}")
                for k in (0, 1):
                    g_op(p, lambda E, k=k: E.indirect_dma_start(
                        out=sidf[:, k:k + 1], out_offset=None,
                        in_=bass.AP(rodram.tensor, 0, [[1, 32768], [1, 1]]),
                        in_offset=IndirectOffsetOnAxis(myi[:], 0),
                        element_offset=k * 64 * 128),
                        waits=[("ldm", p.at("ldm")), ("ve_c", p.at("ve_c"))],
                        inc="gt_dma", inc_k=16, tag=f"DS{t}_{k}")
                    g_op(p, lambda E, k=k: E.indirect_dma_start(
                        out=wvD[:, k:k + 1], out_offset=None,
                        in_=bass.AP(rodram.tensor, 0, [[1, 32768], [1, 1]]),
                        in_offset=IndirectOffsetOnAxis(myi[:], 0),
                        element_offset=(2 + k) * 64 * 128),
                        inc="gt_dma", inc_k=16, tag=f"DW{t}_{k}")
                gt0 = p.at("gt_dma")
                for k in (0, 1):
                    cu = v_op(p, lambda E, k=k: E.tensor_copy(sidu[:], sidf[:, k:k + 1]),
                              waits=[("gt_dma", gt0)], tag=f"DC{t}_{k}")
                    g_op(p, lambda E: E.indirect_dma_start(
                        out=gbuf[:], out_offset=None, in_=outf_ag,
                        in_offset=IndirectOffsetOnAxis(sidu[:], 0)),
                        waits=[("ve_c", cu), ("cc2", 1), ("ve_c", p.at("ve_c"))],
                        inc="gt_dma", inc_k=16, tag=f"DG{t}_{k}")
                    gtk = p.at("gt_dma")
                    if k == 0:
                        v_op(p, lambda E: E.tensor_scalar(yacc[:], gbuf[:], wvD[:, 0:1],
                                                          None, ALU.mult),
                             waits=[("gt_dma", gtk)], tag=f"D0{t}")
                    else:
                        v_op(p, lambda E: E.scalar_tensor_tensor(
                            yacc[:], gbuf[:], wvD[:, 1:2], yacc[:], ALU.mult, ALU.add),
                            waits=[("gt_dma", gtk)], tag=f"D1{t}")
                        v_op(p, lambda E, t=t: E.tensor_copy(
                            y_all[:, t * 2048:(t + 1) * 2048], yacc[:]), tag=f"D2{t}")
            y_done = p.at("ve_c")

            guD = p.sb("D_gu", [128, 22 * 512], BF16)
            psD = PanelStream("D_ps")
            wpgD2 = [p.sb("D_wpg0", [128, 16 * 128], BF16),
                     p.sb("D_wpg1", [128, 16 * 128], BF16)]
            wpuD2 = [p.sb("D_wpu0", [128, 16 * 128], BF16),
                     p.sb("D_wpu1", [128, 16 * 128], BF16)]
            wpdD = p.sb("D_wpd", [128, 22 * 512], BF16)
            teDg = {}
            g_sbD = p.sb("D_g", [128, 512], BF16)
            u_sbD = p.sb("D_u", [128, 512], BF16)
            ysD = p.sb("D_ys", [128, 512], F32)
            silD = p.sb("D_sil", [128, 512], BF16)
            xslD = p.sb("D_xsl", [128, 512], F32)
            o_sbD = p.sb("D_o", [128, 512], F32)
            gpsD = p.ps("D_gps", [128, 512], F32)
            upsD = p.ps("D_ups", [128, 512], F32)
            dpsD = p.ps("D_dps", [128, 512], F32)
            for tg in range(2):
                for i in range(22):
                    wpgD = wpgD2[i % 2]
                    wpuD = wpuD2[i % 2]
                    psD.load(f"DA{tg}_{i}",
                             [(bass.AP(wsg.tensor, i * 128,
                                       [[2816, 128], [128 * 2816, 16], [1, 128]]),
                               2048, wpgD[:, 0:2048])], te_guard=teDg.get(i % 2, 0))
                    psD.load(f"DB{tg}_{i}",
                             [(bass.AP(wsu.tensor, i * 128,
                                       [[2816, 128], [128 * 2816, 16], [1, 128]]),
                               2048, wpuD[:, 0:2048])], te_guard=teDg.get(i % 2, 0))
                    for c in range(16):
                        t_matmul(p, gpsD[:, 0:512], wpgD[:, c * 128:(c + 1) * 128],
                                 xn_fm_sb[:, c * 1024 + tg * 512:
                                          c * 1024 + (tg + 1) * 512],
                                 start=(c == 0), stop=(c == 15),
                                 waits=[("ve_c", p.at("ve_c"))] if c == 0 else (),
                                 inc="te_c", tag=f"DM{tg}_{i}_{c}")
                    te_g = p.at("te_c")
                    agD = a_op(p, lambda E: E.activation(g_sbD[:], gpsD[:, 0:512],
                                                         AF.Silu),
                               waits=[("te_c", te_g)], tag=f"DI{tg}_{i}")
                    for c in range(16):
                        t_matmul(p, upsD[:, 0:512], wpuD[:, c * 128:(c + 1) * 128],
                                 xn_fm_sb[:, c * 1024 + tg * 512:
                                          c * 1024 + (tg + 1) * 512],
                                 start=(c == 0), stop=(c == 15),
                                 inc="te_c", tag=f"DU{tg}_{i}_{c}")
                    te_u = p.at("te_c")
                    v_op(p, lambda E: E.tensor_copy(u_sbD[:], upsD[:, 0:512]),
                         waits=[("te_c", te_u)], tag=f"DV{tg}_{i}")
                    v_op(p, lambda E, i=i: E.tensor_mul(guD[:, i * 512:(i + 1) * 512],
                                                        g_sbD[:], u_sbD[:]),
                         waits=[("act_c", agD)], tag=f"DX{tg}_{i}")
                    teDg[i % 2] = p.at("te_c")
                gu_doneD = p.at("ve_c")
                for og in range(4):
                    cvd = psD.load(
                        f"DD{tg}_{og}",
                        [(bass.AP(wsd.tensor, (i0 * 128) * 2048 + og * 512,
                                  [[2048, 128], [128 * 2048, ni], [1, 512]]),
                          ni * 512, wpdD[:, i0 * 512:(i0 + ni) * 512])
                         for (i0, ni) in ((0, 4), (4, 4), (8, 4), (12, 4), (16, 4),
                                          (20, 2))])
                    for tt_ in range(4):
                        t = tg * 4 + tt_
                        for i in range(22):
                            t_matmul(p, dpsD[:, 0:512],
                                     guD[:, i * 512 + tt_ * 128:
                                         i * 512 + (tt_ + 1) * 128],
                                     wpdD[:, i * 512:(i + 1) * 512],
                                     start=(i == 0), stop=(i == 21),
                                     waits=[("ve_c", max(cvd, gu_doneD, p.at("ve_c")))] if i == 0 else (),
                                     inc="te_c", tag=f"DP{tg}_{og}_{tt_}_{i}")
                        te_d = p.at("te_c")
                        lx = p.dma("sp", xslD[:], x_in[t * 128:(t + 1) * 128,
                                                       og * 512:(og + 1) * 512],
                                   waits=[("ve_c", p.at("ve_c"))], stream="ldsD",
                                   tag=f"DL{t}_{og}")
                        sv = v_op(p, lambda E, t=t, og=og: E.scalar_tensor_tensor(
                            ysD[:], dpsD[:, 0:512], 1.0,
                            y_all[:, t * 2048 + og * 512: t * 2048 + (og + 1) * 512],
                            ALU.mult, ALU.add),
                            waits=[("te_c", te_d), ("ve_c", y_done)], tag=f"DY{t}_{og}")
                        asil = a_op(p, lambda E: E.activation(silD[:], ysD[:], AF.Silu),
                                    waits=[("ve_c", sv)], tag=f"DF{t}_{og}")
                        v_op(p, lambda E, t=t, og=og: E.tensor_mul(
                            ysD[:], silD[:],
                            s_tm_sb[:, t * 2048 + og * 512: t * 2048 + (og + 1) * 512]),
                            waits=[("act_c", asil), ("ve_c", s_done)], tag=f"DZ{t}_{og}")
                        ov = v_op(p, lambda E: E.tensor_add(o_sbD[:], ysD[:], xslD[:]),
                                  waits=[("stD", p.counts.get("stD", 0)),
                                         ("ldsD", lx)], tag=f"DO{t}_{og}")
                        p.dma("sp", x1_d[t * 128:(t + 1) * 128,
                                         og * 512:(og + 1) * 512], o_sbD[:],
                              waits=[("ve_c", ov)], stream="stD", tag=f"DR{t}_{og}")
    x1_done = p.at("stD")

    with p.scope():
        xn1_fm_sb = p.sb("XN1", [128, 16 * 1024], BF16)
        s1_sb = p.sb("S1", [128, 16 * 1024], BF16)
        with p.scope():
            q1_sb = p.sb("Q1", [128, 16 * 1024], BF16)
            with p.scope():
                ln_tiles("E", x1_d, xn1_fm_sb, xn_tm_dram=None, logits=None,
                         x_ready=[("stD", x1_done)])
                xn1_done = p.at("ve_c")
                bq_sb = p.sb("E_bq", [128, 16], F32)
                bk_sb = p.sb("E_bk", [128, 4], F32)
                bv_sb = p.sb("E_bv", [128, 4], F32)
                p.dma("sp", bq_sb[:], bqT, stream="ldEb", tag="Ebq")
                p.dma("sp", bk_sb[:], bkT, stream="ldEb", tag="Ebk")
                p.dma("sp", bv_sb[:], bvT, stream="ldEb", tag="Ebv")
                ld_b = p.at("ldEb")
                psE = PanelStream("E_ps")
                wpanE2 = [p.sb("E_wpan0", [128, 16 * 128], BF16),
                          p.sb("E_wpan1", [128, 16 * 128], BF16)]
                teE = {}
                qeE = p.sb("E_qe", [128, 1024], BF16)
                vtmE = p.sb("E_vtm", [128, 128], BF16)
                qpsE = p.ps("E_qps", [128, 512], F32)
                trbfE = p.ps("E_trbf", [128, 128], BF16)

                def projE(wdram, bias_sb, nheads, dest, wtag, to_tm=False):
                    for h in range(nheads):
                        wpanE = wpanE2[h % 2]
                        cv = psE.load(f"E{wtag}{h}",
                                      [(bass.AP(wdram.tensor, h * 2048 * 128,
                                                [[128, 128], [128 * 128, 16], [1, 128]]),
                                        2048, wpanE[:, 0:2048])],
                                      te_guard=teE.get(h % 2, 0))
                        for grp in range(2):
                            for c in range(16):
                                t_matmul(p, qpsE[:, 0:512],
                                         wpanE[:, c * 128:(c + 1) * 128],
                                         xn1_fm_sb[:, c * 1024 + grp * 512:
                                                   c * 1024 + (grp + 1) * 512],
                                         start=(c == 0), stop=(c == 15),
                                         waits=[("ve_c", max(cv, xn1_done)),
                                                ("act_c", p.at("act_c"))] if c == 0 else (),
                                         inc="te_c", tag=f"E{wtag}m{h}_{grp}_{c}")
                            te_q = p.at("te_c")
                            a_op(p, lambda E, h=h, grp=grp: E.activation(
                                qeE[:, grp * 512:(grp + 1) * 512], qpsE[:, 0:512],
                                AF.Identity, bias=bias_sb[:, h:h + 1], scale=1.0),
                                waits=[("te_c", te_q), ("ldEb", ld_b),
                                       ("stE", p.counts.get("stE", 0)),
                                       ("ve_c", p.at("ve_c"))],
                                tag=f"E{wtag}e{h}_{grp}")
                        ae_all = p.at("act_c")
                        if not to_tm:
                            if dest is None:
                                v_op(p, lambda E, h=h: E.tensor_copy(
                                    q1_sb[:, h * 1024:(h + 1) * 1024], qeE[:]),
                                    waits=[("act_c", ae_all)], tag=f"Eq{h}")
                            else:
                                p.dma("sp", dest[h * 128:(h + 1) * 128, :], qeE[:],
                                      waits=[("act_c", ae_all)], stream="stE",
                                      tag=f"E{wtag}o{h}")
                        else:
                            for t in range(8):
                                tt = t_transpose(p, trbfE[:],
                                                 qeE[:, t * 128:(t + 1) * 128], id_b[:],
                                                 waits=[("act_c", ae_all),
                                                        ("ve_c", p.at("ve_c"))],
                                                 inc="te_c", tag=f"E{wtag}t{h}_{t}")
                                ev = v_op(p, lambda E: E.tensor_copy(vtmE[:], trbfE[:]),
                                          waits=[("te_c", tt),
                                                 ("stE", p.counts.get("stE", 0))],
                                          tag=f"E{wtag}v{h}_{t}")
                                p.dma("sp", dest[t * 128:(t + 1) * 128,
                                                 h * 128:(h + 1) * 128], vtmE[:],
                                      waits=[("ve_c", ev)], stream="stE",
                                      tag=f"E{wtag}o{h}_{t}")
                        teE[h % 2] = p.at("te_c")

                projE(wq1, bq_sb, 16, None, "q")
                projE(wk1, bk_sb, 4, k1_c, "k")
                projE(wv1, bv_sb, 4, v1_c, "v", to_tm=True)

            for nm, (ci, co) in {"cc3a": (k1_c, k1_ag), "cc3b": (v1_c, v1_ag)}.items():
                g_op(p, lambda E, ci=ci, co=co: E.collective_compute(
                    "AllGather", ALU.bypass, replica_groups=[list(range(8))],
                    ins=[ci.opt()], outs=[co.opt()]),
                    waits=[("stE", p.at("stE"))] if nm == "cc3a" else (),
                    inc=nm, inc_k=1, tag="AG3" + nm)
                p.ops["po"].append((lambda E: None, [(nm, 1)], None, 1, "AG3w"))

            with p.scope():
                kbufF = p.sb("F_k", [128, 8192], BF16)
                vbufF = p.sb("F_v", [128, 64 * 128], BF16)
                mk_sb = p.sb("F_mk", [128, 64 * 128], BF16)
                esF2 = [p.sb("F_es0", [128, 1024], BF16),
                        p.sb("F_es1", [128, 1024], BF16),
                        p.sb("F_es2", [128, 1024], BF16)]
                daccF = p.sb("F_dacc", [128, 1024], F32)
                rcF = p.sb("F_rc", [1, 1024], F32)
                bcF = p.sb("F_bc", [128, 1024], F32)
                lgpF2 = [p.ps("F_lgp0", [128, 1024], F32),
                         p.ps("F_lgp1", [128, 1024], F32),
                         p.ps("F_lgp2", [128, 1024], F32)]
                avpF = p.ps("F_avp", [128, 1024], F32)
                p.dma("sp", mk_sb[:],
                      bass.AP(masks.tensor, 0, [[128, 128], [16384, 64], [1, 128]]),
                      stream="ldF", tag="Fmk")
                ld_mk = p.at("ldF")
                for h in range(16):
                    g = h // 4
                    prev_sweep = p.at("te_c")
                    for r_ in range(8):
                        p.dma("sp", kbufF[:, r_ * 1024:(r_ + 1) * 1024],
                              k1_ag[512 * r_ + 128 * g: 512 * r_ + 128 * (g + 1), :],
                              waits=[("te_c", prev_sweep), ("cc3a", 1), ("cc3b", 1)] if r_ == 0 else (),
                              stream="ldkF", tag=f"Fk{h}_{r_}")
                    p.dma("sp", vbufF[:],
                          bass.AP(v1_ag.tensor, g * 128,
                                  [[512, 128], [128 * 512, 64], [1, 128]]),
                          stream="ldkF", tag=f"Fv{h}")
                    ldkF = p.at("ldkF")
                    last_exp = [p.at("act_c")] * 3
                    last_cons = [p.at("ve_c")] * 3
                    last_av = [p.at("te_c")] * 3
                    first_ve = p.at("ve_c")
                    for kc in range(64):
                        par = kc % 3
                        ac = agcol(kc)
                        m0 = kc // 8
                        s = m0 * 128
                        pieces = []
                        if s < 512:
                            pieces.append((s, 512))
                        pieces.append((max(s, 512), 1024))
                        w0 = [("ldkF", ldkF), ("act_c", last_exp[par])]
                        if kc < 3:
                            w0.append(("ve_c", first_ve))
                        first = True
                        for (a, b) in pieces:
                            t_matmul(p, lgpF2[par][:, a:b],
                                     kbufF[:, ac * 128:(ac + 1) * 128],
                                     q1_sb[:, h * 1024 + a: h * 1024 + b], True, True,
                                     waits=w0 if first else (), inc="te_c",
                                     tag=f"Fl{h}_{kc}_{a}")
                            first = False
                        te_lg = p.at("te_c")
                        a_exp = a_op(p, lambda E, s=s, par=par: E.activation(
                            esF2[par][:, s:1024], lgpF2[par][:, s:1024], AF.Exp,
                            bias=0.0, scale=SCALE),
                            waits=[("te_c", te_lg), ("te_c", last_av[par]),
                                   ("ve_c", last_cons[par])], tag=f"Fe{h}_{kc}")
                        last_exp[par] = a_exp
                        mm = v_op(p, lambda E, s=s, kc=kc, par=par: E.tensor_mul(
                            esF2[par][:, s:s + 128], esF2[par][:, s:s + 128],
                            mk_sb[:, kc * 128:(kc + 1) * 128]),
                            waits=[("act_c", a_exp), ("ldF", ld_mk)], tag=f"Fm{h}_{kc}")
                        if kc == 0:
                            dv = v_op(p, lambda E, par=par: E.tensor_copy(
                                daccF[:], esF2[par][:]),
                                waits=[("ve_c", mm), ("te_c", p.at("te_c"))],
                                tag=f"Fq{h}_{kc}")
                        else:
                            dv = v_op(p, lambda E, par=par, s=s: E.tensor_add(
                                daccF[:, s:1024], daccF[:, s:1024],
                                esF2[par][:, s:1024]),
                                waits=[("ve_c", mm)], tag=f"Fq{h}_{kc}")
                        last_cons[par] = dv
                        for (a, b) in pieces:
                            st_ = (kc == 0)
                            sp_ = (kc == 31) if b == 512 else (kc == 63)
                            t_matmul(p, avpF[:, a:b], vbufF[:, ac * 128:(ac + 1) * 128],
                                     esF2[par][:, a:b], st_, sp_,
                                     waits=[("ve_c", mm)], inc="te_c",
                                     tag=f"Fa{h}_{kc}_{a}")
                        last_av[par] = p.at("te_c")
                    dacc_done = p.at("ve_c")
                    t_matmul(p, lgpF2[0][0:1, 0:512], onescol[:], daccF[:, 0:512],
                             True, True,
                             waits=[("ve_c", dacc_done), ("act_c", p.at("act_c"))],
                             inc="te_c", tag=f"Fn{h}a")
                    dn_mm = t_matmul(p, lgpF2[0][0:1, 512:1024], onescol[:],
                                     daccF[:, 512:1024], True, True, inc="te_c",
                                     tag=f"Fn{h}b")
                    rce = v_op(p, lambda E: E.tensor_copy(rcF[:], lgpF2[0][0:1, :]),
                               waits=[("te_c", dn_mm)], tag=f"Fre{h}")
                    rcv = v_op(p, lambda E: E.reciprocal(rcF[:], rcF[:]),
                               waits=[("ve_c", rce)], tag=f"Fr{h}")
                    t_matmul(p, lgpF2[1][:, 0:512], ones_row[:], rcF[0:1, 0:512], True,
                             True, waits=[("ve_c", rcv), ("act_c", p.at("act_c"))],
                             inc="te_c", tag=f"Fb{h}a")
                    bc_mm = t_matmul(p, lgpF2[1][:, 512:1024], ones_row[:],
                                     rcF[0:1, 512:1024], True, True, inc="te_c",
                                     tag=f"Fb{h}b")
                    bce = v_op(p, lambda E: E.tensor_copy(bcF[:], lgpF2[1][:]),
                               waits=[("te_c", bc_mm)], tag=f"Fc{h}")
                    v_op(p, lambda E, h=h: E.scalar_tensor_tensor(
                        s1_sb[:, h * 1024:(h + 1) * 1024], avpF[:], 1.0, bcF[:],
                        ALU.mult, ALU.mult), waits=[("ve_c", bce)], tag=f"Fo{h}")
        if os.environ.get("DBG_S1"):
            s1dbg = p.dtmp("s1dbg", [2048, 1024], BF16)
            _sv = p.at("ve_c")
            for _h in range(16):
                p.dma("sp", s1dbg[_h * 128:(_h + 1) * 128, :],
                      s1_sb[:, _h * 1024:(_h + 1) * 1024],
                      waits=[("ve_c", _sv)], stream="s1dbg", tag=f"s1d{_h}")
        s1_done = p.at("ve_c")

        with p.scope():
            junk = p.sb("G_junk", [128, 2048], F32)
            csum = p.sb("G_csum", [128, 1], F32)
            accG = p.sb("G_acc", [128, 2], F32)
            scal = p.sb("G_scal", [1, 8], F32)
            perm = p.sb("G_perm", [1, 6], F32)
            thrb = p.sb("G_thrb", [128, 6], F32)
            tot_ps = p.ps("G_tot", [1, 8], F32)
            bc_ps = p.ps("G_bcp", [128, 6], F32)
            v_op(p, lambda E: E.memset(accG[:], 0.0), tag="Ga0")
            acc0 = p.at("ve_c")
            av = 0
            absst = [p.sb("G_ab0", [128, 2048], F32), p.sb("G_ab1", [128, 2048], F32)]
            prev_abs = [0, 0]
            ii = 0
            for wi, (wdr, nr, nc_, col) in enumerate([(wfcT, 2048, 5120, 0),
                                                      (wprT, 5120, 2048, 1)]):
                for rr in range(nr // 128):
                    for cc0 in range(0, nc_, 2048):
                        cw = min(2048, nc_ - cc0)
                        par = ii % 2
                        ii += 1
                        ldw = p.dma("sp", absst[par][:, 0:cw],
                                    wdr[rr * 128:(rr + 1) * 128, cc0:cc0 + cw],
                                    waits=[("act_c", prev_abs[par])],
                                    stream=f"ldab{par}", tag=f"GA{wi}_{rr}_{cc0}")
                        aa = a_op(p, lambda E, par=par, cw=cw: E.activation(
                            junk[:, 0:cw], absst[par][:, 0:cw], AF.Abs,
                            accum_out=csum[:]),
                            waits=[(f"ldab{par}", ldw), ("ve_c", av)],
                            tag=f"GB{wi}_{rr}_{cc0}")
                        prev_abs[par] = aa
                        av = v_op(p, lambda E, col=col: E.tensor_add(
                            accG[:, col:col + 1], accG[:, col:col + 1], csum[:]),
                            waits=[("act_c", aa), ("ve_c", acc0)],
                            tag=f"GC{wi}_{rr}_{cc0}")
            acc_done = p.at("ve_c")
            tm = t_matmul(p, tot_ps[0:1, 0:2], onescol[:], accG[:, 0:2], True, True,
                          waits=[("ve_c", acc_done)], inc="te_c", tag="GT")
            v_op(p, lambda E: E.tensor_copy(scal[0:1, 0:2], tot_ps[0:1, 0:2]),
                 waits=[("te_c", tm)], tag="G0")
            v_op(p, lambda E: E.tensor_scalar(scal[0:1, 0:2], scal[0:1, 0:2],
                                              1.0 / (2048.0 * 5120.0), None, ALU.mult),
                 tag="G1")
            v_op(p, lambda E: E.tensor_scalar_max(scal[0:1, 0:2], scal[0:1, 0:2],
                                                  1e-5), tag="G2")
            v_op(p, lambda E: E.tensor_scalar(scal[0:1, 2:4], scal[0:1, 0:2], 0.7,
                                              None, ALU.mult), tag="G3")
            v_op(p, lambda E: E.tensor_scalar(scal[0:1, 4:6], scal[0:1, 2:4], -1.0,
                                              None, ALU.mult), tag="G4")
            v_op(p, lambda E: E.tensor_copy(perm[0:1, 0:1], scal[0:1, 2:3]), tag="G5")
            v_op(p, lambda E: E.tensor_copy(perm[0:1, 1:2], scal[0:1, 4:5]), tag="G6")
            v_op(p, lambda E: E.tensor_copy(perm[0:1, 2:3], scal[0:1, 0:1]), tag="G7")
            v_op(p, lambda E: E.tensor_copy(perm[0:1, 3:4], scal[0:1, 3:4]), tag="G8")
            v_op(p, lambda E: E.tensor_copy(perm[0:1, 4:5], scal[0:1, 5:6]), tag="G9")
            v_op(p, lambda E: E.tensor_copy(perm[0:1, 5:6], scal[0:1, 1:2]), tag="Gx")
            pm_done = p.at("ve_c")
            bm = t_matmul(p, bc_ps[:, 0:6], ones_row[:], perm[0:1, 0:6], True, True,
                          waits=[("ve_c", pm_done)], inc="te_c", tag="Gy")
            thr_done = v_op(p, lambda E: E.tensor_copy(thrb[:], bc_ps[:, 0:6]),
                            waits=[("te_c", bm)], tag="Gz")

            c1G = p.sb("G_c1", [128, 2048], F32)
            c2G = p.sb("G_c2", [128, 2048], F32)
            psG = PanelStream("G_ps")
            wpan42 = [p.sb("G_wp40", [128, 16 * 512], BF16),
                      p.sb("G_wp41", [128, 16 * 512], BF16)]
            teG4 = {}
            wpanp = p.sb("G_wpp", [128, 40 * 128], BF16)
            h2 = p.sb("G_h2", [128, 40 * 512], BF16)
            r_sbG = p.sb("G_r", [128, 512], BF16)
            g_sbG = p.sb("G_g", [128, 512], BF16)
            x1sl = p.sb("G_x1sl", [128, 128], F32)
            obuf = p.sb("G_obuf", [128, 128], F32)
            fpsG = p.ps("G_fps", [128, 512], F32)
            ppsG = p.ps("G_pps", [128, 512], F32)
            trbfG = p.ps("G_trbf", [128, 128], BF16)

            def mk_ternz(thr_col):
                def conv(stage_ap, dst_ap, waits):
                    width = stage_ap.ap[-1][1] * (stage_ap.ap[-2][1]
                                                  if len(stage_ap.ap) > 2 else 1)
                    v_op(p, lambda E: E.tensor_scalar(c1G[:, 0:width], stage_ap,
                                                      thrb[:, thr_col:thr_col + 1],
                                                      None, ALU.is_gt),
                         waits=list(waits) + [("ve_c", thr_done)], tag="tz1")
                    v_op(p, lambda E: E.tensor_scalar(c2G[:, 0:width], stage_ap,
                                                      thrb[:, thr_col + 1:thr_col + 2],
                                                      None, ALU.is_lt), tag="tz2")
                    v_op(p, lambda E: E.tensor_sub(c1G[:, 0:width], c1G[:, 0:width],
                                                   c2G[:, 0:width]), tag="tz3")
                    return v_op(p, lambda E: E.tensor_scalar(
                        dst_ap, c1G[:, 0:width],
                        thrb[:, thr_col + 2:thr_col + 3], None, ALU.mult), tag="tz4")
                return conv
            ternz_fc = mk_ternz(0)
            ternz_pr = mk_ternz(3)

            for tg in range(2):
                for hg in range(10):
                    wpan4 = wpan42[hg % 2]
                    cvp = psG.load(
                        f"GF{tg}_{hg}",
                        [(wsub3(wfcT, 5120, 4 * s, 4, hg * 512, 512), 2048,
                          wpan4[:, s * 4 * 512:(s + 1) * 4 * 512])
                         for s in range(4)], conv=ternz_fc,
                         te_guard=teG4.get(hg % 2, 0))
                    for hsub in range(4):
                        hc = hg * 4 + hsub
                        for c in range(16):
                            t_matmul(p, fpsG[:, 0:512],
                                     wpan4[:, c * 512 + hsub * 128:
                                           c * 512 + (hsub + 1) * 128],
                                     xn1_fm_sb[:, c * 1024 + tg * 512:
                                               c * 1024 + (tg + 1) * 512],
                                     start=(c == 0), stop=(c == 15),
                                     waits=[("ve_c", max(cvp, p.at("ve_c")))] if c == 0 else (),
                                     inc="te_c", tag=f"GM{tg}_{hc}_{c}")
                        te_f = p.at("te_c")
                        ar = a_op(p, lambda E: E.activation(r_sbG[:], fpsG[:, 0:512],
                                                            AF.Relu),
                                  waits=[("te_c", te_f)], tag=f"GR{tg}_{hc}")
                        v_op(p, lambda E, hc=hc: E.tensor_mul(
                            h2[:, hc * 512:(hc + 1) * 512], r_sbG[:], r_sbG[:]),
                            waits=[("act_c", ar)], tag=f"GQ{tg}_{hc}")
                    teG4[hg % 2] = p.at("te_c")
                h2_done = p.at("ve_c")
                for og in range(16):
                    cvp = psG.load(
                        f"GP{tg}_{og}",
                        [(bass.AP(wprT.tensor, (i0 * 128) * 2048 + og * 128,
                                  [[2048, 128], [128 * 2048, ni], [1, 128]]),
                          ni * 128, wpanp[:, i0 * 128:(i0 + ni) * 128])
                         for (i0, ni) in ((0, 16), (16, 16), (32, 8))],
                        conv=ternz_pr)
                    for i in range(40):
                        t_matmul(p, ppsG[:, 0:512], wpanp[:, i * 128:(i + 1) * 128],
                                 h2[:, i * 512:(i + 1) * 512],
                                 start=(i == 0), stop=(i == 39),
                                 waits=[("ve_c", max(cvp, h2_done))] if i == 0 else (),
                                 inc="te_c", tag=f"GN{tg}_{og}_{i}")
                    te_p = p.at("te_c")
                    asil = a_op(p, lambda E: E.activation(g_sbG[:], ppsG[:, 0:512],
                                                          AF.Silu),
                                waits=[("te_c", te_p)], tag=f"GS{tg}_{og}")
                    v_op(p, lambda E, og=og, tg=tg: E.tensor_mul(
                        g_sbG[:], g_sbG[:],
                        s1_sb[:, og * 1024 + tg * 512: og * 1024 + (tg + 1) * 512]),
                        waits=[("act_c", asil), ("ve_c", s1_done)], tag=f"GG{tg}_{og}")
                    for tsub in range(4):
                        t = tg * 4 + tsub
                        lx = p.dma("sp", x1sl[:],
                                   x1_d[t * 128:(t + 1) * 128,
                                        og * 128:(og + 1) * 128],
                                   waits=[("ve_c", p.at("ve_c")), ("stD", x1_done)],
                                   stream="ldx1G", tag=f"GX{t}_{og}")
                        tt = t_transpose(p, trbfG[:],
                                         g_sbG[:, tsub * 128:(tsub + 1) * 128],
                                         id_b[:], waits=[("ve_c", p.at("ve_c"))],
                                         inc="te_c", tag=f"GU{tg}_{og}_{tsub}")
                        ov = v_op(p, lambda E: E.tensor_add(obuf[:], x1sl[:], trbfG[:]),
                                  waits=[("te_c", tt), ("ldx1G", lx),
                                         ("stG", p.counts.get("stG", 0))],
                                  tag=f"GO{tg}_{og}_{tsub}")
                        p.dma("sp", out[t * 128:(t + 1) * 128,
                                        og * 128:(og + 1) * 128], obuf[:],
                              waits=[("ve_c", ov)], stream="stG", tag=f"GZ{t}_{og}")

    nc = p.finalize(final_waits=[("stG", p.at("stG"))])
    return nc, p


T = 8192
NCORE = 8

def tok_idx(c):
    return np.concatenate([np.arange(128 * j, 128 * (j + 1)) for j in chunks_of(c)])

def _mk_masks(c):
    ch = chunks_of(c)
    m = np.zeros((64, 128, 128), np.float32)
    kk = np.arange(128)[:, None]; qq = np.arange(128)[None, :]
    for kc in range(64):
        j = ch[kc // 8]
        if kc < j:
            m[kc] = 1.0
        elif kc == j:
            m[kc] = (128 * kc + kk <= 128 * j + qq)
    return m.astype(ml_dtypes.bfloat16)

_NC_CACHE = {}

def kernel(**inputs):
    if "nc" not in _NC_CACHE:
        _NC_CACHE["nc"] = build_fused()[0]
    nc = _NC_CACHE["nc"]
    x_full = np.ascontiguousarray(
        np.asarray(inputs["L0_attn_x_q"], np.float32).reshape(T, 2048))
    wq0 = np.ascontiguousarray(np.asarray(inputs["L0_attn_Wq"]).reshape(2048, 2048))
    wk0 = np.ascontiguousarray(np.asarray(inputs["L0_attn_Wk"]).reshape(2048, 512))
    wv0 = np.ascontiguousarray(np.asarray(inputs["L0_attn_Wv"]).reshape(2048, 512))
    gwt = np.ascontiguousarray(np.asarray(inputs["L0_ffn_gate_w"]).T)
    wo0 = np.ascontiguousarray(np.asarray(inputs["L0_attn_Wo"]))
    wg_all = np.asarray(inputs["L0_ffn_w_gate"])
    wu_all = np.asarray(inputs["L0_ffn_w_up"])
    wd_all = np.asarray(inputs["L0_ffn_w_down"])
    wsg = np.ascontiguousarray(np.asarray(inputs["L0_ffn_ws_gate"]))
    wsu = np.ascontiguousarray(np.asarray(inputs["L0_ffn_ws_up"]))
    wsd = np.ascontiguousarray(np.asarray(inputs["L0_ffn_ws_down"]))
    wq1 = np.ascontiguousarray(np.asarray(inputs["L1_attn_Wq"]))
    wk1 = np.ascontiguousarray(np.asarray(inputs["L1_attn_Wk"]))
    wv1 = np.ascontiguousarray(np.asarray(inputs["L1_attn_Wv"]))
    bqT = np.ascontiguousarray(np.asarray(inputs["L1_attn_bq"]).T)
    bkT = np.ascontiguousarray(np.asarray(inputs["L1_attn_bk"]).T)
    bvT = np.ascontiguousarray(np.asarray(inputs["L1_attn_bv"]).T)
    wfcT = np.ascontiguousarray(np.asarray(inputs["L1_ffn_w_fc"]).T)
    wprT = np.ascontiguousarray(np.asarray(inputs["L1_ffn_w_proj"]).T)

    in_maps = []
    for c in range(NCORE):
        ti = tok_idx(c)
        in_maps.append(dict(
            x=np.ascontiguousarray(x_full[ti]),
            wq=wq0, wk=wk0, wv=wv0, gwt=gwt, wo=wo0,
            e0vec=np.full((128, 1), 2.0 * c, np.float32),
            wg=np.ascontiguousarray(wg_all[2 * c:2 * c + 2]),
            wu=np.ascontiguousarray(wu_all[2 * c:2 * c + 2]),
            wd=np.ascontiguousarray(wd_all[2 * c:2 * c + 2]),
            myidx=ti.astype(np.uint32).reshape(1024, 1),
            wsg=wsg, wsu=wsu, wsd=wsd,
            wq1=wq1, bqT=bqT, wk1=wk1, bkT=bkT, wv1=wv1, bvT=bvT,
            masks=_mk_masks(c), wfcT=wfcT, wprT=wprT,
        ))
    res = run_bass_kernel_spmd(nc, in_maps, core_ids=list(range(NCORE)))
    final = np.empty((T, 2048), np.float32)
    for c in range(NCORE):
        final[tok_idx(c)] = np.asarray(res.results[c]["out"])
    return final.reshape(1, T, 2048)
